# version 03 (final)
# baseline (speedup 1.0000x reference)
import math
from contextlib import ExitStack

import numpy as np
import ml_dtypes

import concourse.bass as bass
import concourse.mybir as mybir
from concourse.bass_utils import run_bass_kernel_spmd

F32 = mybir.dt.float32
BF16 = mybir.dt.bfloat16
AF = mybir.ActivationFunctionType
ALU = mybir.AluOpType
AX = mybir.AxisListType
BF = ml_dtypes.bfloat16

NCORE = 8
D = 2048
NCH = D // 128
B = 4
L = 4096
TC = 2048
TW = 512
LN_EPS = 1e-5
NEG_BIG = -1.0e30



class Tick:
    __slots__ = ("q", "v", "hz")

    def __init__(self, q, v, hz=False):
        self.q = q
        self.v = v
        self.hz = hz


class CCTick:
    __slots__ = ("semh", "name")

    def __init__(self, semh, name):
        self.semh = semh
        self.name = name


class EQ:

    def __init__(self, name):
        self.name = name
        self.ops = []
        self.n = 0
        self.semh = None
        self.waits = {}

    def run(self, fn, step=1, hazard=False):
        self.n += step
        me = self

        def wrapped(eng, _fn=fn, _s=step):
            _fn(eng).then_inc(me.semh, _s)

        self.ops.append(wrapped)
        return Tick(self, self.n, hz=hazard)

    def runq(self, fn):
        self.ops.append(lambda eng, _fn=fn: _fn(eng))
        return None

    def wait(self, tick, allow_self=False):
        if tick is None:
            return
        if isinstance(tick, CCTick):
            if self.waits.get(tick.name, -1) >= 1:
                return
            self.waits[tick.name] = 1
            self.ops.append(lambda eng, s=tick.semh: eng.wait_ge(s, 1))
            return
        if tick.q is self and self.name == "T":
            return
        key = "SELF" if tick.q is self else tick.q.name
        if self.waits.get(key, -1) >= tick.v:
            return
        self.waits[key] = tick.v
        self.ops.append(lambda eng, q=tick.q, v=tick.v: eng.wait_ge(q.semh, v))

    def wait_many(self, ticks):
        best = {}
        for t in ticks:
            if t is None:
                continue
            if isinstance(t, CCTick):
                self.wait(t)
                continue
            key = "SELF" if t.q is self else t.q.name
            if key not in best or best[key].v < t.v:
                best[key] = t
        for t in best.values():
            self.wait(t)


class Buf:
    __slots__ = ("ap", "wtick", "rticks", "name")

    def __init__(self, ap, name=""):
        self.ap = ap
        self.name = name
        self.wtick = None
        self.rticks = []


class Prog:
    def __init__(self):
        self.nc = bass.Bass("TRN2", target_bir_lowering=False)
        self.TQ = EQ("T")
        self.VQ = EQ("V")
        self.SQ = EQ("S")
        self.PQ = EQ("P")
        self.DQ = EQ("D")
        self.queues = [self.TQ, self.VQ, self.SQ, self.PQ, self.DQ]
        self.ctx = ExitStack()
        self._cur = self.ctx
        self.pfx = ""

    def arena(self):
        from contextlib import contextmanager

        @contextmanager
        def _a():
            st = ExitStack()
            old = self._cur
            self._cur = st
            self._ccs = {}
            try:
                yield
            finally:
                st.close()
                self._cur = old
                self._ccs = {}
                self.barrier()
        return _a()

    def barrier(self):
        counts = [(q, q.n) for q in self.queues]
        for q in self.queues:
            for (o, v) in counts:
                if o is not q and v > 0:
                    q.wait(Tick(o, v))

    def din(self, name, shape, dtype=F32):
        return self.nc.dram_tensor(name, list(shape), dtype, kind="ExternalInput").ap()

    def dout(self, name, shape, dtype=F32):
        return self.nc.dram_tensor(name, list(shape), dtype, kind="ExternalOutput").ap()

    def dtmp(self, name, shape, dtype=F32):
        return self.nc.dram_tensor(self.pfx + name, list(shape), dtype).ap()

    def sb(self, name, shape, dtype=F32):
        name = self.pfx + name
        h = self._cur.enter_context(self.nc.sbuf_tensor(name, list(shape), dtype))
        return Buf(h.ap(), name)

    def constcol(self, value, name=None):
        name = name or f"cc_{len(getattr(self, '_ccs', {}))}"
        if not hasattr(self, "_ccs"):
            self._ccs = {}
        if value in self._ccs:
            return self._ccs[value]
        b = self.sb(name, [128, 1], F32)
        self.write(self.VQ, b, [], lambda e, o=b.ap: e.memset(o, value))
        self._ccs[value] = b
        return b

    def ps(self, name, shape, dtype=F32):
        name = self.pfx + name
        h = self._cur.enter_context(self.nc.psum_tensor(name, list(shape), dtype))
        return Buf(h.ap(), name)

    def read(self, q, bufs, fn, step=1, hazard=False):
        if not isinstance(bufs, (list, tuple)):
            bufs = [bufs]
        for b in bufs:
            if isinstance(b, Buf):
                q.wait(b.wtick)
        t = q.run(fn, step=step, hazard=hazard)
        for b in bufs:
            if isinstance(b, Buf):
                b.rticks.append(t)
        return t

    def write(self, q, dst, srcs, fn, step=1, fence=False, hazard=False):
        if not isinstance(srcs, (list, tuple)):
            srcs = [srcs]
        for s in srcs:
            if isinstance(s, Buf):
                q.wait(s.wtick, allow_self=fence)
        if dst is not None:
            q.wait_many(dst.rticks)
            if dst.wtick is not None:
                q.wait(dst.wtick)
        t = q.run(fn, step=step, hazard=hazard)
        if dst is not None:
            dst.rticks = []
            dst.wtick = t
        for s in srcs:
            if isinstance(s, Buf):
                s.rticks.append(t)
        return t

    def rw(self, q, dst, srcs, fn, step=1, hazard=False):
        if not isinstance(srcs, (list, tuple)):
            srcs = [srcs]
        return self.write(q, dst, list(srcs) + [dst], fn, step=step, hazard=hazard)

    def assemble(self):
        nc = self.nc
        for q in self.queues:
            q.semh = self.ctx.enter_context(nc.semaphore("sem_" + q.name))
        for q in (self.PQ, self.DQ):
            if q.n > 0:
                q.ops.append(lambda eng, s=q, v=q.n: eng.wait_ge(s.semh, v))

        with nc.Block() as block:
            @block.tensor
            def _(eng):
                for op in self.TQ.ops:
                    op(eng)

            @block.vector
            def _(eng):
                for op in self.VQ.ops:
                    op(eng)

            @block.scalar
            def _(eng):
                for op in self.SQ.ops:
                    op(eng)

            @block.gpsimd
            def _(eng):
                for op in self.PQ.ops:
                    op(eng)

            @block.sync
            def _(eng):
                for op in self.DQ.ops:
                    op(eng)

        self.ctx.close()
        return nc



class WStream:

    def __init__(self, P, name, W_dram, n_kc, mcw=128, nbuf=2, cast=True):
        self.P = P
        self.W = W_dram
        self.n_kc = n_kc
        self.mcw = mcw
        self.cast = cast
        self.bufs = [P.sb(f"{name}_wb{i}", [128, n_kc * mcw], BF16) for i in range(nbuf)]
        self.nbuf = nbuf
        self.loaded = {}
        self.next_slot = 0

    def prefetch(self, mc):
        if mc in self.loaded:
            return
        slot = self.next_slot
        self.next_slot = (self.next_slot + 1) % self.nbuf
        for k in [k for k, v in self.loaded.items() if v == slot]:
            del self.loaded[k]
        buf = self.bufs[slot]
        src = self.W[mc, :, :]
        q = self.P.PQ if self.cast else self.P.DQ
        self.P.write(q, buf, [],
                     lambda e, d=buf.ap, s=src: e.dma_start(out=d, in_=s), step=16)
        self.loaded[mc] = slot

    def chunk(self, mc, kc):
        self.prefetch(mc)
        buf = self.bufs[self.loaded[mc]]
        return buf, buf.ap[:, kc * self.mcw:(kc + 1) * self.mcw]


class WResident:

    def __init__(self, P, name, W_dram, n_kc, n_mc, mcw=128):
        self.n_kc = n_kc
        self.mcw = mcw
        self.n_mc = n_mc
        self.buf = P.sb(name + "_w", [128, n_mc * n_kc * mcw], BF16)
        cw = n_kc * mcw
        for mc in range(n_mc):
            P.write(P.PQ, self.buf, [],
                    lambda e, d=self.buf.ap[:, mc * cw:(mc + 1) * cw],
                    s=W_dram[mc, :, :]:
                    e.dma_start(out=d, in_=s), step=16)

    def prefetch(self, mc):
        pass

    def chunk(self, mc, kc):
        o = (mc * self.n_kc + kc) * self.mcw
        return self.buf, self.buf.ap[:, o:o + self.mcw]


def emit_mm(P, psum, psum_slice, pieces, start=True, stop=True):
    TQ = P.TQ
    for (lb, la, rb, ra) in pieces:
        TQ.wait(lb.wtick)
        TQ.wait(rb.wtick)
    TQ.wait_many(psum.rticks)
    if psum.wtick is not None and start:
        pass
    n = len(pieces)
    out_ap = psum.ap[psum_slice] if psum_slice is not None else psum.ap
    t = None
    for i, (lb, la, rb, ra) in enumerate(pieces):
        s = start and i == 0
        p = stop and i == n - 1
        if i == n - 1:
            t = TQ.run(lambda e, o=out_ap, w=la, r=ra, s_=s, p_=p:
                       e.matmul(o, w, r, start=s_, stop=p_))
        else:
            TQ.runq(lambda e, o=out_ap, w=la, r=ra, s_=s, p_=p:
                    e.matmul(o, w, r, start=s_, stop=p_))
    if stop:
        psum.rticks = []
        psum.wtick = t
    else:
        psum.wtick = t
    for (lb, la, rb, ra) in pieces:
        lb.rticks.append(t)
        rb.rticks.append(t)
    return t



class LNKit:
    def __init__(self, P, tag=""):
        self.P = P
        _xb = P.sb(f"ln{tag}_xb", [128, TW], BF16)
        _x2b = P.sb(f"ln{tag}_x2b", [128, TW], BF16)
        self.xb = [_xb, _xb]
        self.x2b = [_x2b, _x2b]
        self.m = P.sb(f"ln{tag}_m", [1, TW])
        self.v = P.sb(f"ln{tag}_v", [1, TW])
        self.r1 = P.sb(f"ln{tag}_r1", [1, TW])
        self.rstd = P.sb(f"ln{tag}_rstd", [1, TW])
        self.mr = P.sb(f"ln{tag}_mr", [1, TW])
        self.rstdB = P.sb(f"ln{tag}_rstdB", [128, TW])
        self.mrB = P.sb(f"ln{tag}_mrB", [128, TW])
        self.t1 = P.sb(f"ln{tag}_t1", [128, TW])
        self.ones_bf = P.sb(f"ln{tag}_onesb", [128, 1], BF16)
        self.ones_f = P.sb(f"ln{tag}_onesf", [1, 128])

    def load_consts(self, ones_bf_dram, ones_f_dram):
        P = self.P
        P.write(P.DQ, self.ones_bf, [],
                lambda e, d=self.ones_bf.ap, s=ones_bf_dram: e.dma_start(out=d, in_=s), step=16)
        P.write(P.DQ, self.ones_f, [],
                lambda e, d=self.ones_f.ap, s=ones_f_dram: e.dma_start(out=d, in_=s), step=16)


def emit_ln(P, kit, x_list, ps_a, ps_b, out_fn):
    for c, (xb_, xa) in enumerate(x_list):
        sc = kit.xb[c % 2]
        sc2 = kit.x2b[c % 2]
        P.write(P.VQ, sc, [xb_], lambda e, o=sc.ap, i=xa: e.tensor_copy(out=o, in_=i))
        P.write(P.SQ, sc2, [xb_], lambda e, o=sc2.ap, i=xa:
                e.activation(out=o, in_=i, func=AF.Square))
        first, last = c == 0, c == NCH - 1
        P.TQ.wait(sc.wtick)
        P.TQ.wait_many(ps_a.rticks)
        t = P.TQ.run(lambda e, o=ps_a.ap[0:1, :], w=kit.ones_bf.ap, r=sc.ap, s=first, p=last:
                     e.matmul(o, w, r, start=s, stop=p))
        sc.rticks.append(t)
        if last:
            ps_a.rticks = []
            ps_a.wtick = t
        P.TQ.wait(sc2.wtick)
        P.TQ.wait_many(ps_b.rticks)
        t2 = P.TQ.run(lambda e, o=ps_b.ap[0:1, :], w=kit.ones_bf.ap, r=sc2.ap, s=first, p=last:
                      e.matmul(o, w, r, start=s, stop=p))
        sc2.rticks.append(t2)
        if last:
            ps_b.rticks = []
            ps_b.wtick = t2
    P.write(P.VQ, kit.m, [ps_a], lambda e, o=kit.m.ap, i=ps_a.ap[0:1, :]:
            e.tensor_scalar_mul(out=o, in0=i, scalar1=1.0 / D))
    P.write(P.VQ, kit.v, [ps_b], lambda e, o=kit.v.ap, i=ps_b.ap[0:1, :]:
            e.tensor_scalar_mul(out=o, in0=i, scalar1=1.0 / D))
    P.write(P.VQ, kit.r1, [kit.m], lambda e, o=kit.r1.ap, a=kit.m.ap:
            e.tensor_mul(out=o, in0=a, in1=a))
    P.rw(P.VQ, kit.v, [kit.r1], lambda e, o=kit.v.ap, a=kit.v.ap, b=kit.r1.ap:
         e.scalar_tensor_tensor(out=o, in0=a, scalar=LN_EPS, in1=b,
                                op0=ALU.add, op1=ALU.subtract))
    P.write(P.SQ, kit.r1, [kit.v], lambda e, o=kit.r1.ap, i=kit.v.ap:
            e.activation(out=o, in_=i, func=AF.Sqrt))
    P.write(P.VQ, kit.rstd, [kit.r1], lambda e, o=kit.rstd.ap, i=kit.r1.ap:
            e.reciprocal(out=o, in_=i))
    P.write(P.VQ, kit.mr, [kit.m, kit.rstd], lambda e, o=kit.mr.ap, a=kit.m.ap, b=kit.rstd.ap:
            e.tensor_mul(out=o, in0=a, in1=b))
    P.TQ.wait(kit.rstd.wtick)
    P.TQ.wait_many(ps_a.rticks)
    t = P.TQ.run(lambda e, o=ps_a.ap, w=kit.ones_f.ap, r=kit.rstd.ap:
                 e.matmul(o, w, r, start=True, stop=True))
    kit.rstd.rticks.append(t)
    ps_a.rticks = []
    ps_a.wtick = t
    P.TQ.wait(kit.mr.wtick)
    P.TQ.wait_many(ps_b.rticks)
    t = P.TQ.run(lambda e, o=ps_b.ap, w=kit.ones_f.ap, r=kit.mr.ap:
                 e.matmul(o, w, r, start=True, stop=True))
    kit.mr.rticks.append(t)
    ps_b.rticks = []
    ps_b.wtick = t
    P.write(P.VQ, kit.rstdB, [ps_a], lambda e, o=kit.rstdB.ap, i=ps_a.ap:
            e.tensor_copy(out=o, in_=i))
    P.write(P.VQ, kit.mrB, [ps_b], lambda e, o=kit.mrB.ap, i=ps_b.ap:
            e.tensor_copy(out=o, in_=i))
    for c, (xb_, xa) in enumerate(x_list):
        P.write(P.PQ, kit.t1, [xb_, kit.rstdB],
                lambda e, o=kit.t1.ap, a=xa, b=kit.rstdB.ap:
                e.tensor_mul(out=o, in0=a, in1=b), step=1)
        out_fn(c, kit.t1, kit.mrB)



def f32c(a):
    return np.ascontiguousarray(a, dtype=np.float32)


def to_cols(vec):
    n = vec.shape[0] // 128
    return np.ascontiguousarray(np.asarray(vec, dtype=np.float32).reshape(n, 128).T)


def swz_w(WT, mcw=128):
    WT = np.asarray(WT, dtype=np.float32)
    Kd, M = WT.shape
    nkc, nmc = Kd // 128, M // mcw
    return np.ascontiguousarray(
        WT.reshape(nkc, 128, nmc, mcw).transpose(2, 1, 0, 3).reshape(nmc, 128, nkc * mcw))


def swz_act(xT, tw=TW):
    xT = np.ascontiguousarray(xT)
    Dr, T = xT.shape
    nch, nT = Dr // 128, T // tw
    return np.ascontiguousarray(
        xT.reshape(nch, 128, nT, tw).transpose(2, 1, 0, 3).reshape(nT, 128, nch * tw))


def unswz_act(a, Dr, T, tw=TW):
    nch, nT = Dr // 128, T // tw
    return np.ascontiguousarray(
        np.asarray(a).reshape(nT, 128, nch, tw).transpose(2, 1, 0, 3).reshape(Dr, T))


_host_consts = None


def host_consts():
    global _host_consts
    if _host_consts is not None:
        return _host_consts
    ones_bf = np.ones((128, 1), BF)
    ones_f = np.ones((1, 128), np.float32)
    masks = np.zeros((4, 128, 512), np.float32)
    for v in range(4):
        qpos = 128 * v + np.arange(128)[:, None]
        masks[v] = np.where(qpos >= np.arange(512)[None, :], 0.0, NEG_BIG)
    masks = masks.astype(BF)
    _host_consts = dict(ones_bf=ones_bf, ones_f=ones_f, masks=masks)
    return _host_consts



def build_launch1():
    P = Prog()
    nT = L // TW
    EC = 8
    xT = P.din("xT", [nT, 128, NCH * TW])
    WqT = P.din("WqT", [EC, 128, D])
    WkT = P.din("WkT", [EC, 128, D])
    WvT = P.din("WvT", [EC, 128, D])
    bq = P.din("bq", [128, EC])
    bk = P.din("bk", [128, EC])
    bv = P.din("bv", [128, EC])
    ones_bf_d = P.din("ones_bf", [128, 1], BF16)
    ones_f_d = P.din("ones_f", [1, 128])
    ctx_o = P.dout("ctx", [nT, 128, EC * TW], BF16)

    SCALE = 128.0 ** -0.5

    kit = LNKit(P)
    kit.load_consts(ones_bf_d, ones_f_d)

    wq = WResident(P, "wq", WqT, NCH, EC)
    wk = WResident(P, "wk", WkT, NCH, EC)
    wv = WResident(P, "wv", WvT, NCH, EC)

    bqs = P.sb("bqs", [128, EC])
    bks = P.sb("bks", [128, EC])
    bvs = P.sb("bvs", [128, EC])
    for buf, src in ((bqs, bq), (bks, bk), (bvs, bv)):
        P.write(P.DQ, buf, [], lambda e, d=buf.ap, s=src: e.dma_start(out=d, in_=s), step=16)

    xt = P.sb("xt", [128, NCH * TW])
    xln = P.sb("xln", [128, NCH * TW], BF16)
    s_qf = P.sb("s_qf", [128, TW])
    s_phq = P.sb("s_phq", [128, TW])
    s_mn = P.sb("s_mn", [128, TW])
    s_ex = P.sb("s_ex", [128, TW])
    s_phk = P.sb("s_phk", [128, TW])
    s_vf = P.sb("s_vf", [128, TW])
    s_kvs = P.sb("s_kvs", [128, TW])
    zeros = P.sb("zeros", [128, TW])
    carry = P.sb("carry", [128, EC])
    ctx_bf = P.sb("ctx_bf", [128, EC * TW], BF16)

    ps_a = P.ps("ps_a", [128, TW])
    ps_b = P.ps("ps_b", [128, TW])
    ps_q = P.ps("ps_q", [128, TW])
    ps_k = P.ps("ps_k", [128, TW])
    ps_v = P.ps("ps_v", [128, TW])

    P.write(P.VQ, zeros, [], lambda e, o=zeros.ap: e.memset(o, 0.0))
    P.write(P.VQ, carry, [], lambda e, o=carry.ap: e.memset(o, 0.0))

    def phi(P, src_psum, bias_buf, bias_col, out_buf):
        P.write(P.VQ, s_qf, [src_psum, bias_buf],
                lambda e, o=s_qf.ap, i=src_psum.ap, b=bias_col:
                e.tensor_scalar_add(out=o, in0=i, scalar1=b))
        P.write(P.PQ, s_mn, [s_qf],
                lambda e, o=s_mn.ap, i=s_qf.ap: e.tensor_scalar_min(out=o, in0=i, scalar1=0.0),
                step=1)
        P.write(P.SQ, s_ex, [s_mn],
                lambda e, o=s_ex.ap, i=s_mn.ap: e.activation(out=o, in_=i, func=AF.Exp))
        P.write(P.VQ, out_buf, [s_qf, s_ex],
                lambda e, o=out_buf.ap, a=s_qf.ap, x=s_ex.ap:
                e.scalar_tensor_tensor(out=o, in0=a, scalar=0.0, in1=x,
                                       op0=ALU.max, op1=ALU.add))

    for tt in range(nT):
        P.write(P.DQ, xt, [],
                lambda e, d=xt.ap, s=xT[tt, :, :]:
                e.dma_start(out=d, in_=s), step=16)
        x_list = [(xt, xt.ap[:, c * TW:(c + 1) * TW]) for c in range(NCH)]

        def ln_out(c, t1, mrB):
            P.write(P.VQ, xln, [t1, mrB],
                    lambda e, o=xln.ap[:, c * TW:(c + 1) * TW], a=t1.ap, b=mrB.ap:
                    e.tensor_sub(out=o, in0=a, in1=b))

        emit_ln(P, kit, x_list, ps_a, ps_b, ln_out)
        xln_chunks = [(xln, xln.ap[:, c * TW:(c + 1) * TW]) for c in range(NCH)]

        for ec in range(EC):
            for (wres, psm) in ((wq, ps_q), (wk, ps_k), (wv, ps_v)):
                pieces = []
                for kc in range(NCH):
                    lb, la = wres.chunk(ec, kc)
                    rb, ra = xln_chunks[kc]
                    pieces.append((lb, la, rb, ra))
                emit_mm(P, psm, None, pieces)
            phi(P, ps_q, bqs, bqs.ap[:, ec:ec + 1], s_phq)
            phi(P, ps_k, bks, bks.ap[:, ec:ec + 1], s_phk)
            P.write(P.VQ, s_vf, [ps_v, bvs],
                    lambda e, o=s_vf.ap, i=ps_v.ap, b=bvs.ap[:, ec:ec + 1]:
                    e.tensor_scalar_add(out=o, in0=i, scalar1=b))
            P.rw(P.VQ, s_phk, [s_vf],
                 lambda e, o=s_phk.ap, a=s_phk.ap, b=s_vf.ap:
                 e.tensor_mul(out=o, in0=a, in1=b))
            P.write(P.VQ, s_kvs, [s_phk, zeros, carry],
                    lambda e, o=s_kvs.ap, d0=s_phk.ap, d1=zeros.ap,
                    ini=carry.ap[:, ec:ec + 1]:
                    e.tensor_tensor_scan(out=o, data0=d0, data1=d1, initial=ini,
                                         op0=ALU.add, op1=ALU.bypass), hazard=True)
            P.rw(P.PQ, carry, [s_kvs],
                 lambda e, o=carry.ap[:, ec:ec + 1], i=s_kvs.ap[:, TW - 1:TW]:
                 e.tensor_copy(out=o, in_=i))
            P.write(P.VQ, ctx_bf, [s_phq, s_kvs],
                    lambda e, o=ctx_bf.ap[:, ec * TW:(ec + 1) * TW], a=s_phq.ap,
                    b=s_kvs.ap:
                    e.scalar_tensor_tensor(out=o, in0=a, scalar=SCALE, in1=b,
                                           op0=ALU.mult, op1=ALU.mult), fence=True)
        P.read(P.PQ, ctx_bf,
               lambda e, d=ctx_o[tt, :, :], s=ctx_bf.ap:
               e.dma_start(out=d, in_=s), step=16)

    return P.assemble()


def run_launch1(x, Wq, bq, Wk, bk, Wv, bv):
    hc = host_consts()
    nc = build_launch1()
    in_maps = []
    for c in range(NCORE):
        b, hh = c // 2, c % 2
        sl = slice(hh * 1024, hh * 1024 + 1024)
        in_maps.append(dict(
            xT=swz_act(f32c(x[b].T)),
            WqT=swz_w(Wq.T[:, sl]), WkT=swz_w(Wk.T[:, sl]), WvT=swz_w(Wv.T[:, sl]),
            bq=to_cols(np.asarray(bq)[sl]), bk=to_cols(np.asarray(bk)[sl]),
            bv=to_cols(np.asarray(bv)[sl]),
            ones_bf=hc["ones_bf"], ones_f=hc["ones_f"],
        ))
    res = run_bass_kernel_spmd(nc, in_maps, core_ids=list(range(NCORE)))
    ctx = np.zeros((B, L, D), np.float32)
    for c in range(NCORE):
        b, hh = c // 2, c % 2
        ctxT = unswz_act(res.results[c]["ctx"].astype(np.float32), 1024, L)
        ctx[b, :, hh * 1024:hh * 1024 + 1024] = ctxT.T
    return ctx



def build_launch2(P=None, io=None):
    standalone = P is None
    nTT = TC // TW
    if standalone:
        P = Prog()
        io = dict(
            xT=P.din("xT", [nTT, 128, NCH * TW]),
            ctx=P.din("ctx", [nTT, 128, NCH * TW], BF16),
            WoT=P.din("WoT", [NCH, 128, D]),
            bo=P.din("bo", [128, NCH]),
            wT=P.din("wT", [NCH, 128, D]),
            rrT=P.din("rrT", [NCH, 128, D]),
            WqT=P.din("WqT", [NCH, 128, D]),
            WkT=P.din("WkT", [NCH, 128, D]),
            WvT=P.din("WvT", [NCH, 128, D]),
            bq=P.din("bq", [128, NCH]),
            bk=P.din("bk", [128, NCH]),
            bv=P.din("bv", [128, NCH]),
            ones_bf=P.din("ones_bf", [128, 1], BF16),
            ones_f=P.din("ones_f", [1, 128]),
            x2=P.dout("x2", [nTT, 128, NCH * TW]),
            q=P.dout("q", [NCH, 128, TC], BF16),
            k=P.dout("k", [NCH, 128, TC], BF16),
            v=P.dout("v", [NCH, 128, TC], BF16),
            wr=P.dout("wr", [NCH, 128, D], BF16),
        )
    xT_i = io["xT"]; ctx_i = io["ctx"]; WoT_i = io["WoT"]; bo_i = io["bo"]
    wT_i = io["wT"]; rrT_i = io["rrT"]
    WqT_i = io["WqT"]; WkT_i = io["WkT"]; WvT_i = io["WvT"]
    bq_i = io["bq"]; bk_i = io["bk"]; bv_i = io["bv"]
    ones_bf_d = io["ones_bf"]; ones_f_d = io["ones_f"]
    x2_o = io["x2"]; q_o = io["q"]; k_o = io["k"]; v_o = io["v"]; wr_o = io["wr"]

    x_scr = P.dtmp("x_scr", [nTT, 128, NCH * TW])
    y_scr = P.dtmp("y_scr", [nTT, 128, NCH * TW], BF16)

    SCALE = 128.0 ** -0.5
    TH_X, SC_X = 0.33, 0.5
    TH_W, SC_W = 0.2, 0.6

    kit = LNKit(P)
    kit.load_consts(ones_bf_d, ones_f_d)

    bo_s = P.sb("bo_s", [128, NCH])
    bq_s = P.sb("bq_s", [128, NCH])
    bk_s = P.sb("bk_s", [128, NCH])
    bv_s = P.sb("bv_s", [128, NCH])
    for buf, src in ((bo_s, bo_i), (bq_s, bq_i), (bk_s, bk_i), (bv_s, bv_i)):
        P.write(P.DQ, buf, [], lambda e, d=buf.ap, s=src: e.dma_start(out=d, in_=s), step=16)

    xt = P.sb("xt", [128, NCH * TW])
    ct = P.sb("ct", [128, NCH * TW], BF16)
    xbig = P.sb("xbig", [128, NCH * TC], BF16)
    xlnf = P.sb("xlnf", [128, TW])
    tg = P.sb("tg", [128, TW])
    tg2 = P.sb("tg2", [128, TW])
    sevac = [P.sb(f"sevac{i}", [128, TW], BF16) for i in range(2)]
    wrcol = [P.sb(f"wrcol{i}", [128, NCH * 128], BF16) for i in range(2)]
    wcol_f = P.sb("wcol_f", [128, NCH * 128])
    rcol_f = P.sb("rcol_f", [128, NCH * 128])
    tw1 = P.sb("tw1", [128, NCH * 128])

    ps_a = P.ps("ps_a", [128, TW])
    ps_b = P.ps("ps_b", [128, TW])
    ps_m = [P.ps(f"ps_m{i}", [128, TW]) for i in range(2)]

    ws_wo = WStream(P, "wo", WoT_i, NCH)
    ws_q = WStream(P, "wsq", WqT_i, NCH)
    ws_k = WStream(P, "wsk", WkT_i, NCH)
    ws_v = WStream(P, "wsv", WvT_i, NCH)

    def xbig_sl(c, tt):
        return xbig.ap[:, (c * nTT + tt) * TW:(c * nTT + tt + 1) * TW]

    for tt in range(nTT):
        P.write(P.DQ, xt, [], lambda e, d=xt.ap, s=xT_i[tt, :, :]:
                e.dma_start(out=d, in_=s), step=16)
        P.write(P.DQ, ct, [], lambda e, d=ct.ap, s=ctx_i[tt, :, :]:
                e.dma_start(out=d, in_=s), step=16)
        ctx_chunks = [(ct, ct.ap[:, c * TW:(c + 1) * TW]) for c in range(NCH)]
        for mc in range(NCH):
            if mc + 1 < NCH:
                ws_wo.prefetch(mc + 1)
            elif tt + 1 < nTT:
                ws_wo.prefetch(0)
            psm = ps_m[mc % 2]
            pieces = []
            for kc in range(NCH):
                lb, la = ws_wo.chunk(mc, kc)
                rb, ra = ctx_chunks[kc]
                pieces.append((lb, la, rb, ra))
            emit_mm(P, psm, None, pieces)
            P.rw(P.VQ, xt, [psm, bo_s],
                 lambda e, o=xt.ap[:, mc * TW:(mc + 1) * TW], i=psm.ap,
                 b=bo_s.ap[:, mc:mc + 1],
                 xx=xt.ap[:, mc * TW:(mc + 1) * TW]:
                 e.scalar_tensor_tensor(out=o, in0=i, scalar=b, in1=xx,
                                        op0=ALU.add, op1=ALU.add))
        x_list = [(xt, xt.ap[:, c * TW:(c + 1) * TW]) for c in range(NCH)]

        def tern_out(c, t1, mrB, tt=tt):
            P.write(P.VQ, xlnf, [t1, mrB],
                    lambda e, o=xlnf.ap, a=t1.ap, b=mrB.ap:
                    e.tensor_sub(out=o, in0=a, in1=b))
            P.write(P.PQ, tg, [xlnf],
                    lambda e, o=tg.ap, i=xlnf.ap:
                    e.tensor_scalar(out=o, in0=i, scalar1=TH_X, scalar2=SC_X,
                                    op0=ALU.is_gt, op1=ALU.mult), step=1)
            P.write(P.VQ, tg2, [xlnf],
                    lambda e, o=tg2.ap, i=xlnf.ap:
                    e.tensor_scalar(out=o, in0=i, scalar1=-TH_X, scalar2=-SC_X,
                                    op0=ALU.is_lt, op1=ALU.mult))
            P.write(P.VQ, xbig, [tg, tg2],
                    lambda e, o=xbig_sl(c, tt), a=tg.ap, b=tg2.ap:
                    e.tensor_add(out=o, in0=a, in1=b))

        emit_ln(P, kit, x_list, ps_a, ps_b, tern_out)
        P.read(P.DQ, xt, lambda e, d=x_scr[tt, :, :], s=xt.ap:
               e.dma_start(out=d, in_=s), step=16)

    for mc in range(NCH):
        wr = wrcol[mc % 2]
        P.write(P.DQ, wcol_f, [], lambda e, d=wcol_f.ap, s=wT_i[mc, :, :]:
                e.dma_start(out=d, in_=s), step=16)
        P.write(P.DQ, rcol_f, [], lambda e, d=rcol_f.ap, s=rrT_i[mc, :, :]:
                e.dma_start(out=d, in_=s), step=16)
        P.write(P.PQ, tw1, [wcol_f],
                lambda e, o=tw1.ap, i=wcol_f.ap:
                e.tensor_scalar(out=o, in0=i, scalar1=TH_W, scalar2=SC_W,
                                op0=ALU.is_gt, op1=ALU.mult), step=1)
        P.rw(P.VQ, tw1, [rcol_f],
             lambda e, o=tw1.ap, a=tw1.ap, b=rcol_f.ap:
             e.tensor_add(out=o, in0=a, in1=b))
        P.write(P.VQ, rcol_f, [wcol_f],
                lambda e, o=rcol_f.ap, i=wcol_f.ap:
                e.tensor_scalar(out=o, in0=i, scalar1=-TH_W, scalar2=-SC_W,
                                op0=ALU.is_lt, op1=ALU.mult))
        P.write(P.VQ, wr, [tw1, rcol_f],
                lambda e, o=wr.ap, a=tw1.ap, b=rcol_f.ap:
                e.tensor_add(out=o, in0=a, in1=b))
        P.read(P.PQ, wr, lambda e, d=wr_o[mc, :, :], s=wr.ap:
               e.dma_start(out=d, in_=s), step=16)
        for tt in range(nTT):
            psm = ps_m[tt % 2]
            pieces = []
            for kc in range(NCH):
                pieces.append((wr, wr.ap[:, kc * 128:(kc + 1) * 128],
                               xbig, xbig_sl(kc, tt)))
            emit_mm(P, psm, None, pieces)
            sv = sevac[tt % 2]
            P.write(P.VQ, sv, [psm],
                    lambda e, o=sv.ap, i=psm.ap: e.tensor_copy(out=o, in_=i))
            P.read(P.DQ, sv,
                   lambda e, d=y_scr[tt, :, mc * TW:(mc + 1) * TW], s=sv.ap:
                   e.dma_start(out=d, in_=s), step=16)

    for tt in range(nTT):
        P.write(P.DQ, xt, [], lambda e, d=xt.ap, s=x_scr[tt, :, :]:
                e.dma_start(out=d, in_=s), step=16)
        P.write(P.DQ, ct, [], lambda e, d=ct.ap, s=y_scr[tt, :, :]:
                e.dma_start(out=d, in_=s), step=16)
        for c in range(NCH):
            P.rw(P.VQ, xt, [ct],
                 lambda e, o=xt.ap[:, c * TW:(c + 1) * TW],
                 b=ct.ap[:, c * TW:(c + 1) * TW]:
                 e.tensor_tensor(out=o, in0=o, in1=b, op=ALU.add))
        x_list = [(xt, xt.ap[:, c * TW:(c + 1) * TW]) for c in range(NCH)]

        def ln_out(c, t1, mrB, tt=tt):
            P.write(P.VQ, xbig, [t1, mrB],
                    lambda e, o=xbig_sl(c, tt), a=t1.ap, b=mrB.ap:
                    e.tensor_sub(out=o, in0=a, in1=b))

        emit_ln(P, kit, x_list, ps_a, ps_b, ln_out)
        P.read(P.DQ, xt, lambda e, d=x2_o[tt, :, :], s=xt.ap:
               e.dma_start(out=d, in_=s), step=16)

    for (ws, out_d, bias_s, qscale) in ((ws_q, q_o, bq_s, SCALE),
                                        (ws_k, k_o, bk_s, None),
                                        (ws_v, v_o, bv_s, None)):
        for mc in range(NCH):
            if mc + 1 < NCH:
                ws.prefetch(mc + 1)
            for tt in range(nTT):
                psm = ps_m[tt % 2]
                pieces = []
                for kc in range(NCH):
                    lb, la = ws.chunk(mc, kc)
                    pieces.append((lb, la, xbig, xbig_sl(kc, tt)))
                emit_mm(P, psm, None, pieces)
                dst = ct.ap[:, tt * TW:(tt + 1) * TW]
                if qscale is not None:
                    scb = P.constcol(qscale)
                    P.write(P.VQ, ct, [psm, bias_s, scb],
                            lambda e, o=dst, i=psm.ap, b=bias_s.ap[:, mc:mc + 1],
                            s2=scb.ap[:, 0:1]:
                            e.tensor_scalar(out=o, in0=i, scalar1=b, scalar2=s2,
                                            op0=ALU.add, op1=ALU.mult))
                else:
                    P.write(P.VQ, ct, [psm, bias_s],
                            lambda e, o=dst, i=psm.ap, b=bias_s.ap[:, mc:mc + 1]:
                            e.tensor_scalar_add(out=o, in0=i, scalar1=b))
            P.read(P.DQ, ct,
                   lambda e, d=out_d[mc, :, :], s=ct.ap[:, 0:TC]:
                   e.dma_start(out=d, in_=s), step=16)

    if standalone:
        return P.assemble()


def run_launch2(x, ctx, Wo, bo, ffn_w, ffn_rr, w_qkv, b_qkv):
    hc = host_consts()
    nc = build_launch2()
    Wq_b = w_qkv[0:D, :]
    Wk_b = w_qkv[D:2 * D, :]
    Wv_b = w_qkv[2 * D:3 * D, :]
    in_maps = []
    for c in range(NCORE):
        b, hf = c // 2, c % 2
        sl = slice(hf * TC, hf * TC + TC)
        in_maps.append(dict(
            xT=swz_act(f32c(x[b, sl].T)),
            ctx=swz_act(np.ascontiguousarray(ctx[b, sl].T.astype(BF))),
            WoT=swz_w(Wo.T), bo=to_cols(bo),
            wT=swz_w(ffn_w.T), rrT=swz_w(ffn_rr.T),
            WqT=swz_w(Wq_b.T), WkT=swz_w(Wk_b.T), WvT=swz_w(Wv_b.T),
            bq=to_cols(b_qkv[0:D]), bk=to_cols(b_qkv[D:2 * D]),
            bv=to_cols(b_qkv[2 * D:3 * D]),
            ones_bf=hc["ones_bf"], ones_f=hc["ones_f"],
        ))
    res = run_bass_kernel_spmd(nc, in_maps, core_ids=list(range(NCORE)))
    x2 = np.zeros((B, L, D), np.float32)
    q = np.zeros((B, L, D), np.float32)
    k = np.zeros((B, L, D), np.float32)
    v = np.zeros((B, L, D), np.float32)
    wr = None
    for c in range(NCORE):
        b, hf = c // 2, c % 2
        sl = slice(hf * TC, hf * TC + TC)
        r = res.results[c]
        x2[b, sl] = unswz_act(r["x2"], D, TC).T
        for name, arr in (("q", q), ("k", k), ("v", v)):
            arr[b, sl] = r[name].astype(np.float32).reshape(D, TC).T
        if wr is None:
            wr = r["wr"]
    return x2, q, k, v, wr



def emit_group(P, q, dst, srcs, fns):
    for s in srcs:
        if isinstance(s, Buf):
            q.wait(s.wtick)
    if dst is not None:
        q.wait_many(dst.rticks)
        if dst.wtick is not None:
            q.wait(dst.wtick)
    t = None
    for i, fn in enumerate(fns):
        if i == len(fns) - 1:
            t = q.run(fn)
        else:
            q.runq(fn)
    if dst is not None:
        dst.rticks = []
        dst.wtick = t
    for s in srcs:
        if isinstance(s, Buf):
            s.rticks.append(t)
    return t


def build_flash(rope):
    P = Prog()
    NH = 8
    NQT = L // 128
    qn_i = P.din("qn", [NH, 128, L], BF16)
    kn_i = P.din("kn", [NH, 128, L], BF16)
    v_i = P.din("v", [NH, 128, L], BF16)
    if rope:
        qr_i = P.din("qr", [NH, 64, L], BF16)
        kr_i = P.din("kr", [NH, 64, L], BF16)
    masks_i = P.din("masks", [4, 128, TW], BF16)
    ident_i = P.din("ident", [128, 128], BF16)
    O_o = P.dout("O", [NH, 128, L], BF16)


    qn = [P.sb(f"qn{i}", [128, L], BF16) for i in range(2)]
    kn = [P.sb(f"kn{i}", [128, L], BF16) for i in range(2)]
    vv = [P.sb(f"vv{i}", [128, L], BF16) for i in range(2)]
    if rope:
        qr = [P.sb(f"qr{i}", [64, L], BF16) for i in range(2)]
        kr = [P.sb(f"kr{i}", [64, L], BF16) for i in range(2)]
    ost = [P.sb(f"ost{i}", [128, L], BF16) for i in range(2)]
    masks = P.sb("masks_sb", [128, 4 * TW], BF16)
    idt = P.sb("idt", [128, 128], BF16)
    P_sb = P.sb("P_sb", [128, TW], BF16)
    pt_sb = P.sb("pt_sb", [128, TW], BF16)
    O_acc = P.sb("O_acc", [128, 128])
    s_nmt = P.sb("s_nmt", [128, 1])
    s_nmrun = P.sb("s_nmrun", [128, 1])
    s_nmnew = P.sb("s_nmnew", [128, 1])
    s_d = P.sb("s_d", [128, 1])
    s_alpha = P.sb("s_alpha", [128, 1])
    s_rsum = P.sb("s_rsum", [128, 1])
    s_lrun = P.sb("s_lrun", [128, 1])
    s_rcp = P.sb("s_rcp", [128, 1])

    ps_s = [P.ps(f"ps_s{i}", [128, TW]) for i in range(2)]
    ps_t = P.ps("ps_t", [128, TW], BF16)
    ps_pv = P.ps("ps_pv", [128, 128])

    for mv_ in range(4):
        P.write(P.DQ, masks, [],
                lambda e, d=masks.ap[:, mv_ * TW:(mv_ + 1) * TW], s=masks_i[mv_, :, :]:
                e.dma_start(out=d, in_=s), step=16)
    P.write(P.DQ, idt, [], lambda e, d=idt.ap, s=ident_i[:, :]:
            e.dma_start(out=d, in_=s), step=16)

    def load_head(h):
        sl = h % 2
        bufs = [(qn[sl], qn_i), (kn[sl], kn_i), (vv[sl], v_i)]
        if rope:
            bufs += [(qr[sl], qr_i), (kr[sl], kr_i)]
        for buf, src in bufs:
            P.write(P.DQ, buf, [], lambda e, d=buf.ap, s=src[h, :, :]:
                    e.dma_start(out=d, in_=s), step=16)

    load_head(0)
    for h in range(NH):
        sl = h % 2
        if h + 1 < NH:
            load_head(h + 1)
        ostage = ost[sl]
        for qt in range(NQT):
            KM = qt // 4 + 1
            for km in range(KM):
                psm = ps_s[km % 2]
                pieces = [(qn[sl], qn[sl].ap[:, qt * 128:(qt + 1) * 128],
                           kn[sl], kn[sl].ap[:, km * TW:(km + 1) * TW])]
                if rope:
                    pieces.append((qr[sl], qr[sl].ap[:, qt * 128:(qt + 1) * 128],
                                   kr[sl], kr[sl].ap[:, km * TW:(km + 1) * TW]))
                if km == qt // 4:
                    mv = qt % 4
                    pieces.append((idt, idt.ap,
                                   masks, masks.ap[:, mv * TW:(mv + 1) * TW]))
                emit_mm(P, psm, None, pieces)
                P.write(P.VQ, s_nmt, [psm],
                        lambda e, o=s_nmt.ap, i=psm.ap:
                        e.reduce_max(out=o, in_=i, axis=AX.X, negate=True),
                        hazard=True)
                if km == 0:
                    P.write(P.VQ, s_nmrun, [s_nmt],
                            lambda e, o=s_nmrun.ap, i=s_nmt.ap: e.tensor_copy(out=o, in_=i))
                else:
                    P.write(P.VQ, s_nmnew, [s_nmt, s_nmrun],
                            lambda e, o=s_nmnew.ap, a=s_nmrun.ap, b=s_nmt.ap:
                            e.tensor_tensor(out=o, in0=a, in1=b, op=ALU.min))
                    P.write(P.VQ, s_d, [s_nmnew, s_nmrun],
                            lambda e, o=s_d.ap, a=s_nmnew.ap, b=s_nmrun.ap:
                            e.tensor_sub(out=o, in0=a, in1=b))
                    P.write(P.VQ, s_nmrun, [s_nmnew],
                            lambda e, o=s_nmrun.ap, i=s_nmnew.ap: e.tensor_copy(out=o, in_=i))
                    P.write(P.SQ, s_alpha, [s_d],
                            lambda e, o=s_alpha.ap, i=s_d.ap:
                            e.activation(out=o, in_=i, func=AF.Exp))
                P.SQ.wait(psm.wtick)
                P.SQ.wait(s_nmrun.wtick)
                P.SQ.wait_many(P_sb.rticks)
                P.SQ.wait_many(s_rsum.rticks)
                if P_sb.wtick is not None:
                    P.SQ.wait(P_sb.wtick)
                t = P.SQ.run(lambda e, o=P_sb.ap, i=psm.ap, b=s_nmrun.ap[:, 0:1],
                             a=s_rsum.ap[:, 0:1]:
                             e.activation(out=o, in_=i, func=AF.Exp, bias=b,
                                          accum_out=a))
                P_sb.rticks = []
                P_sb.wtick = t
                s_rsum.rticks = []
                s_rsum.wtick = t
                psm.rticks.append(t)
                s_nmrun.rticks.append(t)
                if km == 0:
                    P.write(P.VQ, s_lrun, [s_rsum],
                            lambda e, o=s_lrun.ap, i=s_rsum.ap: e.tensor_copy(out=o, in_=i))
                else:
                    P.rw(P.VQ, s_lrun, [s_rsum, s_alpha],
                         lambda e, o=s_lrun.ap, a=s_lrun.ap, sc=s_alpha.ap[:, 0:1],
                         b=s_rsum.ap:
                         e.scalar_tensor_tensor(out=o, in0=a, scalar=sc, in1=b,
                                                op0=ALU.mult, op1=ALU.add))
                fns = [
                    (lambda e, o=ps_t.ap[:, j * 128:(j + 1) * 128],
                     i=P_sb.ap[:, j * 128:(j + 1) * 128], dd=idt.ap:
                     e.transpose(o, i, dd))
                    for j in range(4)
                ]
                emit_group(P, P.TQ, ps_t, [P_sb, idt], fns)
                P.write(P.VQ, pt_sb, [ps_t],
                        lambda e, o=pt_sb.ap, i=ps_t.ap: e.tensor_copy(out=o, in_=i))
                pieces = [(pt_sb, pt_sb.ap[:, j * 128:(j + 1) * 128],
                           vv[sl], vv[sl].ap[:, (4 * km + j) * 128:(4 * km + j + 1) * 128])
                          for j in range(4)]
                emit_mm(P, ps_pv, None, pieces)
                if km == 0:
                    P.write(P.VQ, O_acc, [ps_pv],
                            lambda e, o=O_acc.ap, i=ps_pv.ap: e.tensor_copy(out=o, in_=i))
                else:
                    P.rw(P.VQ, O_acc, [ps_pv, s_alpha],
                         lambda e, o=O_acc.ap, a=O_acc.ap, sc=s_alpha.ap[:, 0:1],
                         b=ps_pv.ap:
                         e.scalar_tensor_tensor(out=o, in0=a, scalar=sc, in1=b,
                                                op0=ALU.mult, op1=ALU.add))
            P.write(P.VQ, s_rcp, [s_lrun],
                    lambda e, o=s_rcp.ap, i=s_lrun.ap: e.reciprocal(out=o, in_=i),
                    hazard=True)
            P.write(P.VQ, ostage, [O_acc, s_rcp],
                    lambda e, o=ostage.ap[:, qt * 128:(qt + 1) * 128], i=O_acc.ap,
                    sc=s_rcp.ap[:, 0:1]:
                    e.tensor_scalar_mul(out=o, in0=i, scalar1=sc), fence=True)
        P.read(P.PQ, ostage, lambda e, d=O_o[h, :, :], s=ostage.ap:
               e.dma_start(out=d, in_=s), step=16)

    return P.assemble()


def _flash_inmaps(q_heads, k_heads, v_heads, qr_heads=None, kr_heads=None):
    hc = host_consts()
    vsw = np.ascontiguousarray(
        v_heads.reshape(8, 32, 128, 128).transpose(0, 2, 1, 3).reshape(8, 128, L))
    m = dict(qn=np.ascontiguousarray(q_heads), kn=np.ascontiguousarray(k_heads),
             v=vsw, masks=hc["masks"], ident=np.eye(128).astype(BF))
    if qr_heads is not None:
        m["qr"] = np.ascontiguousarray(qr_heads)
        m["kr"] = np.ascontiguousarray(kr_heads)
    return m


def _flash_out(res_c):
    O = res_c["O"].astype(np.float32)
    return O.reshape(8, 128, 32, 128).transpose(2, 1, 0, 3).reshape(L, 1024)


def run_launch3(q, k, v):
    nc = build_flash(rope=False)
    in_maps = []
    for c in range(NCORE):
        b, hh = c // 2, c % 2
        qT = q[b].T.astype(BF)
        kT = k[b].T.astype(BF)
        hsl = slice(hh * 8, hh * 8 + 8)
        qh = qT.reshape(16, 128, L)[hsl]
        kh = kT.reshape(16, 128, L)[hsl]
        vh = np.ascontiguousarray(
            v[b].astype(BF).reshape(L, 16, 128).transpose(1, 0, 2)[hsl])
        in_maps.append(_flash_inmaps(qh, kh, vh))
    res = run_bass_kernel_spmd(nc, in_maps, core_ids=list(range(NCORE)))
    out = np.zeros((B, L, D), np.float32)
    for c in range(NCORE):
        b, hh = c // 2, c % 2
        out[b, :, hh * 1024:hh * 1024 + 1024] = _flash_out(res.results[c])
    return out



def build_launch4(P=None, io=None):
    standalone = P is None
    nTT = TC // TW
    S192 = 192.0 ** -0.5
    if standalone:
        P = Prog()
        io = dict(
            xT=P.din("xT", [nTT, 128, NCH * TW]),
            O=P.din("O", [nTT, 128, NCH * TW], BF16),
            WoT=P.din("WoT", [NCH, 128, D]),
            bo=P.din("bo", [128, NCH]),
            wr=P.din("wr", [NCH, 128, D], BF16),
            Wqkv=P.din("Wqkv", [28, 128, D]),
            Wku=P.din("Wku", [6, 128, 512]),
            Wvu=P.din("Wvu", [4, 128, 512]),
            cosq=P.din("cosq", [128, TC], BF16),
            sinq=P.din("sinq", [128, TC], BF16),
            cosk=P.din("cosk", [128, TC], BF16),
            sink=P.din("sink", [128, TC], BF16),
            RT=P.din("RT", [128, 128], BF16),
            ones_bf=P.din("ones_bf", [128, 1], BF16),
            ones_f=P.din("ones_f", [1, 128]),
            x4=P.dout("x4", [nTT, 128, NCH * TW]),
            qn=P.dout("qn", [16, 128, TC], BF16),
            qr=P.dout("qr", [8, 128, TC], BF16),
            kn=P.dout("kn", [4, 128, TC], BF16),
            kr=P.dout("kr", [2, 128, TC], BF16),
            vv=P.dout("vv", [4, 128, TC], BF16),
        )
    xT_i = io["xT"]; O_i = io["O"]; WoT_i = io["WoT"]; bo_i = io["bo"]
    wr_i = io["wr"]; Wqkv_i = io["Wqkv"]; Wku_i = io["Wku"]; Wvu_i = io["Wvu"]
    cosq_i = io["cosq"]; sinq_i = io["sinq"]; cosk_i = io["cosk"]; sink_i = io["sink"]
    RT_i = io["RT"]; ones_bf_d = io["ones_bf"]; ones_f_d = io["ones_f"]
    x4_o = io["x4"]; qn_o = io["qn"]; qr_o = io["qr"]; kn_o = io["kn"]
    kr_o = io["kr"]; vv_o = io["vv"]

    x_scr = P.dtmp("x_scr", [nTT, 128, NCH * TW])
    y_scr = P.dtmp("y_scr", [nTT, 128, NCH * TW], BF16)

    TH_X, SC_X = 0.33, 0.5

    kit = LNKit(P)
    kit.load_consts(ones_bf_d, ones_f_d)

    bo_s = P.sb("bo_s", [128, NCH])
    P.write(P.DQ, bo_s, [], lambda e, d=bo_s.ap, s=bo_i: e.dma_start(out=d, in_=s), step=16)
    cosq = P.sb("cosq_s", [128, TC], BF16)
    sinq = P.sb("sinq_s", [128, TC], BF16)
    cosk = P.sb("cosk_s", [128, TC], BF16)
    sink = P.sb("sink_s", [128, TC], BF16)
    for buf, src in ((cosq, cosq_i), (sinq, sinq_i), (cosk, cosk_i), (sink, sink_i)):
        P.write(P.DQ, buf, [], lambda e, d=buf.ap, s=src: e.dma_start(out=d, in_=s), step=16)
    RT = P.sb("RT_s", [128, 128], BF16)
    P.write(P.DQ, RT, [], lambda e, d=RT.ap, s=RT_i: e.dma_start(out=d, in_=s), step=16)

    xt = P.sb("xt", [128, NCH * TW])
    ct = P.sb("ct", [128, NCH * TW], BF16)
    xbig = P.sb("xbig", [128, NCH * TC], BF16)
    xlnf = P.sb("xlnf", [128, TW])
    tg = P.sb("tg", [128, TW])
    tg2 = P.sb("tg2", [128, TW])
    sevac = [P.sb(f"sevac{i}", [128, TW], BF16) for i in range(2)]
    wrcol = [P.sb(f"wrcol{i}", [128, NCH * 128], BF16) for i in range(2)]
    ckv = P.sb("ckv", [128, 4 * TC], BF16)
    rope_in = P.sb("rope_in", [128, TW], BF16)
    r_t1 = P.sb("r_t1", [128, TW])
    r_t2 = P.sb("r_t2", [128, TW])

    ps_a = P.ps("ps_a", [128, TW])
    ps_b = P.ps("ps_b", [128, TW])
    ps_m = [P.ps(f"ps_m{i}", [128, TW]) for i in range(2)]
    ps_r = P.ps("ps_r", [128, TW])

    ws_wo = WStream(P, "wo", WoT_i, NCH)
    ws_qkv = WStream(P, "wqkv", Wqkv_i, NCH)
    ws_ku = WStream(P, "wku", Wku_i, 4, nbuf=2)
    ws_vu = WStream(P, "wvu", Wvu_i, 4, nbuf=2)
    sc192 = P.constcol(S192)

    def xbig_sl(c, tt):
        return xbig.ap[:, (c * nTT + tt) * TW:(c * nTT + tt + 1) * TW]

    for tt in range(nTT):
        P.write(P.DQ, xt, [], lambda e, d=xt.ap, s=xT_i[tt, :, :]:
                e.dma_start(out=d, in_=s), step=16)
        oq = io.get("O_queue", P.DQ)
        P.write(oq, ct, [], lambda e, d=ct.ap, tt=tt: e.dma_start(out=d, in_=O_i[tt, :, :]),
                step=16)
        o_chunks = [(ct, ct.ap[:, c * TW:(c + 1) * TW]) for c in range(NCH)]
        for mc in range(NCH):
            if mc + 1 < NCH:
                ws_wo.prefetch(mc + 1)
            psm = ps_m[mc % 2]
            pieces = [(ws_wo.chunk(mc, kc)[0], ws_wo.chunk(mc, kc)[1],
                       o_chunks[kc][0], o_chunks[kc][1]) for kc in range(NCH)]
            emit_mm(P, psm, None, pieces)
            P.rw(P.VQ, xt, [psm, bo_s],
                 lambda e, o=xt.ap[:, mc * TW:(mc + 1) * TW], i=psm.ap,
                 b=bo_s.ap[:, mc:mc + 1],
                 xx=xt.ap[:, mc * TW:(mc + 1) * TW]:
                 e.scalar_tensor_tensor(out=o, in0=i, scalar=b, in1=xx,
                                        op0=ALU.add, op1=ALU.add))
        x_list = [(xt, xt.ap[:, c * TW:(c + 1) * TW]) for c in range(NCH)]

        def tern_out(c, t1, mrB, tt=tt):
            P.write(P.VQ, xlnf, [t1, mrB],
                    lambda e, o=xlnf.ap, a=t1.ap, b=mrB.ap:
                    e.tensor_sub(out=o, in0=a, in1=b))
            P.write(P.PQ, tg, [xlnf],
                    lambda e, o=tg.ap, i=xlnf.ap:
                    e.tensor_scalar(out=o, in0=i, scalar1=TH_X, scalar2=SC_X,
                                    op0=ALU.is_gt, op1=ALU.mult), step=1)
            P.write(P.VQ, tg2, [xlnf],
                    lambda e, o=tg2.ap, i=xlnf.ap:
                    e.tensor_scalar(out=o, in0=i, scalar1=-TH_X, scalar2=-SC_X,
                                    op0=ALU.is_lt, op1=ALU.mult))
            P.write(P.VQ, xbig, [tg, tg2],
                    lambda e, o=xbig_sl(c, tt), a=tg.ap, b=tg2.ap:
                    e.tensor_add(out=o, in0=a, in1=b))

        emit_ln(P, kit, x_list, ps_a, ps_b, tern_out)
        P.read(P.DQ, xt, lambda e, d=x_scr[tt, :, :], s=xt.ap:
               e.dma_start(out=d, in_=s), step=16)

    for mc in range(NCH):
        wr = wrcol[mc % 2]
        P.write(P.DQ, wr, [], lambda e, d=wr.ap, s=wr_i[mc, :, :]:
                e.dma_start(out=d, in_=s), step=16)
        for tt in range(nTT):
            psm = ps_m[tt % 2]
            pieces = [(wr, wr.ap[:, kc * 128:(kc + 1) * 128],
                       xbig, xbig_sl(kc, tt)) for kc in range(NCH)]
            emit_mm(P, psm, None, pieces)
            sv = sevac[tt % 2]
            P.write(P.VQ, sv, [psm],
                    lambda e, o=sv.ap, i=psm.ap: e.tensor_copy(out=o, in_=i))
            P.read(P.DQ, sv,
                   lambda e, d=y_scr[tt, :, mc * TW:(mc + 1) * TW], s=sv.ap:
                   e.dma_start(out=d, in_=s), step=16)

    for tt in range(nTT):
        P.write(P.DQ, xt, [], lambda e, d=xt.ap, s=x_scr[tt, :, :]:
                e.dma_start(out=d, in_=s), step=16)
        P.write(P.DQ, ct, [], lambda e, d=ct.ap, s=y_scr[tt, :, :]:
                e.dma_start(out=d, in_=s), step=16)
        for c in range(NCH):
            P.rw(P.VQ, xt, [ct],
                 lambda e, o=xt.ap[:, c * TW:(c + 1) * TW],
                 b=ct.ap[:, c * TW:(c + 1) * TW]:
                 e.tensor_tensor(out=o, in0=o, in1=b, op=ALU.add))
        x_list = [(xt, xt.ap[:, c * TW:(c + 1) * TW]) for c in range(NCH)]

        def ln_out(c, t1, mrB, tt=tt):
            P.write(P.VQ, xbig, [t1, mrB],
                    lambda e, o=xbig_sl(c, tt), a=t1.ap, b=mrB.ap:
                    e.tensor_sub(out=o, in0=a, in1=b))

        emit_ln(P, kit, x_list, ps_a, ps_b, ln_out)
        P.read(P.DQ, xt, lambda e, d=x4_o[tt, :, :], s=xt.ap:
               e.dma_start(out=d, in_=s), step=16)

    def rope_emit(psm, cos_sb, sin_sb, tt, dst_stage_ap):
        P.write(P.VQ, rope_in, [psm],
                lambda e, o=rope_in.ap, i=psm.ap: e.tensor_copy(out=o, in_=i))
        P.write(P.VQ, r_t1, [rope_in, cos_sb],
                lambda e, o=r_t1.ap, a=rope_in.ap,
                b=cos_sb.ap[:, tt * TW:(tt + 1) * TW]:
                e.tensor_mul(out=o, in0=a, in1=b))
        emit_mm(P, ps_r, None, [(RT, RT.ap, rope_in, rope_in.ap)])
        P.write(P.VQ, r_t2, [ps_r, sin_sb],
                lambda e, o=r_t2.ap, a=ps_r.ap,
                b=sin_sb.ap[:, tt * TW:(tt + 1) * TW]:
                e.tensor_mul(out=o, in0=a, in1=b))
        P.write(P.VQ, ct, [r_t1, r_t2],
                lambda e, o=dst_stage_ap, a=r_t1.ap, b=r_t2.ap:
                e.tensor_add(out=o, in0=a, in1=b))

    for mc in range(28):
        if mc + 1 < 28:
            ws_qkv.prefetch(mc + 1)
        for tt in range(nTT):
            psm = ps_m[tt % 2]
            pieces = [(ws_qkv.chunk(mc, kc)[0], ws_qkv.chunk(mc, kc)[1],
                       xbig, xbig_sl(kc, tt)) for kc in range(NCH)]
            emit_mm(P, psm, None, pieces)
            dst = ct.ap[:, tt * TW:(tt + 1) * TW]
            if mc < 16:
                P.write(P.VQ, ct, [psm, sc192],
                        lambda e, o=dst, i=psm.ap, s2=sc192.ap[:, 0:1]:
                        e.tensor_scalar_mul(out=o, in0=i, scalar1=s2))
            elif mc < 24:
                rope_emit(psm, cosq, sinq, tt, dst)
            else:
                cc = mc - 24
                P.write(P.VQ, ckv, [psm],
                        lambda e, o=ckv.ap[:, (cc * nTT + tt) * TW:(cc * nTT + tt + 1) * TW],
                        i=psm.ap: e.tensor_copy(out=o, in_=i))
        if mc < 16:
            P.read(P.DQ, ct, lambda e, d=qn_o[mc, :, :], s=ct.ap[:, 0:TC]:
                   e.dma_start(out=d, in_=s), step=16)
        elif mc < 24:
            P.read(P.DQ, ct, lambda e, d=qr_o[mc - 16, :, :], s=ct.ap[:, 0:TC]:
                   e.dma_start(out=d, in_=s), step=16)

    ckv_chunks = lambda tt: [(ckv, ckv.ap[:, (c * nTT + tt) * TW:(c * nTT + tt + 1) * TW])
                             for c in range(4)]
    for mc in range(6):
        if mc + 1 < 6:
            ws_ku.prefetch(mc + 1)
        for tt in range(nTT):
            psm = ps_m[tt % 2]
            pieces = [(ws_ku.chunk(mc, kc)[0], ws_ku.chunk(mc, kc)[1],
                       ckv_chunks(tt)[kc][0], ckv_chunks(tt)[kc][1]) for kc in range(4)]
            emit_mm(P, psm, None, pieces)
            dst = ct.ap[:, tt * TW:(tt + 1) * TW]
            if mc < 4:
                P.write(P.VQ, ct, [psm],
                        lambda e, o=dst, i=psm.ap: e.tensor_copy(out=o, in_=i))
            else:
                rope_emit(psm, cosk, sink, tt, dst)
        if mc < 4:
            P.read(P.DQ, ct, lambda e, d=kn_o[mc, :, :], s=ct.ap[:, 0:TC]:
                   e.dma_start(out=d, in_=s), step=16)
        else:
            P.read(P.DQ, ct, lambda e, d=kr_o[mc - 4, :, :], s=ct.ap[:, 0:TC]:
                   e.dma_start(out=d, in_=s), step=16)
    for mc in range(4):
        if mc + 1 < 4:
            ws_vu.prefetch(mc + 1)
        for tt in range(nTT):
            psm = ps_m[tt % 2]
            pieces = [(ws_vu.chunk(mc, kc)[0], ws_vu.chunk(mc, kc)[1],
                       ckv_chunks(tt)[kc][0], ckv_chunks(tt)[kc][1]) for kc in range(4)]
            emit_mm(P, psm, None, pieces)
            P.write(P.VQ, ct, [psm],
                    lambda e, o=ct.ap[:, tt * TW:(tt + 1) * TW], i=psm.ap:
                    e.tensor_copy(out=o, in_=i))
        P.read(P.DQ, ct, lambda e, d=vv_o[mc, :, :], s=ct.ap[:, 0:TC]:
               e.dma_start(out=d, in_=s), step=16)

    if standalone:
        return P.assemble()


def reorder_wqkv_cols(Wqkv):
    idx = []
    for h in range(16):
        idx.extend(range(h * 192, h * 192 + 128))
    for h in range(16):
        idx.extend(range(h * 192 + 128, h * 192 + 192))
    idx.extend(range(3072, 3584))
    return np.ascontiguousarray(Wqkv[:, idx])


def reorder_wup_cols(Wu, dqk=192, dn=128):
    nkv = Wu.shape[1] // dqk
    idx = []
    for g in range(nkv):
        idx.extend(range(g * dqk, g * dqk + dn))
    for g in range(nkv):
        idx.extend(range(g * dqk + dn, g * dqk + dqk))
    return np.ascontiguousarray(Wu[:, idx])


def rope_tables():
    inv = 1.0 / (10000.0 ** (np.arange(0, 64, 2, dtype=np.float64) / 64.0))
    t = np.arange(L, dtype=np.float64)
    freqs = np.outer(t, inv)
    emb = np.concatenate([freqs, freqs], 1)
    cosT = np.cos(emb).T.astype(np.float32)
    sinT = np.sin(emb).T.astype(np.float32)
    return cosT, sinT


def rot_matrix():
    Rb = np.zeros((64, 64), np.float32)
    for i in range(32):
        Rb[i, i + 32] = -1.0
        Rb[i + 32, i] = 1.0
    Rfull = np.zeros((128, 128), np.float32)
    Rfull[:64, :64] = Rb
    Rfull[64:, 64:] = Rb
    return Rfull.T.copy()


def rope_host_tables(scale):
    cosT, sinT = rope_tables()
    cos2 = np.vstack([cosT, cosT]) * scale
    sin2 = np.vstack([sinT, sinT]) * scale
    return f32c(cos2), f32c(sin2)


def run_launch4(x2, O, w_out, b_out, wr, Wqkv, Wk_up, Wv_up):
    hc = host_consts()
    nc = build_launch4()
    Wqkv_r = reorder_wqkv_cols(np.asarray(Wqkv))
    Wku_r = reorder_wup_cols(np.asarray(Wk_up))
    S192 = 192.0 ** -0.5
    cq_full, sq_full = rope_host_tables(S192)
    ck_full, sk_full = rope_host_tables(1.0)
    in_maps = []
    for c in range(NCORE):
        b, hf = c // 2, c % 2
        sl = slice(hf * TC, hf * TC + TC)
        in_maps.append(dict(
            xT=swz_act(f32c(x2[b, sl].T)),
            O=swz_act(np.ascontiguousarray(O[b, sl].T.astype(BF))),
            WoT=swz_w(np.asarray(w_out).T), bo=to_cols(b_out),
            wr=wr,
            Wqkv=swz_w(Wqkv_r), Wku=swz_w(Wku_r), Wvu=swz_w(np.asarray(Wv_up)),
            cosq=cq_full[:, sl].astype(BF), sinq=sq_full[:, sl].astype(BF),
            cosk=ck_full[:, sl].astype(BF), sink=sk_full[:, sl].astype(BF),
            RT=rot_matrix().astype(BF),
            ones_bf=hc["ones_bf"], ones_f=hc["ones_f"],
        ))
    res = run_bass_kernel_spmd(nc, in_maps, core_ids=list(range(NCORE)))
    x4 = np.zeros((B, L, D), np.float32)
    qn = np.zeros((B, L, 2048), np.float32)
    qr = np.zeros((B, L, 1024), np.float32)
    kn = np.zeros((B, L, 512), np.float32)
    kr = np.zeros((B, L, 256), np.float32)
    vv = np.zeros((B, L, 512), np.float32)
    for c in range(NCORE):
        b, hf = c // 2, c % 2
        sl = slice(hf * TC, hf * TC + TC)
        r = res.results[c]
        x4[b, sl] = unswz_act(r["x4"], D, TC).T
        for name, arr, nc_ in (("qn", qn, 16), ("qr", qr, 8), ("kn", kn, 4),
                               ("kr", kr, 2), ("vv", vv, 4)):
            arr[b, sl] = r[name].astype(np.float32).reshape(nc_ * 128, TC).T
    return x4, qn, qr, kn, kr, vv


def run_launch5(qn, qr, kn, kr, vv):
    nc = build_flash(rope=True)
    in_maps = []
    for c in range(NCORE):
        b, hh = c // 2, c % 2
        heads = range(hh * 8, hh * 8 + 8)
        qh = np.stack([qn[b, :, h * 128:(h + 1) * 128].T.astype(BF) for h in heads])
        qrh = np.stack([qr[b, :, h * 64:(h + 1) * 64].T.astype(BF) for h in heads])
        kh = np.stack([kn[b, :, (h // 4) * 128:(h // 4 + 1) * 128].T.astype(BF)
                       for h in heads])
        krh = np.stack([kr[b, :, (h // 4) * 64:(h // 4 + 1) * 64].T.astype(BF)
                        for h in heads])
        vh = np.stack([vv[b, :, (h // 4) * 128:(h // 4 + 1) * 128].astype(BF)
                       for h in heads])
        in_maps.append(_flash_inmaps(qh, kh, vh, qrh, krh))
    res = run_bass_kernel_spmd(nc, in_maps, core_ids=list(range(NCORE)))
    out = np.zeros((B, L, D), np.float32)
    for c in range(NCORE):
        b, hh = c // 2, c % 2
        out[b, :, hh * 1024:hh * 1024 + 1024] = _flash_out(res.results[c])
    return out



def build_launch6(with_outproj, emit_xln, P=None, io=None):
    standalone = P is None
    nTT = TC // TW
    NEX = 16
    if standalone:
        P = Prog()
        io = dict(
            xT=P.din("xT", [nTT, 128, NCH * TW]),
            gwT=P.din("gwT", [1, 128, NEX * NCH]),
            eb=P.din("eb", [1, NEX]),
            sel=P.din("sel", [16, NEX * 128], BF16),
            gw=P.din("gw", [NEX, 8, 128, D]),
            uw=P.din("uw", [NEX, 8, 128, D]),
            dw=P.din("dw", [NEX, 16, 128, 1024]),
            shg=P.din("shg", [16, 128, D]),
            shu=P.din("shu", [16, 128, D]),
            shd=P.din("shd", [16, 128, D]),
            idt16=P.din("idt16", [16, 16], BF16),
            idt128f=P.din("idt128f", [128, 128], BF16),
            ones_bf=P.din("ones_bf", [128, 1], BF16),
            ones_f=P.din("ones_f", [1, 128]),
            x=P.dout("x", [nTT, 128, NCH * TW]),
        )
        if with_outproj:
            io["O"] = P.din("O", [nTT, 128, NCH * TW], BF16)
            io["WoT"] = P.din("WoT", [NCH, 128, D])
        if emit_xln:
            io["xln"] = P.dout("xln", [nTT, 128, NCH * TW], BF16)
    xT_i = io["xT"]
    if with_outproj:
        O_i = io["O"]; WoT_i = io["WoT"]
    gwT_i = io["gwT"]; eb_i = io["eb"]; sel_i = io["sel"]
    gw_i = io["gw"]; uw_i = io["uw"]; dw_i = io["dw"]
    shg_i = io["shg"]; shu_i = io["shu"]; shd_i = io["shd"]
    idt16_i = io["idt16"]; idt128f_i = io["idt128f"]
    ones_bf_d = io["ones_bf"]; ones_f_d = io["ones_f"]
    x_o = io["x"]
    if emit_xln:
        xln_o = io["xln"]
    xln_scr = P.dtmp("xln_scr", [nTT, 128, NCH * TW], BF16)
    x_scr = P.dtmp("x_scr6", [nTT, 128, NCH * TW]) if with_outproj else xT_i

    kit = LNKit(P)
    kit.load_consts(ones_bf_d, ones_f_d)

    gwres = WResident(P, "gwt", gwT_i, n_kc=NCH, n_mc=1, mcw=16)
    sel_s = P.sb("sel_s", [16, NEX * 128], BF16)
    P.write(P.DQ, sel_s, [], lambda e, d=sel_s.ap, s=sel_i: e.dma_start(out=d, in_=s), step=16)
    eb_row = P.sb("eb_row", [1, NEX])
    P.write(P.DQ, eb_row, [], lambda e, d=eb_row.ap, s=eb_i: e.dma_start(out=d, in_=s), step=16)
    idt16 = P.sb("idt16_s", [16, 16], BF16)
    P.write(P.DQ, idt16, [], lambda e, d=idt16.ap, s=idt16_i: e.dma_start(out=d, in_=s), step=16)
    idt128f = P.sb("idt128f_s", [128, 128], BF16)
    P.write(P.DQ, idt128f, [], lambda e, d=idt128f.ap, s=idt128f_i: e.dma_start(out=d, in_=s), step=16)

    xt = P.sb("xt", [128, NCH * TW])
    xbh = P.sb("xbh", [128, NCH * 2 * TW], BF16)
    ct = xbh
    acc = P.sb("acc", [128, NCH * 2 * TW])
    aw = P.sb("aw", [128, 8 * 2 * TW], BF16)
    gcol = [P.sb(f"gcol{i}", [128, D], BF16) for i in range(2)]
    ucol = [P.sb(f"ucol{i}", [128, D], BF16) for i in range(2)]
    dcol = [P.sb(f"dcol{i}", [128, 1024], BF16) for i in range(2)]
    wb_sb = P.sb("wb_sb", [128, 2 * TW], BF16)
    r_sg = P.sb("r_sg", [128, TW])
    r_t = P.sb("r_t", [128, TW])
    ln_stage = P.sb("ln_stage", [128, TW], BF16)
    sg_T = P.sb("sg_T", [16, TC], BF16)
    wT_sb = P.sb("wT_sb", [16, TC], BF16)
    sc_tm = P.sb("sc_tm", [128, NEX])
    ebB = P.sb("ebB", [128, NEX])
    r_s4c = P.sb("r_s4c", [128, NEX])
    r_m1 = P.sb("r_m1", [128, 4])
    r_m2 = P.sb("r_m2", [128, 4])
    r_eq = P.sb("r_eq", [128, NEX])
    r_msk = P.sb("r_msk", [128, NEX])
    r_ms0 = P.sb("r_ms0", [128, NEX])
    r_gsc = P.sb("r_gsc", [128, 4])
    r_gm = P.sb("r_gm", [128, 4])
    r_pen = P.sb("r_pen", [128, 4])
    r_mx = P.sb("r_mx", [128, 1])
    r_w = P.sb("r_w", [128, NEX])
    r_wb = P.sb("r_wb", [128, NEX], BF16)
    r_sum = P.sb("r_sum", [128, 1])
    r_rcp = P.sb("r_rcp", [128, 1])
    cc25 = P.constcol(2.5)

    ps_a = P.ps("ps_a", [128, TW])
    ps_b = P.ps("ps_b", [128, TW])
    ps_g = P.ps("ps_g", [128, TW])
    ps_u = P.ps("ps_u", [128, TW])
    ps_d = P.ps("ps_d", [128, TW])
    ps_wb = P.ps("ps_wb", [128, TW])
    ps_r16 = P.ps("ps_r16", [16, TW])
    ps_tr = P.ps("ps_tr", [128, 128], BF16)

    if with_outproj:
        ws_wo = WStream(P, "wo", WoT_i, NCH, nbuf=1)

    for tt in range(nTT):
        P.write(P.DQ, xt, [], lambda e, d=xt.ap, s=xT_i[tt, :, :]:
                e.dma_start(out=d, in_=s), step=16)
        if with_outproj:
            oq = io.get("O_queue", P.DQ)
            P.write(oq, ct, [], lambda e, d=ct.ap[:, 0:NCH * TW], tt=tt:
                    e.dma_start(out=d, in_=O_i[tt, :, :]), step=16)
            o_chunks = [(ct, ct.ap[:, c * TW:(c + 1) * TW]) for c in range(NCH)]
            for mc in range(NCH):
                if mc + 1 < NCH:
                    ws_wo.prefetch(mc + 1)
                psm = ps_g if mc % 2 == 0 else ps_u
                pieces = [(ws_wo.chunk(mc, kc)[0], ws_wo.chunk(mc, kc)[1],
                           o_chunks[kc][0], o_chunks[kc][1]) for kc in range(NCH)]
                emit_mm(P, psm, None, pieces)
                P.rw(P.VQ, xt, [psm],
                     lambda e, o=xt.ap[:, mc * TW:(mc + 1) * TW], i=psm.ap,
                     xx=xt.ap[:, mc * TW:(mc + 1) * TW]:
                     e.tensor_tensor(out=o, in0=i, in1=xx, op=ALU.add))
        x_list = [(xt, xt.ap[:, c * TW:(c + 1) * TW]) for c in range(NCH)]

        def ln_out(c, t1, mrB, tt=tt):
            P.write(P.VQ, ln_stage, [t1, mrB],
                    lambda e, o=ln_stage.ap, a=t1.ap, b=mrB.ap:
                    e.tensor_sub(out=o, in0=a, in1=b))
            P.read(P.DQ, ln_stage,
                   lambda e, d=xln_scr[tt, :, c * TW:(c + 1) * TW], s=ln_stage.ap:
                   e.dma_start(out=d, in_=s), step=16)

        emit_ln(P, kit, x_list, ps_a, ps_b, ln_out)
        if with_outproj:
            P.read(P.DQ, xt, lambda e, d=x_scr[tt, :, :], s=xt.ap:
                   e.dma_start(out=d, in_=s), step=16)

    def xbh_sl(c, tt2):
        return xbh.ap[:, (c * 2 + tt2) * TW:(c * 2 + tt2 + 1) * TW]

    P.TQ.wait(eb_row.wtick)
    P.TQ.wait(kit.ones_f.wtick)
    t = P.TQ.run(lambda e, o=ps_wb.ap[:, 0:NEX], w=kit.ones_f.ap, r=eb_row.ap:
                 e.matmul(o, w, r, start=True, stop=True))
    eb_row.rticks.append(t)
    ps_wb.rticks = []
    ps_wb.wtick = t
    P.write(P.VQ, ebB, [ps_wb], lambda e, o=ebB.ap, i=ps_wb.ap[:, 0:NEX]:
            e.tensor_copy(out=o, in_=i))

    for tt in range(nTT):
        P.write(P.DQ, xbh, [],
                lambda e, d=xbh.ap[:, 0:NCH * TW].rearrange("p (c t) -> p c t", c=NCH),
                s=xln_scr[tt, :, :].rearrange("p (c t) -> p c t", c=NCH):
                e.dma_start(out=d, in_=s), step=16)
        pieces = [(gwres.chunk(0, kc)[0], gwres.chunk(0, kc)[1],
                   xbh, xbh.ap[:, kc * TW:(kc + 1) * TW]) for kc in range(NCH)]
        emit_mm(P, ps_r16, None, pieces)
        P.write(P.SQ, sg_T, [ps_r16],
                lambda e, o=sg_T.ap[:, tt * TW:(tt + 1) * TW], i=ps_r16.ap:
                e.activation(out=o, in_=i, func=AF.Sigmoid))
    for j in range(TC // 128):
        emit_group(P, P.TQ, ps_tr, [sg_T, idt16],
                   [lambda e, o=ps_tr.ap[:, 0:NEX], i=sg_T.ap[:, j * 128:(j + 1) * 128],
                    dd=idt16.ap: e.transpose(o, i, dd)])
        P.write(P.VQ, sc_tm, [ps_tr], lambda e, o=sc_tm.ap, i=ps_tr.ap[:, 0:NEX]:
                e.tensor_copy(out=o, in_=i))
        P.write(P.VQ, r_s4c, [sc_tm, ebB], lambda e, o=r_s4c.ap, a=sc_tm.ap, b=ebB.ap:
                e.tensor_add(out=o, in0=a, in1=b))
        for g in range(4):
            P.write(P.VQ, r_m1, [r_s4c],
                    lambda e, o=r_m1.ap[:, g:g + 1], i=r_s4c.ap[:, 4 * g:4 * g + 4]:
                    e.reduce_max(out=o, in_=i, axis=AX.X))
        for g in range(4):
            P.write(P.VQ, r_eq, [r_s4c, r_m1],
                    lambda e, o=r_eq.ap[:, 4 * g:4 * g + 4],
                    i=r_s4c.ap[:, 4 * g:4 * g + 4], m=r_m1.ap[:, g:g + 1]:
                    e.tensor_scalar(out=o, in0=i, scalar1=m, scalar2=None,
                                    op0=ALU.is_equal))
        P.write(P.VQ, r_msk, [r_eq, r_s4c],
                lambda e, o=r_msk.ap, a=r_eq.ap, b=r_s4c.ap:
                e.scalar_tensor_tensor(out=o, in0=a, scalar=NEG_BIG, in1=b,
                                       op0=ALU.mult, op1=ALU.add))
        for g in range(4):
            P.write(P.VQ, r_m2, [r_msk],
                    lambda e, o=r_m2.ap[:, g:g + 1], i=r_msk.ap[:, 4 * g:4 * g + 4]:
                    e.reduce_max(out=o, in_=i, axis=AX.X))
        P.write(P.VQ, r_gsc, [r_m1, r_m2], lambda e, o=r_gsc.ap, a=r_m1.ap, b=r_m2.ap:
                e.tensor_add(out=o, in0=a, in1=b))
        P.write(P.VQ, r_mx, [r_gsc], lambda e, o=r_mx.ap, i=r_gsc.ap:
                e.reduce_max(out=o, in_=i, axis=AX.X))
        P.write(P.VQ, r_gm, [r_gsc, r_mx],
                lambda e, o=r_gm.ap, i=r_gsc.ap, m=r_mx.ap[:, 0:1]:
                e.tensor_scalar(out=o, in0=i, scalar1=m, scalar2=None,
                                op0=ALU.is_equal))
        P.rw(P.VQ, r_gm, [r_gsc],
             lambda e, o=r_gm.ap, a=r_gm.ap, b=r_gsc.ap:
             e.scalar_tensor_tensor(out=o, in0=a, scalar=NEG_BIG, in1=b,
                                    op0=ALU.mult, op1=ALU.add))
        P.write(P.VQ, r_mx, [r_gm], lambda e, o=r_mx.ap, i=r_gm.ap:
                e.reduce_max(out=o, in_=i, axis=AX.X))
        P.write(P.VQ, r_gm, [r_gsc, r_mx],
                lambda e, o=r_gm.ap, i=r_gsc.ap, m=r_mx.ap[:, 0:1]:
                e.tensor_scalar(out=o, in0=i, scalar1=m, scalar2=None,
                                op0=ALU.is_ge))
        P.write(P.VQ, r_pen, [r_gm],
                lambda e, o=r_pen.ap, i=r_gm.ap:
                e.tensor_scalar(out=o, in0=i, scalar1=-1.0, scalar2=-NEG_BIG,
                                op0=ALU.add, op1=ALU.mult))
        for g in range(4):
            P.write(P.VQ, r_ms0, [r_s4c, r_pen],
                    lambda e, o=r_ms0.ap[:, 4 * g:4 * g + 4],
                    i=r_s4c.ap[:, 4 * g:4 * g + 4], m=r_pen.ap[:, g:g + 1]:
                    e.tensor_scalar(out=o, in0=i, scalar1=m, scalar2=None,
                                    op0=ALU.add))
        P.write(P.VQ, r_msk, [r_ms0], lambda e, o=r_msk.ap, i=r_ms0.ap:
                e.tensor_copy(out=o, in_=i))
        for _round in range(3):
            P.write(P.VQ, r_mx, [r_msk], lambda e, o=r_mx.ap, i=r_msk.ap:
                    e.reduce_max(out=o, in_=i, axis=AX.X))
            P.write(P.VQ, r_eq, [r_msk, r_mx],
                    lambda e, o=r_eq.ap, i=r_msk.ap, m=r_mx.ap[:, 0:1]:
                    e.tensor_scalar(out=o, in0=i, scalar1=m, scalar2=None,
                                    op0=ALU.is_equal))
            P.rw(P.VQ, r_msk, [r_eq],
                 lambda e, o=r_msk.ap, a=r_eq.ap, b=r_msk.ap:
                 e.scalar_tensor_tensor(out=o, in0=a, scalar=NEG_BIG, in1=b,
                                        op0=ALU.mult, op1=ALU.add))
        P.write(P.VQ, r_mx, [r_msk], lambda e, o=r_mx.ap, i=r_msk.ap:
                e.reduce_max(out=o, in_=i, axis=AX.X))
        P.write(P.VQ, r_eq, [r_ms0, r_mx],
                lambda e, o=r_eq.ap, i=r_ms0.ap, m=r_mx.ap[:, 0:1]:
                e.tensor_scalar(out=o, in0=i, scalar1=m, scalar2=None,
                                op0=ALU.is_ge))
        P.write(P.VQ, r_w, [sc_tm, r_eq], lambda e, o=r_w.ap, a=sc_tm.ap, b=r_eq.ap:
                e.tensor_mul(out=o, in0=a, in1=b))
        P.write(P.VQ, r_sum, [r_w], lambda e, o=r_sum.ap, i=r_w.ap:
                e.reduce_sum(out=o, in_=i, axis=AX.X))
        P.rw(P.VQ, r_sum, [], lambda e, o=r_sum.ap:
             e.tensor_scalar_add(out=o, in0=o, scalar1=1e-20))
        P.write(P.VQ, r_rcp, [r_sum], lambda e, o=r_rcp.ap, i=r_sum.ap:
                e.reciprocal(out=o, in_=i))
        P.rw(P.VQ, r_w, [r_rcp, cc25],
             lambda e, o=r_w.ap, i=r_w.ap, m=r_rcp.ap[:, 0:1], c2=cc25.ap[:, 0:1]:
             e.tensor_scalar(out=o, in0=i, scalar1=m, scalar2=c2,
                             op0=ALU.mult, op1=ALU.mult))
        P.write(P.VQ, r_wb, [r_w],
                lambda e, o=r_wb.ap, i=r_w.ap: e.tensor_copy(out=o, in_=i))
        emit_group(P, P.TQ, ps_tr, [r_wb, idt128f],
                   [lambda e, o=ps_tr.ap[0:16, 0:128], i=r_wb.ap, dd=idt128f.ap:
                    e.transpose(o, i, dd)])
        P.write(P.VQ, wT_sb, [ps_tr],
                lambda e, o=wT_sb.ap[:, j * 128:(j + 1) * 128], i=ps_tr.ap[0:16, 0:128]:
                e.tensor_copy(out=o, in_=i))

    for H in range(2):
        for tt2 in range(2):
            P.write(P.DQ, xbh, [],
                    lambda e,
                    d=xbh.ap.rearrange("p (c two t) -> p c two t", two=2, t=TW)[:, :, tt2, :],
                    s=xln_scr[2 * H + tt2, :, :].rearrange("p (c t) -> p c t", c=NCH):
                    e.dma_start(out=d, in_=s), step=16)
        P.write(P.VQ, acc, [], lambda e, o=acc.ap: e.memset(o, 0.0))
        jobs = [("e", n) for n in range(NEX)] + [("s", 0), ("s", 1)]
        for (kind, n) in jobs:
            if kind == "e":
                for tt2 in range(2):
                    pieces = [(sel_s, sel_s.ap[:, n * 128:(n + 1) * 128],
                               wT_sb, wT_sb.ap[:, (2 * H + tt2) * TW:(2 * H + tt2 + 1) * TW])]
                    emit_mm(P, ps_wb, None, pieces)
                    P.write(P.VQ, wb_sb, [ps_wb],
                            lambda e, o=wb_sb.ap[:, tt2 * TW:(tt2 + 1) * TW],
                            i=ps_wb.ap: e.tensor_copy(out=o, in_=i))
            for fc in range(8):
                gc = gcol[fc % 2]
                uc = ucol[fc % 2]
                gsrc = gw_i[n, fc, :, :] if kind == "e" else shg_i[n * 8 + fc, :, :]
                usrc = uw_i[n, fc, :, :] if kind == "e" else shu_i[n * 8 + fc, :, :]
                P.write(P.PQ, gc, [], lambda e, d=gc.ap, s=gsrc:
                        e.dma_start(out=d, in_=s), step=16)
                P.write(P.PQ, uc, [], lambda e, d=uc.ap, s=usrc:
                        e.dma_start(out=d, in_=s), step=16)
                for tt2 in range(2):
                    pieces = [(gc, gc.ap[:, kc * 128:(kc + 1) * 128],
                               xbh, xbh_sl(kc, tt2)) for kc in range(NCH)]
                    emit_mm(P, ps_g, None, pieces)
                    pieces = [(uc, uc.ap[:, kc * 128:(kc + 1) * 128],
                               xbh, xbh_sl(kc, tt2)) for kc in range(NCH)]
                    emit_mm(P, ps_u, None, pieces)
                    P.write(P.SQ, r_sg, [ps_g],
                            lambda e, o=r_sg.ap, i=ps_g.ap:
                            e.activation(out=o, in_=i, func=AF.Silu))
                    P.write(P.VQ, r_t, [r_sg, ps_u],
                            lambda e, o=r_t.ap, a=r_sg.ap, b=ps_u.ap:
                            e.tensor_mul(out=o, in0=a, in1=b))
                    dst = aw.ap[:, (fc * 2 + tt2) * TW:(fc * 2 + tt2 + 1) * TW]
                    if kind == "e":
                        P.write(P.VQ, aw, [r_t, wb_sb],
                                lambda e, o=dst, a=r_t.ap,
                                b=wb_sb.ap[:, tt2 * TW:(tt2 + 1) * TW]:
                                e.tensor_mul(out=o, in0=a, in1=b))
                    else:
                        P.write(P.VQ, aw, [r_t],
                                lambda e, o=dst, a=r_t.ap: e.tensor_copy(out=o, in_=a))
            for hc in range(NCH):
                dc = dcol[hc % 2]
                dsrc = (dw_i[n, hc, :, :] if kind == "e"
                        else shd_i[hc, :, n * 1024:(n + 1) * 1024])
                P.write(P.PQ, dc, [], lambda e, d=dc.ap, s=dsrc:
                        e.dma_start(out=d, in_=s), step=16)
                nkc = 8
                for tt2 in range(2):
                    pieces = [(dc, dc.ap[:, kc * 128:(kc + 1) * 128],
                               aw, aw.ap[:, (kc * 2 + tt2) * TW:(kc * 2 + tt2 + 1) * TW])
                              for kc in range(nkc)]
                    emit_mm(P, ps_d, None, pieces)
                    P.rw(P.VQ, acc, [ps_d],
                         lambda e, o=acc.ap[:, (hc * 2 + tt2) * TW:(hc * 2 + tt2 + 1) * TW],
                         i=ps_d.ap,
                         xx=acc.ap[:, (hc * 2 + tt2) * TW:(hc * 2 + tt2 + 1) * TW]:
                         e.tensor_tensor(out=o, in0=i, in1=xx, op=ALU.add))
        for tt2 in range(2):
            tt = 2 * H + tt2
            P.write(P.DQ, xt, [], lambda e, d=xt.ap, s=x_scr[tt, :, :]:
                    e.dma_start(out=d, in_=s), step=16)
            for c in range(NCH):
                P.rw(P.VQ, xt, [acc],
                     lambda e, o=xt.ap[:, c * TW:(c + 1) * TW],
                     a=acc.ap[:, (c * 2 + tt2) * TW:(c * 2 + tt2 + 1) * TW],
                     xx=xt.ap[:, c * TW:(c + 1) * TW]:
                     e.tensor_tensor(out=o, in0=a, in1=xx, op=ALU.add))
            if emit_xln:
                x_list = [(xt, xt.ap[:, c * TW:(c + 1) * TW]) for c in range(NCH)]

                def ln_out2(c, t1, mrB, tt=tt):
                    P.write(P.VQ, ln_stage, [t1, mrB],
                            lambda e, o=ln_stage.ap, a=t1.ap, b=mrB.ap:
                            e.tensor_sub(out=o, in0=a, in1=b))
                    P.read(P.DQ, ln_stage,
                           lambda e, d=xln_o[tt, :, c * TW:(c + 1) * TW],
                           s=ln_stage.ap: e.dma_start(out=d, in_=s), step=16)

                emit_ln(P, kit, x_list, ps_a, ps_b, ln_out2)
            P.read(P.DQ, xt, lambda e, d=x_o[tt, :, :], s=xt.ap:
                   e.dma_start(out=d, in_=s), step=16)

    if standalone:
        return P.assemble()


def moe_host_weights(inp, prefix="L1_ffn"):
    gate_w = np.asarray(inp[f"{prefix}_gate_w"])
    up_w = np.asarray(inp[f"{prefix}_up_w"])
    down_w = np.asarray(inp[f"{prefix}_down_w"])
    gw = np.stack([swz_w(gate_w[n].T) for n in range(16)])
    uw = np.stack([swz_w(up_w[n].T) for n in range(16)])
    dw = np.stack([swz_w(down_w[n].T) for n in range(16)])
    shg = swz_w(np.asarray(inp[f"{prefix}_sh_gate_w"]).T)
    shu = swz_w(np.asarray(inp[f"{prefix}_sh_up_w"]).T)
    shd = swz_w(np.asarray(inp[f"{prefix}_sh_down_w"]).T)
    gwT = swz_w(np.asarray(inp[f"{prefix}_gate_weight"]).T, mcw=16)
    sel = np.zeros((16, 16 * 128), np.float32)
    for n in range(16):
        sel[n, n * 128:(n + 1) * 128] = 1.0
    return dict(gw=gw, uw=uw, dw=dw, shg=shg, shu=shu, shd=shd, gwT=gwT,
                sel=sel.astype(BF),
                eb=f32c(np.asarray(inp[f"{prefix}_e_score_correction_bias"]).reshape(1, 16)),
                idt16=np.eye(16).astype(BF),
                idt128f=np.eye(128).astype(BF))


def run_launch6(x, O, Wo, moe_w, with_outproj=True, emit_xln=True):
    hc = host_consts()
    nc = build_launch6(with_outproj, emit_xln)
    common = dict(moe_w)
    common.update(ones_bf=hc["ones_bf"], ones_f=hc["ones_f"])
    if with_outproj:
        common["WoT"] = swz_w(np.asarray(Wo))
    in_maps = []
    for c in range(NCORE):
        b, hf = c // 2, c % 2
        sl = slice(hf * TC, hf * TC + TC)
        m = dict(common)
        m["xT"] = swz_act(f32c(x[b, sl].T))
        if with_outproj:
            m["O"] = swz_act(np.ascontiguousarray(O[b, sl].T.astype(BF)))
        in_maps.append(m)
    res = run_bass_kernel_spmd(nc, in_maps, core_ids=list(range(NCORE)))
    xo = np.zeros((B, L, D), np.float32)
    xlno = np.zeros((B, L, D), np.float32) if emit_xln else None
    for c in range(NCORE):
        b, hf = c // 2, c % 2
        sl = slice(hf * TC, hf * TC + TC)
        r = res.results[c]
        xo[b, sl] = unswz_act(r["x"], D, TC).T
        if emit_xln:
            xlno[b, sl] = unswz_act(r["xln"].astype(np.float32), D, TC).T
    return xo, xlno



def build_launch7(P=None, io=None):
    standalone = P is None
    nTT = TC // TW
    if standalone:
        P = Prog()
        io = dict(
            xT=P.din("xT", [nTT, 128, NCH * TW]),
            xc=P.din("xc", [4, nTT, 128, NCH * TW], BF16),
            ar=P.din("ar", [128, 128]),
            ai=P.din("ai", [128, 128]),
            cr=P.din("cr", [128, 128]),
            ci=P.din("ci", [128, 128]),
            ldt8=P.din("ldt8", [128, 128]),
            dp=P.din("dp", [128, NCH]),
            x=P.dout("x", [nTT, 128, NCH * TW]),
        )
    xT_i = io["xT"]; xc_i = io["xc"]
    ar_i = io["ar"]; ai_i = io["ai"]; cr_i = io["cr"]; ci_i = io["ci"]
    ldt8_i = io["ldt8"]; dp_i = io["dp"]; x_o = io["x"]
    kmask_i = io.get("kmask")

    names = ["ar", "ai", "cr", "ci", "ldt8", "dt8", "t_a", "t_b", "t_c", "t_d",
             "e1r", "e1i", "Cr2", "Ci2", "er", "ei", "er2", "ei2", "prod"]
    w = {n: P.sb("w_" + n, [128, 128]) for n in names}
    kct = [P.sb(f"kct{t}", [128, NCH]) for t in range(4)]
    dp = P.sb("dp_s", [128, NCH])
    xt = P.sb("xt", [128, NCH * TW])
    cb = P.sb("cb", [128, NCH * TW], BF16)
    pihalf = P.constcol(math.pi / 2.0)

    for nm, src in (("ar", ar_i), ("ai", ai_i), ("cr", cr_i), ("ci", ci_i),
                    ("ldt8", ldt8_i)):
        P.write(P.DQ, w[nm], [], lambda e, d=w[nm].ap, s=src:
                e.dma_start(out=d, in_=s), step=16)
    P.write(P.DQ, dp, [], lambda e, d=dp.ap, s=dp_i: e.dma_start(out=d, in_=s), step=16)

    def S_(dst, srcb, func, bias=None, scale=None):
        kw = {}
        if bias is not None:
            kw["bias"] = bias
        if scale is not None:
            kw["scale"] = scale
        P.write(P.SQ, w[dst], [w[srcb]] + ([pihalf] if bias is not None else []),
                lambda e, o=w[dst].ap, i=w[srcb].ap, k=dict(kw):
                e.activation(out=o, in_=i, func=func, **k))

    def V2(dst, a, b_, op):
        P.write(P.VQ, w[dst], [w[a], w[b_]],
                lambda e, o=w[dst].ap, x=w[a].ap, y=w[b_].ap:
                e.tensor_tensor(out=o, in0=x, in1=y, op=op))

    def Vs(dst, a, s1, op):
        P.write(P.VQ, w[dst], [w[a]],
                lambda e, o=w[dst].ap, x=w[a].ap:
                e.tensor_single_scalar(out=o, in_=x, scalar=s1, op=op))

    S_("dt8", "ldt8", AF.Exp)
    S_("t_a", "ar", AF.Exp)
    Vs("ar", "t_a", -1.0, ALU.mult)
    V2("t_a", "ar", "dt8", ALU.mult)
    V2("t_b", "ai", "dt8", ALU.mult)
    S_("t_c", "t_a", AF.Exp)
    S_("t_d", "t_b", AF.Sin)
    V2("e1i", "t_c", "t_d", ALU.mult)
    Vs("t_d", "t_b", -1.0, ALU.mult)
    P.write(P.SQ, w["t_b"], [w["t_d"], pihalf],
            lambda e, o=w["t_b"].ap, i=w["t_d"].ap, b=pihalf.ap[:, 0:1]:
            e.activation(out=o, in_=i, func=AF.Sin, bias=b))
    V2("e1r", "t_c", "t_b", ALU.mult)
    V2("t_a", "ar", "ar", ALU.mult)
    V2("t_b", "ai", "ai", ALU.mult)
    V2("t_a", "t_a", "t_b", ALU.add)
    P.write(P.VQ, w["t_b"], [w["t_a"]],
            lambda e, o=w["t_b"].ap, i=w["t_a"].ap: e.reciprocal(out=o, in_=i),
            hazard=True)
    V2("t_c", "ar", "t_b", ALU.mult)
    V2("t_d", "ai", "t_b", ALU.mult)
    Vs("t_d", "t_d", -1.0, ALU.mult)
    Vs("t_a", "e1r", -1.0, ALU.add)
    V2("er", "t_a", "t_c", ALU.mult)
    V2("ei", "e1i", "t_d", ALU.mult)
    V2("er2", "t_a", "t_d", ALU.mult)
    V2("ei2", "e1i", "t_c", ALU.mult)
    V2("t_b", "er", "ei", ALU.subtract)
    V2("t_c", "er2", "ei2", ALU.add)
    V2("er", "cr", "t_b", ALU.mult)
    V2("ei", "ci", "t_c", ALU.mult)
    V2("Cr2", "er", "ei", ALU.subtract)
    Vs("Cr2", "Cr2", 2.0, ALU.mult)
    V2("er", "cr", "t_c", ALU.mult)
    V2("ei", "ci", "t_b", ALU.mult)
    V2("Ci2", "er", "ei", ALU.add)
    Vs("Ci2", "Ci2", 2.0, ALU.mult)

    def segred(dst_buf, src_name):
        P.write(P.VQ, dst_buf, [w[src_name]],
                lambda e, o=dst_buf.ap,
                i=w[src_name].ap.rearrange("p (c n) -> p c n", n=8):
                e.reduce_sum(out=o, in_=i, axis=AX.X), hazard=True)

    segred(kct[0], "Cr2")
    P.write(P.VQ, w["er"], [w["e1r"]],
            lambda e, o=w["er"].ap, i=w["e1r"].ap: e.tensor_copy(out=o, in_=i))
    P.write(P.VQ, w["ei"], [w["e1i"]],
            lambda e, o=w["ei"].ap, i=w["e1i"].ap: e.tensor_copy(out=o, in_=i))
    for t in range(1, 4):
        V2("t_a", "Cr2", "er", ALU.mult)
        V2("t_b", "Ci2", "ei", ALU.mult)
        V2("prod", "t_a", "t_b", ALU.subtract)
        segred(kct[t], "prod")
        if t < 3:
            V2("t_a", "er", "e1r", ALU.mult)
            V2("t_b", "ei", "e1i", ALU.mult)
            V2("t_c", "er", "e1i", ALU.mult)
            V2("t_d", "ei", "e1r", ALU.mult)
            V2("er2", "t_a", "t_b", ALU.subtract)
            V2("ei2", "t_c", "t_d", ALU.add)
            P.write(P.VQ, w["er"], [w["er2"]],
                    lambda e, o=w["er"].ap, i=w["er2"].ap: e.tensor_copy(out=o, in_=i))
            P.write(P.VQ, w["ei"], [w["ei2"]],
                    lambda e, o=w["ei"].ap, i=w["ei2"].ap: e.tensor_copy(out=o, in_=i))

    if kmask_i is not None:
        km_s = P.sb("km_s", [128, 4])
        P.write(P.DQ, km_s, [], lambda e, d=km_s.ap, s=kmask_i:
                e.dma_start(out=d, in_=s), step=16)
        for j in range(4):
            P.rw(P.VQ, kct[j], [km_s],
                 lambda e, o=kct[j].ap, i=kct[j].ap, m=km_s.ap[:, j:j + 1]:
                 e.tensor_scalar_mul(out=o, in0=i, scalar1=m))

    for tt in range(nTT):
        P.write(P.DQ, xt, [], lambda e, d=xt.ap, s=xT_i[tt, :, :]:
                e.dma_start(out=d, in_=s), step=16)
        xq_ = io.get("xc_queue", P.DQ)
        for j in range(4):
            P.write(xq_, cb, [], lambda e, d=cb.ap, j=j, tt=tt:
                    e.dma_start(out=d, in_=xc_i[j, tt, :, :]), step=16)
            for c in range(NCH):
                P.rw(P.VQ, xt, [cb, kct[j]],
                     lambda e, o=xt.ap[:, c * TW:(c + 1) * TW],
                     a=cb.ap[:, c * TW:(c + 1) * TW], k=kct[j].ap[:, c:c + 1],
                     xx=xt.ap[:, c * TW:(c + 1) * TW]:
                     e.scalar_tensor_tensor(out=o, in0=a, scalar=k, in1=xx,
                                            op0=ALU.mult, op1=ALU.add))
            if j == 0:
                for c in range(NCH):
                    P.rw(P.VQ, xt, [cb, dp],
                         lambda e, o=xt.ap[:, c * TW:(c + 1) * TW],
                         a=cb.ap[:, c * TW:(c + 1) * TW], k=dp.ap[:, c:c + 1],
                         xx=xt.ap[:, c * TW:(c + 1) * TW]:
                         e.scalar_tensor_tensor(out=o, in0=a, scalar=k, in1=xx,
                                                op0=ALU.mult, op1=ALU.add))
        P.read(P.DQ, xt, lambda e, d=x_o[tt, :, :], s=xt.ap:
               e.dma_start(out=d, in_=s), step=16)

    if standalone:
        return P.assemble()


def pmaj8(X):
    return np.ascontiguousarray(
        np.asarray(X, np.float32).reshape(16, 128, 8).transpose(1, 0, 2).reshape(128, 128))


def run_launch7(x6, xln6, D_param, log_dt, A_real_log, A_imag, C_re, C_im):
    nc = build_launch7()
    zeros_half = np.zeros((TC // TW, 128, NCH * TW), BF)
    ldt8 = np.ascontiguousarray(np.repeat(to_cols(log_dt), 8, axis=1))
    common = dict(ar=pmaj8(A_real_log), ai=pmaj8(A_imag), cr=pmaj8(C_re),
                  ci=pmaj8(C_im), ldt8=ldt8, dp=to_cols(D_param))
    in_maps = []
    for c in range(NCORE):
        b, hf = c // 2, c % 2
        sl = slice(hf * TC, hf * TC + TC)
        xc = np.stack([
            swz_act(np.ascontiguousarray(xln6[b - j, sl].T.astype(BF)))
            if b - j >= 0 else zeros_half
            for j in range(4)])
        m = dict(common)
        m["xT"] = swz_act(f32c(x6[b, sl].T))
        m["xc"] = xc
        in_maps.append(m)
    res = run_bass_kernel_spmd(nc, in_maps, core_ids=list(range(NCORE)))
    x7 = np.zeros((B, L, D), np.float32)
    for c in range(NCORE):
        b, hf = c // 2, c % 2
        sl = slice(hf * TC, hf * TC + TC)
        x7[b, sl] = unswz_act(res.results[c]["x"], D, TC).T
    return x7



def kernel(**inputs):
    inp = {k: np.asarray(v) for k, v in inputs.items()}
    return kernel_fused(inp)


def kernel_multilaunch(inp):
    x = f32c(inp["L0_attn_a_x"])
    ctx = run_launch1(x, inp["L0_attn_a_Wq"], inp["L0_attn_a_bq"],
                      inp["L0_attn_a_Wk"], inp["L0_attn_a_bk"],
                      inp["L0_attn_a_Wv"], inp["L0_attn_a_bv"])
    x2, q, k, v, wr = run_launch2(
        x, ctx, inp["L0_attn_a_Wo"], inp["L0_attn_a_bo"],
        inp["L0_ffn_weight"], inp["L0_ffn_rram_variation"],
        inp["L0_attn_b_w_qkv"], inp["L0_attn_b_b_qkv"])
    O = run_launch3(q, k, v)
    x4, qn, qr, kn, kr, vv = run_launch4(
        x2, O, inp["L0_attn_b_w_out"], inp["L0_attn_b_b_out"], wr,
        inp["L1_attn_a_Wqkv"], inp["L1_attn_a_Wk_up"], inp["L1_attn_a_Wv_up"])
    O1 = run_launch5(qn, qr, kn, kr, vv)
    moe_w = moe_host_weights(inp, "L1_ffn")
    x6, xln6 = run_launch6(x4, O1, inp["L1_attn_a_Wo"], moe_w,
                           with_outproj=True, emit_xln=True)
    x7 = run_launch7(x6, xln6, inp["L1_attn_b_D_param"], inp["L1_attn_b_log_dt"],
                     inp["L1_attn_b_A_real_log"], inp["L1_attn_b_A_imag"],
                     inp["L1_attn_b_C_re"], inp["L1_attn_b_C_im"])
    x8, _ = run_launch6(x7, None, None, moe_w, with_outproj=False, emit_xln=False)
    return x8.astype(np.float32)



_cc_counter = [0]


def emit_allgather(P, in_ap, out_ap, groups, wait_ticks):
    P.PQ.wait_many(wait_ticks)
    _cc_counter[0] += 1
    name = f"ccsem_{_cc_counter[0]}"
    semh = P.ctx.enter_context(P.nc.semaphore(name))
    P.PQ.runq(lambda e, i=in_ap, o=out_ap, g=groups, s=semh:
              e.collective_compute("AllGather", ALU.bypass, replica_groups=g,
                                   ins=[i.opt()], outs=[o.opt()]).then_inc(s, 1))
    return CCTick(semh, name)


class XCView:

    def __init__(self, gathered_ap, sec):
        self.g = gathered_ap
        self.sec = sec

    def __getitem__(self, idx):
        j, tt = idx[0], idx[1]
        r0 = (self.sec[j] * 4 + tt) * 128
        return self.g[r0:r0 + 128, :]


def emit_flash_fused(P, io, rope):
    NQT = L // 128
    qkv = io["qkv"]
    SEC = io["sec"]

    qn = [P.sb(f"fqn{i}", [128, L], BF16) for i in range(2)]
    kn = [P.sb(f"fkn{i}", [128, L], BF16) for i in range(2)]
    vraw = [P.sb(f"fvr{i}", [128, L], BF16) for i in range(2)]
    vv = [P.sb(f"fvv{i}", [128, L], BF16) for i in range(2)]
    if rope:
        qr = [P.sb(f"fqr{i}", [64, L], BF16) for i in range(2)]
        kr = [P.sb(f"fkr{i}", [64, L], BF16) for i in range(2)]
    ost = [P.sb(f"fost{i}", [128, L], BF16) for i in range(2)]
    masks = P.sb("fmasks", [128, 4 * TW], BF16)
    idt = P.sb("fidt", [128, 128], BF16)
    P_sb = P.sb("fP_sb", [128, TW], BF16)
    pt_sb = P.sb("fpt_sb", [128, TW], BF16)
    O_acc = P.sb("fO_acc", [128, 128])
    O_bf = P.sb("fO_bf", [128, 128], BF16)
    s_nmt = P.sb("fs_nmt", [128, 1])
    s_nmrun = P.sb("fs_nmrun", [128, 1])
    s_nmnew = P.sb("fs_nmnew", [128, 1])
    s_d = P.sb("fs_d", [128, 1])
    s_alpha = P.sb("fs_alpha", [128, 1])
    s_rsum = P.sb("fs_rsum", [128, 1])
    s_lrun = P.sb("fs_lrun", [128, 1])
    s_rcp = P.sb("fs_rcp", [128, 1])

    ps_s = [P.ps(f"fps_s{i}", [128, TW]) for i in range(2)]
    ps_t = P.ps("fps_t", [128, TW], BF16)
    ps_pv = P.ps("fps_pv", [128, 128])

    for mv_ in range(4):
        P.write(P.DQ, masks, [],
                lambda e, d=masks.ap[:, mv_ * TW:(mv_ + 1) * TW], s=io["masks"][mv_, :, :]:
                e.dma_start(out=d, in_=s), step=16)
    P.write(P.DQ, idt, [], lambda e, d=idt.ap, s=io["ident"][:, :]:
            e.dma_start(out=d, in_=s), step=16)
    P.PQ.wait(io["in_tick"])

    snap = io["snap"]
    HB = io["hbsz"]
    qkv_my = io["qkv_my"]
    myticks = []
    for r in range(2):
        t = P.PQ.run(lambda e, r=r:
                     e.dma_start(out=qkv_my[r * HB:(r + 1) * HB, :],
                                 in_=qkv[bass.ds(snap[io["hb_key"]] + r * SEC, HB), :]),
                     step=16)
        myticks.append(t)

    def load_rows(buf, base, local0, nrows):
        for r in range(2):
            P.DQ.wait(myticks[r])
            P.write(P.DQ, buf, [],
                    lambda e, d=buf.ap[0:nrows, r * TC:(r + 1) * TC],
                    s=qkv_my[r * HB + base + local0: r * HB + base + local0 + nrows, :]:
                    e.dma_start(out=d, in_=s), step=16)

    def load_head(hidx):
        sl = hidx % 2
        i = hidx
        if rope:
            load_rows(qn[sl], 0, i * 128, 128)
            load_rows(qr[sl], 1024, (i // 2) * 128 + 64 * (i % 2), 64)
            load_rows(kn[sl], 1536, (i // 4) * 128, 128)
            load_rows(kr[sl], 1792, 64 * (i // 4), 64)
            load_rows(vraw[sl], 1920, (i // 4) * 128, 128)
        else:
            load_rows(qn[sl], 0, i * 128, 128)
            load_rows(kn[sl], 1024, i * 128, 128)
            load_rows(vraw[sl], 2048, i * 128, 128)

    o_ticks = []
    load_head(0)
    for hidx in range(8):
        sl = hidx % 2
        if hidx + 1 < 8:
            load_head(hidx + 1)
        for tc in range(32):
            emit_group(P, P.TQ, ps_t, [vraw[sl], idt],
                       [lambda e, o=ps_t.ap[:, (tc % 4) * 128:(tc % 4 + 1) * 128],
                        i=vraw[sl].ap[:, tc * 128:(tc + 1) * 128], dd=idt.ap:
                        e.transpose(o, i, dd)])
            P.write(P.VQ, vv[sl], [ps_t],
                    lambda e, o=vv[sl].ap[:, tc * 128:(tc + 1) * 128],
                    i=ps_t.ap[:, (tc % 4) * 128:(tc % 4 + 1) * 128]:
                    e.tensor_copy(out=o, in_=i))
        ostage = ost[sl]
        for qt in range(NQT):
            KM = qt // 4 + 1
            for km in range(KM):
                psm = ps_s[km % 2]
                pieces = [(qn[sl], qn[sl].ap[:, qt * 128:(qt + 1) * 128],
                           kn[sl], kn[sl].ap[:, km * TW:(km + 1) * TW])]
                if rope:
                    pieces.append((qr[sl], qr[sl].ap[:, qt * 128:(qt + 1) * 128],
                                   kr[sl], kr[sl].ap[:, km * TW:(km + 1) * TW]))
                if km == qt // 4:
                    mv = qt % 4
                    pieces.append((idt, idt.ap,
                                   masks, masks.ap[:, mv * TW:(mv + 1) * TW]))
                emit_mm(P, psm, None, pieces)
                P.write(P.VQ, s_nmt, [psm],
                        lambda e, o=s_nmt.ap, i=psm.ap:
                        e.reduce_max(out=o, in_=i, axis=AX.X, negate=True),
                        hazard=True)
                if km == 0:
                    P.write(P.VQ, s_nmrun, [s_nmt],
                            lambda e, o=s_nmrun.ap, i=s_nmt.ap: e.tensor_copy(out=o, in_=i))
                else:
                    P.write(P.VQ, s_nmnew, [s_nmt, s_nmrun],
                            lambda e, o=s_nmnew.ap, a=s_nmrun.ap, b=s_nmt.ap:
                            e.tensor_tensor(out=o, in0=a, in1=b, op=ALU.min))
                    P.write(P.VQ, s_d, [s_nmnew, s_nmrun],
                            lambda e, o=s_d.ap, a=s_nmnew.ap, b=s_nmrun.ap:
                            e.tensor_sub(out=o, in0=a, in1=b))
                    P.write(P.VQ, s_nmrun, [s_nmnew],
                            lambda e, o=s_nmrun.ap, i=s_nmnew.ap: e.tensor_copy(out=o, in_=i))
                    P.write(P.SQ, s_alpha, [s_d],
                            lambda e, o=s_alpha.ap, i=s_d.ap:
                            e.activation(out=o, in_=i, func=AF.Exp))
                P.SQ.wait(psm.wtick)
                P.SQ.wait(s_nmrun.wtick)
                P.SQ.wait_many(P_sb.rticks)
                P.SQ.wait_many(s_rsum.rticks)
                if P_sb.wtick is not None:
                    P.SQ.wait(P_sb.wtick)
                t = P.SQ.run(lambda e, o=P_sb.ap, i=psm.ap, b=s_nmrun.ap[:, 0:1],
                             a=s_rsum.ap[:, 0:1]:
                             e.activation(out=o, in_=i, func=AF.Exp, bias=b,
                                          accum_out=a))
                P_sb.rticks = []
                P_sb.wtick = t
                s_rsum.rticks = []
                s_rsum.wtick = t
                psm.rticks.append(t)
                s_nmrun.rticks.append(t)
                if km == 0:
                    P.write(P.VQ, s_lrun, [s_rsum],
                            lambda e, o=s_lrun.ap, i=s_rsum.ap: e.tensor_copy(out=o, in_=i))
                else:
                    P.rw(P.VQ, s_lrun, [s_rsum, s_alpha],
                         lambda e, o=s_lrun.ap, a=s_lrun.ap, sc=s_alpha.ap[:, 0:1],
                         b=s_rsum.ap:
                         e.scalar_tensor_tensor(out=o, in0=a, scalar=sc, in1=b,
                                                op0=ALU.mult, op1=ALU.add))
                fns = [
                    (lambda e, o=ps_t.ap[:, j * 128:(j + 1) * 128],
                     i=P_sb.ap[:, j * 128:(j + 1) * 128], dd=idt.ap:
                     e.transpose(o, i, dd))
                    for j in range(4)
                ]
                emit_group(P, P.TQ, ps_t, [P_sb, idt], fns)
                P.write(P.VQ, pt_sb, [ps_t],
                        lambda e, o=pt_sb.ap, i=ps_t.ap: e.tensor_copy(out=o, in_=i))
                pieces = [(pt_sb, pt_sb.ap[:, j * 128:(j + 1) * 128],
                           vv[sl], vv[sl].ap[:, (4 * km + j) * 128:(4 * km + j + 1) * 128])
                          for j in range(4)]
                emit_mm(P, ps_pv, None, pieces)
                if km == 0:
                    P.write(P.VQ, O_acc, [ps_pv],
                            lambda e, o=O_acc.ap, i=ps_pv.ap: e.tensor_copy(out=o, in_=i))
                else:
                    P.rw(P.VQ, O_acc, [ps_pv, s_alpha],
                         lambda e, o=O_acc.ap, a=O_acc.ap, sc=s_alpha.ap[:, 0:1],
                         b=ps_pv.ap:
                         e.scalar_tensor_tensor(out=o, in0=a, scalar=sc, in1=b,
                                                op0=ALU.mult, op1=ALU.add))
            P.write(P.VQ, s_rcp, [s_lrun],
                    lambda e, o=s_rcp.ap, i=s_lrun.ap: e.reciprocal(out=o, in_=i),
                    hazard=True)
            P.write(P.VQ, O_bf, [O_acc, s_rcp],
                    lambda e, o=O_bf.ap, i=O_acc.ap, sc=s_rcp.ap[:, 0:1]:
                    e.tensor_scalar_mul(out=o, in0=i, scalar1=sc), fence=True)
            emit_group(P, P.TQ, ps_t, [O_bf, idt],
                       [lambda e, o=ps_t.ap[:, 0:128], i=O_bf.ap, dd=idt.ap:
                        e.transpose(o, i, dd)])
            P.write(P.VQ, ostage, [ps_t],
                    lambda e, o=ostage.ap[:, qt * 128:(qt + 1) * 128],
                    i=ps_t.ap[:, 0:128]: e.tensor_copy(out=o, in_=i))
        t = P.read(P.PQ, ostage, lambda e, d=io["Ocat"][hidx * 128:(hidx + 1) * 128, :],
                   s=ostage.ap: e.dma_start(out=d, in_=s), step=16)
        o_ticks.append(t)
    return o_ticks


def emit_ph1_fused(P, io):
    nTT = TC // TW
    SCALE = 128.0 ** -0.5

    maskc = P.sb("p1maskc", [128, 1])
    P.write(P.DQ, maskc, [], lambda e, d=maskc.ap, s=io["maskc"]:
            e.dma_start(out=d, in_=s), step=16)
    bqs = P.sb("p1bq", [128, NCH])
    bks = P.sb("p1bk", [128, NCH])
    bvs = P.sb("p1bv", [128, NCH])
    for buf, key in ((bqs, "bq"), (bks, "bk"), (bvs, "bv")):
        P.write(P.DQ, buf, [], lambda e, d=buf.ap, s=io[key]: e.dma_start(out=d, in_=s), step=16)
    xln = P.sb("p1xln", [128, NCH * TC], BF16)

    def xln_sl(c, tt):
        return xln.ap[:, (c * nTT + tt) * TW:(c * nTT + tt + 1) * TW]

    with P.arena():
        kit = LNKit(P, tag="p1")
        kit.load_consts(io["ones_bf"], io["ones_f"])
        xt = P.sb("p1xt", [128, NCH * TW])
        ps_a = P.ps("p1ps_a", [128, TW])
        ps_b = P.ps("p1ps_b", [128, TW])
        for tt in range(nTT):
            P.write(P.DQ, xt, [], lambda e, d=xt.ap, s=io["xT"][tt, :, :]:
                    e.dma_start(out=d, in_=s), step=16)
            x_list = [(xt, xt.ap[:, c * TW:(c + 1) * TW]) for c in range(NCH)]

            def ln_out(c, t1, mrB, tt=tt):
                P.write(P.VQ, xln, [t1, mrB],
                        lambda e, o=xln_sl(c, tt), a=t1.ap, b=mrB.ap:
                        e.tensor_sub(out=o, in0=a, in1=b))

            emit_ln(P, kit, x_list, ps_a, ps_b, ln_out)

    GN = 8
    ctx_bf = P.sb("p1ctx", [128, GN * TC], BF16)
    phisc = P.sb("p1phisc", [128, GN * TC], BF16)
    s_qf = P.sb("p1qf", [128, TW])
    s_phq = P.sb("p1phq", [128, TW])
    s_mn = P.sb("p1mn", [128, TW])
    s_ex = P.sb("p1ex", [128, TW])
    s_phk = P.sb("p1phk", [128, TW])
    s_vf = P.sb("p1vf", [128, TW])
    s_kvs = P.sb("p1kvs", [128, TW])
    zeros = P.sb("p1zeros", [128, TW])
    carry = P.sb("p1carry", [128, GN])
    carry_use = P.sb("p1cu", [128, GN])
    ps_q = P.ps("p1ps_q", [128, TW])
    ps_k = P.ps("p1ps_k", [128, TW])
    ps_v = P.ps("p1ps_v", [128, TW])

    P.write(P.VQ, zeros, [], lambda e, o=zeros.ap: e.memset(o, 0.0))

    ws_q = WStream(P, "p1wq", io["WqT"], NCH)
    ws_k = WStream(P, "p1wk", io["WkT"], NCH)
    ws_v = WStream(P, "p1wv", io["WvT"], NCH)

    def gsl(buf, gc, tt):
        return buf.ap[:, (gc * nTT + tt) * TW:(gc * nTT + tt + 1) * TW]

    def phi(src_psum, bias_buf, bias_col, out_buf):
        P.write(P.VQ, s_qf, [src_psum, bias_buf],
                lambda e, o=s_qf.ap, i=src_psum.ap, b=bias_col:
                e.tensor_scalar_add(out=o, in0=i, scalar1=b))
        P.write(P.PQ, s_mn, [s_qf],
                lambda e, o=s_mn.ap, i=s_qf.ap: e.tensor_scalar_min(out=o, in0=i, scalar1=0.0),
                step=1)
        P.write(P.SQ, s_ex, [s_mn],
                lambda e, o=s_ex.ap, i=s_mn.ap: e.activation(out=o, in_=i, func=AF.Exp))
        P.write(P.VQ, out_buf, [s_qf, s_ex],
                lambda e, o=out_buf.ap, a=s_qf.ap, x=s_ex.ap:
                e.scalar_tensor_tensor(out=o, in0=a, scalar=0.0, in1=x,
                                       op0=ALU.max, op1=ALU.add))

    for g in range(2):
        P.write(P.VQ, carry, [], lambda e, o=carry.ap: e.memset(o, 0.0))
        for gc in range(GN):
            mc = g * GN + gc
            for ws in (ws_q, ws_k, ws_v):
                if mc + 1 < NCH:
                    ws.prefetch(mc + 1)
            for tt in range(nTT):
                for (ws, psm) in ((ws_q, ps_q), (ws_k, ps_k), (ws_v, ps_v)):
                    pieces = [(ws.chunk(mc, kc)[0], ws.chunk(mc, kc)[1],
                               xln, xln_sl(kc, tt)) for kc in range(NCH)]
                    emit_mm(P, psm, None, pieces)
                phi(ps_q, bqs, bqs.ap[:, mc:mc + 1], s_phq)
                phi(ps_k, bks, bks.ap[:, mc:mc + 1], s_phk)
                P.write(P.VQ, s_vf, [ps_v, bvs],
                        lambda e, o=s_vf.ap, i=ps_v.ap, b=bvs.ap[:, mc:mc + 1]:
                        e.tensor_scalar_add(out=o, in0=i, scalar1=b))
                P.rw(P.VQ, s_phk, [s_vf],
                     lambda e, o=s_phk.ap, a=s_phk.ap, b=s_vf.ap:
                     e.tensor_mul(out=o, in0=a, in1=b))
                P.write(P.VQ, s_kvs, [s_phk, zeros, carry],
                        lambda e, o=s_kvs.ap, d0=s_phk.ap, d1=zeros.ap,
                        ini=carry.ap[:, gc:gc + 1]:
                        e.tensor_tensor_scan(out=o, data0=d0, data1=d1, initial=ini,
                                             op0=ALU.add, op1=ALU.bypass), hazard=True)
                P.rw(P.PQ, carry, [s_kvs],
                     lambda e, o=carry.ap[:, gc:gc + 1], i=s_kvs.ap[:, TW - 1:TW]:
                     e.tensor_copy(out=o, in_=i), step=1)
                P.write(P.VQ, phisc, [s_phq],
                        lambda e, o=gsl(phisc, gc, tt), a=s_phq.ap:
                        e.tensor_scalar_mul(out=o, in0=a, scalar1=SCALE))
                P.write(P.VQ, ctx_bf, [s_phq, s_kvs],
                        lambda e, o=gsl(ctx_bf, gc, tt), a=s_phq.ap, b=s_kvs.ap:
                        e.scalar_tensor_tensor(out=o, in0=a, scalar=SCALE, in1=b,
                                               op0=ALU.mult, op1=ALU.mult), fence=True)
        carry_src = P.dtmp(f"p1carry_src{g}", [128, GN])
        carry_all = P.dtmp(f"p1carry_all{g}", [256, GN])
        t_st = P.read(P.DQ, carry, lambda e, d=carry_src[:, :], s=carry.ap:
                      e.dma_start(out=d, in_=s), step=16)
        t_cc = emit_allgather(P, carry_src, carry_all,
                              [[0, 1], [2, 3], [4, 5], [6, 7]], [t_st])
        P.DQ.wait(t_cc)
        P.write(P.DQ, carry_use, [], lambda e, d=carry_use.ap, s=carry_all[0:128, :]:
                e.dma_start(out=d, in_=s), step=16)
        P.rw(P.VQ, carry_use, [maskc],
             lambda e, o=carry_use.ap, i=carry_use.ap, m=maskc.ap[:, 0:1]:
             e.tensor_scalar_mul(out=o, in0=i, scalar1=m))
        for tt in range(nTT):
            for gc in range(GN):
                mc = g * GN + gc
                P.rw(P.VQ, ctx_bf, [phisc, carry_use],
                     lambda e, o=gsl(ctx_bf, gc, tt), a=gsl(phisc, gc, tt),
                     k=carry_use.ap[:, gc:gc + 1], xx=gsl(ctx_bf, gc, tt):
                     e.scalar_tensor_tensor(out=o, in0=a, scalar=k, in1=xx,
                                            op0=ALU.mult, op1=ALU.add))
                P.read(P.DQ, ctx_bf,
                       lambda e, d=io["ctx"][tt, :, mc * TW:(mc + 1) * TW],
                       s=gsl(ctx_bf, gc, tt): e.dma_start(out=d, in_=s), step=16)



I32 = mybir.dt.int32


class RowMapView:

    def __init__(self, ap, rowmap, nrows=128):
        self.ap_ = ap
        self.rm = rowmap
        self.n = nrows

    def __getitem__(self, idx):
        mc = idx[0]
        r0 = self.rm(mc)
        return self.ap_[r0:r0 + self.n, :]


class OViewStatic:

    def __init__(self, ap):
        self.ap_ = ap

    def __getitem__(self, idx):
        tt = idx[0]
        return self.ap_.rearrange("(c p) t -> p c t", p=128)[:, :, tt * TW:(tt + 1) * TW]


class XCViewStatic:

    def __init__(self, ap):
        self.ap_ = ap

    def __getitem__(self, idx):
        j, tt = idx[0], idx[1]
        return self.ap_[(j * 4 + tt) * 128:(j * 4 + tt + 1) * 128, :]


def build_fused(debug=False):
    P = Prog()
    nTT = TC // TW
    PAIRS = [[0, 1], [2, 3], [4, 5], [6, 7]]
    ALL8 = [list(range(8))]

    ones_bf = P.din("ones_bf", [128, 1], BF16)
    ones_f = P.din("ones_f", [1, 128])
    ident = P.din("ident", [128, 128], BF16)
    masks = P.din("masks", [4, 128, TW], BF16)
    offs_i = P.din("offs", [1, 16], I32)
    maskc_i = P.din("maskc", [128, 1])
    kmask_i = P.din("kmask", [128, 4])

    x_in = P.din("xT1", [nTT, 128, NCH * TW])
    l1 = {k: P.din("l1_" + k, [NCH, 128, D]) for k in ("WqT", "WkT", "WvT")}
    l1b = {k: P.din("l1_" + k, [128, NCH]) for k in ("bq", "bk", "bv")}
    l2 = {k: P.din("l2_" + k, [NCH, 128, D]) for k in
          ("WoT", "wT", "rrT", "WqT", "WkT", "WvT")}
    l2b = {k: P.din("l2_" + k, [128, NCH]) for k in ("bo", "bq", "bk", "bv")}
    l4_WoT = P.din("l4_WoT", [NCH, 128, D])
    l4_bo = P.din("l4_bo", [128, NCH])
    l4_Wqkv = P.din("l4_Wqkv", [28, 128, D])
    l4_Wku = P.din("l4_Wku", [6, 128, 512])
    l4_Wvu = P.din("l4_Wvu", [4, 128, 512])
    l4_cosq = P.din("l4_cosq", [128, TC], BF16)
    l4_sinq = P.din("l4_sinq", [128, TC], BF16)
    l4_cosk = P.din("l4_cosk", [128, TC], BF16)
    l4_sink = P.din("l4_sink", [128, TC], BF16)
    l4_RT = P.din("l4_RT", [128, 128], BF16)
    l6_WoT = P.din("l6_WoT", [NCH, 128, D])
    moe = dict(
        gwT=P.din("m_gwT", [1, 128, 16 * NCH]),
        eb=P.din("m_eb", [1, 16]),
        sel=P.din("m_sel", [16, 16 * 128], BF16),
        gw=P.din("m_gw", [16, 8, 128, D]),
        uw=P.din("m_uw", [16, 8, 128, D]),
        dw=P.din("m_dw", [16, 16, 128, 1024]),
        shg=P.din("m_shg", [16, 128, D]),
        shu=P.din("m_shu", [16, 128, D]),
        shd=P.din("m_shd", [16, 128, D]),
        idt16=P.din("m_idt16", [16, 16], BF16),
        idt128f=P.din("m_idt128f", [128, 128], BF16),
    )
    s4d = {k: P.din("s_" + k, [128, 128]) for k in ("ar", "ai", "cr", "ci", "ldt8")}
    s4d_dp = P.din("s_dp", [128, NCH])

    x8_o = P.dout("x8", [nTT, 128, NCH * TW])

    ctx_scr = P.dtmp("ctx_scr", [nTT, 128, NCH * TW], BF16)
    x2_scr = P.dtmp("x2_scr", [nTT, 128, NCH * TW])
    qkv_cat = P.dtmp("qkv_cat", [6144, TC], BF16)
    qkv_all = P.dtmp("qkv_all", [12288, TC], BF16)
    Ocat = P.dtmp("Ocat", [1024, L], BF16)
    Ocat_all = P.dtmp("Ocat_all", [2048, L], BF16)
    x4_scr = P.dtmp("x4_scr", [nTT, 128, NCH * TW])
    qkv2_cat = P.dtmp("qkv2_cat", [4352, TC], BF16)
    qkv2_all = P.dtmp("qkv2_all", [8704, TC], BF16)
    Ocat2 = P.dtmp("Ocat2", [1024, L], BF16)
    Ocat2_all = P.dtmp("Ocat2_all", [2048, L], BF16)
    x6_scr = P.dtmp("x6_scr", [nTT, 128, NCH * TW])
    xln6_scr = P.dtmp("xln6_scr", [nTT, 128, NCH * TW], BF16)
    xc_all = P.dtmp("xc_all", [8 * 512, NCH * TW], BF16)
    x7_scr = P.dtmp("x7_scr", [nTT, 128, NCH * TW])
    wr_scr = P.dtmp("wr_scr", [NCH, 128, D], BF16)

    offs_sb = P.sb("offs_sb", [1, 16], I32)
    t_ofs = P.write(P.DQ, offs_sb, [], lambda e, d=offs_sb.ap, s=offs_i:
                    e.dma_start(out=d, in_=s), step=16)
    snapH = {}
    REGTAB = [(0, "h3072", 3072), (1, "h2176", 2176), (2, "ocol", 2048),
              (3, "xc0", 3584), (4, "xc1", 3584), (5, "xc2", 3584), (6, "xc3", 3584)]
    P.PQ.wait(t_ofs)

    def _setup_regs(e, H=snapH, tab=REGTAB, sb=offs_sb):
        for idx, key, mx in tab:
            r = e.alloc_register()
            e.reg_load(r, sb.ap[0:1, idx:idx + 1])
            H[key] = e.snap(r, min_val=0, max_val=mx)
    P.PQ.runq(_setup_regs)

    qkv_my1 = P.dtmp("qkv_my1", [6144, TC], BF16)
    qkv_my2 = P.dtmp("qkv_my2", [4352, TC], BF16)
    O_my1 = P.dtmp("O_my1", [2048, TC], BF16)
    O_my2 = P.dtmp("O_my2", [2048, TC], BF16)
    xc_my = P.dtmp("xc_my", [2048, NCH * TW], BF16)

    with P.arena():
        P.pfx = "f1_"
        emit_ph1_fused(P, dict(
            xT=x_in, WqT=l1["WqT"], WkT=l1["WkT"], WvT=l1["WvT"],
            bq=l1b["bq"], bk=l1b["bk"], bv=l1b["bv"],
            ones_bf=ones_bf, ones_f=ones_f, maskc=maskc_i, ctx=ctx_scr))
    with P.arena():
        P.pfx = "f2_"
        build_launch2(P, dict(
            xT=x_in, ctx=ctx_scr, WoT=l2["WoT"], bo=l2b["bo"],
            wT=l2["wT"], rrT=l2["rrT"], WqT=l2["WqT"], WkT=l2["WkT"],
            WvT=l2["WvT"], bq=l2b["bq"], bk=l2b["bk"], bv=l2b["bv"],
            ones_bf=ones_bf, ones_f=ones_f,
            x2=x2_scr,
            q=RowMapView(qkv_cat, lambda mc: (mc // 8) * 3072 + (mc % 8) * 128),
            k=RowMapView(qkv_cat, lambda mc: (mc // 8) * 3072 + 1024 + (mc % 8) * 128),
            v=RowMapView(qkv_cat, lambda mc: (mc // 8) * 3072 + 2048 + (mc % 8) * 128),
            wr=wr_scr))
    P.pfx = ""
    t_g1 = emit_allgather(P, qkv_cat, qkv_all, PAIRS, [])
    with P.arena():
        P.pfx = "f3_"
        emit_flash_fused(P, dict(
            qkv=qkv_all, sec=6144, hbsz=3072, hb_key="h3072", qkv_my=qkv_my1,
            snap=snapH, masks=masks, ident=ident, Ocat=Ocat, in_tick=t_g1), rope=False)
    P.pfx = ""
    t_g2 = emit_allgather(P, Ocat, Ocat_all, PAIRS, [])
    P.PQ.wait(t_g2)
    t_oc = P.PQ.run(lambda e: e.dma_start(
        out=O_my1[:, :], in_=Ocat_all[:, bass.ds(snapH["ocol"], TC)]), step=16)
    P.DQ.wait(t_oc)
    with P.arena():
        P.pfx = "f4_"
        build_launch4(P, dict(
            xT=x2_scr, O=OViewStatic(O_my1),
            WoT=l4_WoT, bo=l4_bo, wr=wr_scr, Wqkv=l4_Wqkv, Wku=l4_Wku,
            Wvu=l4_Wvu, cosq=l4_cosq, sinq=l4_sinq, cosk=l4_cosk, sink=l4_sink,
            RT=l4_RT, ones_bf=ones_bf, ones_f=ones_f,
            x4=x4_scr,
            qn=RowMapView(qkv2_cat, lambda mc: (mc // 8) * 2176 + (mc % 8) * 128),
            qr=RowMapView(qkv2_cat, lambda mc: (mc // 4) * 2176 + 1024 + (mc % 4) * 128),
            kn=RowMapView(qkv2_cat, lambda mc: (mc // 2) * 2176 + 1536 + (mc % 2) * 128),
            kr=RowMapView(qkv2_cat, lambda mc: mc * 2176 + 1792),
            vv=RowMapView(qkv2_cat, lambda mc: (mc // 2) * 2176 + 1920 + (mc % 2) * 128)))
    P.pfx = ""
    t_g3 = emit_allgather(P, qkv2_cat, qkv2_all, PAIRS, [])
    with P.arena():
        P.pfx = "f5_"
        emit_flash_fused(P, dict(
            qkv=qkv2_all, sec=4352, hbsz=2176, hb_key="h2176", qkv_my=qkv_my2,
            snap=snapH, masks=masks, ident=ident, Ocat=Ocat2, in_tick=t_g3), rope=True)
    P.pfx = ""
    t_g4 = emit_allgather(P, Ocat2, Ocat2_all, PAIRS, [])
    P.PQ.wait(t_g4)
    t_oc2 = P.PQ.run(lambda e: e.dma_start(
        out=O_my2[:, :], in_=Ocat2_all[:, bass.ds(snapH["ocol"], TC)]), step=16)
    P.DQ.wait(t_oc2)
    with P.arena():
        P.pfx = "f6_"
        io6 = dict(moe)
        io6.update(xT=x4_scr, O=OViewStatic(O_my2),
                   WoT=l6_WoT, ones_bf=ones_bf, ones_f=ones_f,
                   x=x6_scr, xln=xln6_scr)
        build_launch6(True, True, P, io6)
    P.pfx = ""
    t_g5 = emit_allgather(P, xln6_scr.rearrange("t p f -> (t p) f"), xc_all, ALL8, [])
    P.PQ.wait(t_g5)
    t_xc = None
    for j in range(4):
        t_xc = P.PQ.run(lambda e, j=j: e.dma_start(
            out=xc_my[j * 512:(j + 1) * 512, :],
            in_=xc_all[bass.ds(snapH["xc%d" % j], 512), :]), step=16)
        P.DQ.wait(t_xc)
    with P.arena():
        P.pfx = "f7_"
        io7 = dict(s4d)
        io7.update(xT=x6_scr, xc=XCViewStatic(xc_my),
                   dp=s4d_dp, x=x7_scr, kmask=kmask_i)
        build_launch7(P, io7)
    with P.arena():
        P.pfx = "f8_"
        io8 = dict(moe)
        io8.update(xT=x7_scr, ones_bf=ones_bf, ones_f=ones_f, x=x8_o)
        build_launch6(False, False, P, io8)
    P.pfx = ""

    if debug:
        dbgs = [("ctx_scr", ctx_scr, [4, 128, NCH * TW], BF16),
                ("x2_scr", x2_scr, [4, 128, NCH * TW], F32),
                ("qkv_cat", qkv_cat, [6144, TC], BF16),
                ("O_my1", O_my1, [2048, TC], BF16),
                ("x4_scr", x4_scr, [4, 128, NCH * TW], F32),
                ("qkv2_cat", qkv2_cat, [4352, TC], BF16),
                ("O_my2", O_my2, [2048, TC], BF16),
                ("x6_scr", x6_scr, [4, 128, NCH * TW], F32),
                ("xln6_scr", xln6_scr, [4, 128, NCH * TW], BF16),
                ("xc_my", xc_my, [2048, NCH * TW], BF16),
                ("x7_scr", x7_scr, [4, 128, NCH * TW], F32)]
        for name, ap, shape, dt_ in dbgs:
            o = P.dout("dbg_" + name, shape, dt_)
            nd = len(shape)
            P.PQ.run(lambda e, d=o, s=ap, nd=nd: e.dma_start(
                out=d[:, :] if nd == 2 else d[:, :, :],
                in_=s if nd == 2 else s[:, :, :]), step=16)

    return P.assemble()


def kernel_fused(inp):
    hc = host_consts()
    nc = build_fused()
    moe_w = moe_host_weights(inp, "L1_ffn")
    Wqkv_r = reorder_wqkv_cols(np.asarray(inp["L1_attn_a_Wqkv"]))
    Wku_r = reorder_wup_cols(np.asarray(inp["L1_attn_a_Wk_up"]))
    S192 = 192.0 ** -0.5
    cq_full, sq_full = rope_host_tables(S192)
    ck_full, sk_full = rope_host_tables(1.0)
    w_qkv = np.asarray(inp["L0_attn_b_w_qkv"])
    b_qkv = np.asarray(inp["L0_attn_b_b_qkv"])
    common = dict(
        ones_bf=hc["ones_bf"], ones_f=hc["ones_f"],
        ident=np.eye(128).astype(BF), masks=hc["masks"],
        l1_WqT=swz_w(np.asarray(inp["L0_attn_a_Wq"]).T),
        l1_WkT=swz_w(np.asarray(inp["L0_attn_a_Wk"]).T),
        l1_WvT=swz_w(np.asarray(inp["L0_attn_a_Wv"]).T),
        l1_bq=to_cols(inp["L0_attn_a_bq"]), l1_bk=to_cols(inp["L0_attn_a_bk"]),
        l1_bv=to_cols(inp["L0_attn_a_bv"]),
        l2_WoT=swz_w(np.asarray(inp["L0_attn_a_Wo"]).T),
        l2_bo=to_cols(inp["L0_attn_a_bo"]),
        l2_wT=swz_w(np.asarray(inp["L0_ffn_weight"]).T),
        l2_rrT=swz_w(np.asarray(inp["L0_ffn_rram_variation"]).T),
        l2_WqT=swz_w(w_qkv[0:D].T), l2_WkT=swz_w(w_qkv[D:2 * D].T),
        l2_WvT=swz_w(w_qkv[2 * D:3 * D].T),
        l2_bq=to_cols(b_qkv[0:D]), l2_bk=to_cols(b_qkv[D:2 * D]),
        l2_bv=to_cols(b_qkv[2 * D:3 * D]),
        l4_WoT=swz_w(np.asarray(inp["L0_attn_b_w_out"]).T),
        l4_bo=to_cols(inp["L0_attn_b_b_out"]),
        l4_Wqkv=swz_w(Wqkv_r), l4_Wku=swz_w(Wku_r),
        l4_Wvu=swz_w(np.asarray(inp["L1_attn_a_Wv_up"])),
        l4_RT=rot_matrix().astype(BF),
        l6_WoT=swz_w(np.asarray(inp["L1_attn_a_Wo"])),
        m_gwT=moe_w["gwT"], m_eb=moe_w["eb"], m_sel=moe_w["sel"],
        m_gw=moe_w["gw"], m_uw=moe_w["uw"], m_dw=moe_w["dw"],
        m_shg=moe_w["shg"], m_shu=moe_w["shu"], m_shd=moe_w["shd"],
        m_idt16=moe_w["idt16"], m_idt128f=moe_w["idt128f"],
        s_ar=pmaj8(inp["L1_attn_b_A_real_log"]), s_ai=pmaj8(inp["L1_attn_b_A_imag"]),
        s_cr=pmaj8(inp["L1_attn_b_C_re"]), s_ci=pmaj8(inp["L1_attn_b_C_im"]),
        s_ldt8=np.ascontiguousarray(np.repeat(to_cols(inp["L1_attn_b_log_dt"]), 8, axis=1)),
        s_dp=to_cols(inp["L1_attn_b_D_param"]),
    )
    x = f32c(np.asarray(inp["L0_attn_a_x"]))
    in_maps = []
    for c in range(NCORE):
        b, hf = c // 2, c % 2
        sl = slice(hf * TC, hf * TC + TC)
        offs = np.zeros((1, 16), np.int32)
        offs[0, 0] = 3072 * hf
        offs[0, 1] = 2176 * hf
        offs[0, 2] = 2048 * hf
        for j in range(4):
            offs[0, 3 + j] = (2 * max(b - j, 0) + hf) * 512
        kmask = np.zeros((128, 4), np.float32)
        for j in range(4):
            kmask[:, j] = 1.0 if b - j >= 0 else 0.0
        m = dict(common)
        m.update(
            xT1=swz_act(f32c(x[b, sl].T)),
            offs=offs,
            maskc=np.full((128, 1), float(hf), np.float32),
            kmask=kmask,
            l4_cosq=cq_full[:, sl].astype(BF), l4_sinq=sq_full[:, sl].astype(BF),
            l4_cosk=ck_full[:, sl].astype(BF), l4_sink=sk_full[:, sl].astype(BF),
        )
        in_maps.append(m)
    res = run_bass_kernel_spmd(nc, in_maps, core_ids=list(range(NCORE)))
    x8 = np.zeros((B, L, D), np.float32)
    for c in range(NCORE):
        b, hf = c // 2, c % 2
        sl = slice(hf * TC, hf * TC + TC)
        x8[b, sl] = unswz_act(res.results[c]["x8"], D, TC).T
    return x8


# version 05
# speedup vs baseline: 1.0483x; 1.0483x over previous
import math
from contextlib import ExitStack

import numpy as np
import ml_dtypes

import concourse.bass as bass
import concourse.mybir as mybir
from concourse.bass_utils import run_bass_kernel_spmd

F32 = mybir.dt.float32
BF16 = mybir.dt.bfloat16
AF = mybir.ActivationFunctionType
ALU = mybir.AluOpType
AX = mybir.AxisListType
BF = ml_dtypes.bfloat16

NCORE = 8
D = 2048
NCH = D // 128
B = 4
L = 4096
TC = 2048
TW = 512
LN_EPS = 1e-5
NEG_BIG = -1.0e30



class Tick:
    __slots__ = ("q", "v", "hz")

    def __init__(self, q, v, hz=False):
        self.q = q
        self.v = v
        self.hz = hz


class CCTick:
    __slots__ = ("semh", "name")

    def __init__(self, semh, name):
        self.semh = semh
        self.name = name


class EQ:

    def __init__(self, name):
        self.name = name
        self.ops = []
        self.n = 0
        self.semh = None
        self.waits = {}

    def run(self, fn, step=1, hazard=False):
        self.n += step
        me = self

        def wrapped(eng, _fn=fn, _s=step):
            _fn(eng).then_inc(me.semh, _s)

        self.ops.append(wrapped)
        return Tick(self, self.n, hz=hazard)

    def runq(self, fn):
        self.ops.append(lambda eng, _fn=fn: _fn(eng))
        return None

    def wait(self, tick, allow_self=False):
        if tick is None:
            return
        if isinstance(tick, CCTick):
            if self.waits.get(tick.name, -1) >= 1:
                return
            self.waits[tick.name] = 1
            self.ops.append(lambda eng, s=tick.semh: eng.wait_ge(s, 1))
            return
        if tick.q is self and self.name == "T":
            return
        key = "SELF" if tick.q is self else tick.q.name
        if self.waits.get(key, -1) >= tick.v:
            return
        self.waits[key] = tick.v
        self.ops.append(lambda eng, q=tick.q, v=tick.v: eng.wait_ge(q.semh, v))

    def wait_many(self, ticks):
        best = {}
        for t in ticks:
            if t is None:
                continue
            if isinstance(t, CCTick):
                self.wait(t)
                continue
            key = "SELF" if t.q is self else t.q.name
            if key not in best or best[key].v < t.v:
                best[key] = t
        for t in best.values():
            self.wait(t)


class Buf:
    __slots__ = ("ap", "wtick", "rticks", "name")

    def __init__(self, ap, name=""):
        self.ap = ap
        self.name = name
        self.wtick = None
        self.rticks = []


class Prog:
    def __init__(self):
        self.nc = bass.Bass("TRN2", target_bir_lowering=False)
        self.TQ = EQ("T")
        self.VQ = EQ("V")
        self.SQ = EQ("S")
        self.PQ = EQ("P")
        self.DQ = EQ("D")
        self.queues = [self.TQ, self.VQ, self.SQ, self.PQ, self.DQ]
        self.ctx = ExitStack()
        self._cur = self.ctx
        self.pfx = ""

    def arena(self):
        from contextlib import contextmanager

        @contextmanager
        def _a():
            st = ExitStack()
            old = self._cur
            self._cur = st
            self._ccs = {}
            try:
                yield
            finally:
                st.close()
                self._cur = old
                self._ccs = {}
                self.barrier()
        return _a()

    def barrier(self):
        counts = [(q, q.n) for q in self.queues]
        for q in self.queues:
            for (o, v) in counts:
                if o is not q and v > 0:
                    q.wait(Tick(o, v))

    def din(self, name, shape, dtype=F32):
        return self.nc.dram_tensor(name, list(shape), dtype, kind="ExternalInput").ap()

    def dout(self, name, shape, dtype=F32):
        return self.nc.dram_tensor(name, list(shape), dtype, kind="ExternalOutput").ap()

    def dtmp(self, name, shape, dtype=F32):
        return self.nc.dram_tensor(self.pfx + name, list(shape), dtype).ap()

    def sb(self, name, shape, dtype=F32):
        name = self.pfx + name
        h = self._cur.enter_context(self.nc.sbuf_tensor(name, list(shape), dtype))
        return Buf(h.ap(), name)

    def constcol(self, value, name=None):
        name = name or f"cc_{len(getattr(self, '_ccs', {}))}"
        if not hasattr(self, "_ccs"):
            self._ccs = {}
        if value in self._ccs:
            return self._ccs[value]
        b = self.sb(name, [128, 1], F32)
        self.write(self.VQ, b, [], lambda e, o=b.ap: e.memset(o, value))
        self._ccs[value] = b
        return b

    def ps(self, name, shape, dtype=F32):
        name = self.pfx + name
        h = self._cur.enter_context(self.nc.psum_tensor(name, list(shape), dtype))
        return Buf(h.ap(), name)

    def read(self, q, bufs, fn, step=1, hazard=False):
        if not isinstance(bufs, (list, tuple)):
            bufs = [bufs]
        for b in bufs:
            if isinstance(b, Buf):
                q.wait(b.wtick)
        t = q.run(fn, step=step, hazard=hazard)
        for b in bufs:
            if isinstance(b, Buf):
                b.rticks.append(t)
        return t

    def write(self, q, dst, srcs, fn, step=1, fence=False, hazard=False):
        if not isinstance(srcs, (list, tuple)):
            srcs = [srcs]
        for s in srcs:
            if isinstance(s, Buf):
                q.wait(s.wtick, allow_self=fence)
        if dst is not None:
            q.wait_many(dst.rticks)
            if dst.wtick is not None:
                q.wait(dst.wtick)
        t = q.run(fn, step=step, hazard=hazard)
        if dst is not None:
            dst.rticks = []
            dst.wtick = t
        for s in srcs:
            if isinstance(s, Buf):
                s.rticks.append(t)
        return t

    def rw(self, q, dst, srcs, fn, step=1, hazard=False):
        if not isinstance(srcs, (list, tuple)):
            srcs = [srcs]
        return self.write(q, dst, list(srcs) + [dst], fn, step=step, hazard=hazard)

    def assemble(self):
        nc = self.nc
        for q in self.queues:
            q.semh = self.ctx.enter_context(nc.semaphore("sem_" + q.name))
        for q in (self.PQ, self.DQ):
            if q.n > 0:
                q.ops.append(lambda eng, s=q, v=q.n: eng.wait_ge(s.semh, v))

        with nc.Block() as block:
            @block.tensor
            def _(eng):
                for op in self.TQ.ops:
                    op(eng)

            @block.vector
            def _(eng):
                for op in self.VQ.ops:
                    op(eng)

            @block.scalar
            def _(eng):
                for op in self.SQ.ops:
                    op(eng)

            @block.gpsimd
            def _(eng):
                for op in self.PQ.ops:
                    op(eng)

            @block.sync
            def _(eng):
                for op in self.DQ.ops:
                    op(eng)

        self.ctx.close()
        return nc



class WStream:

    def __init__(self, P, name, W_dram, n_kc, mcw=128, nbuf=2, cast=True):
        self.P = P
        self.W = W_dram
        self.n_kc = n_kc
        self.mcw = mcw
        self.cast = cast
        self.bufs = [P.sb(f"{name}_wb{i}", [128, n_kc * mcw], BF16) for i in range(nbuf)]
        self.nbuf = nbuf
        self.loaded = {}
        self.next_slot = 0

    def prefetch(self, mc):
        if mc in self.loaded:
            return
        slot = self.next_slot
        self.next_slot = (self.next_slot + 1) % self.nbuf
        for k in [k for k, v in self.loaded.items() if v == slot]:
            del self.loaded[k]
        buf = self.bufs[slot]
        src = self.W[mc, :, :]
        q = self.P.PQ if self.cast else self.P.DQ
        self.P.write(q, buf, [],
                     lambda e, d=buf.ap, s=src: e.dma_start(out=d, in_=s), step=16)
        self.loaded[mc] = slot

    def chunk(self, mc, kc):
        self.prefetch(mc)
        buf = self.bufs[self.loaded[mc]]
        return buf, buf.ap[:, kc * self.mcw:(kc + 1) * self.mcw]


class WResident:

    def __init__(self, P, name, W_dram, n_kc, n_mc, mcw=128):
        self.n_kc = n_kc
        self.mcw = mcw
        self.n_mc = n_mc
        self.buf = P.sb(name + "_w", [128, n_mc * n_kc * mcw], BF16)
        cw = n_kc * mcw
        for mc in range(n_mc):
            P.write(P.PQ, self.buf, [],
                    lambda e, d=self.buf.ap[:, mc * cw:(mc + 1) * cw],
                    s=W_dram[mc, :, :]:
                    e.dma_start(out=d, in_=s), step=16)

    def prefetch(self, mc):
        pass

    def chunk(self, mc, kc):
        o = (mc * self.n_kc + kc) * self.mcw
        return self.buf, self.buf.ap[:, o:o + self.mcw]


def emit_mm(P, psum, psum_slice, pieces, start=True, stop=True):
    TQ = P.TQ
    for (lb, la, rb, ra) in pieces:
        TQ.wait(lb.wtick)
        TQ.wait(rb.wtick)
    TQ.wait_many(psum.rticks)
    if psum.wtick is not None and start:
        pass
    n = len(pieces)
    out_ap = psum.ap[psum_slice] if psum_slice is not None else psum.ap
    t = None
    for i, (lb, la, rb, ra) in enumerate(pieces):
        s = start and i == 0
        p = stop and i == n - 1
        if i == n - 1:
            t = TQ.run(lambda e, o=out_ap, w=la, r=ra, s_=s, p_=p:
                       e.matmul(o, w, r, start=s_, stop=p_))
        else:
            TQ.runq(lambda e, o=out_ap, w=la, r=ra, s_=s, p_=p:
                    e.matmul(o, w, r, start=s_, stop=p_))
    if stop:
        psum.rticks = []
        psum.wtick = t
    else:
        psum.wtick = t
    for (lb, la, rb, ra) in pieces:
        lb.rticks.append(t)
        rb.rticks.append(t)
    return t



class LNKit:
    def __init__(self, P, tag=""):
        self.P = P
        _xb = P.sb(f"ln{tag}_xb", [128, TW], BF16)
        _x2b = P.sb(f"ln{tag}_x2b", [128, TW], BF16)
        self.xb = [_xb, _xb]
        self.x2b = [_x2b, _x2b]
        self.m = P.sb(f"ln{tag}_m", [1, TW])
        self.v = P.sb(f"ln{tag}_v", [1, TW])
        self.r1 = P.sb(f"ln{tag}_r1", [1, TW])
        self.rstd = P.sb(f"ln{tag}_rstd", [1, TW])
        self.mr = P.sb(f"ln{tag}_mr", [1, TW])
        self.rstdB = P.sb(f"ln{tag}_rstdB", [128, TW])
        self.mrB = P.sb(f"ln{tag}_mrB", [128, TW])
        self.t1 = P.sb(f"ln{tag}_t1", [128, TW])
        self.ones_bf = P.sb(f"ln{tag}_onesb", [128, 1], BF16)
        self.ones_f = P.sb(f"ln{tag}_onesf", [1, 128])

    def load_consts(self, ones_bf_dram, ones_f_dram):
        P = self.P
        P.write(P.DQ, self.ones_bf, [],
                lambda e, d=self.ones_bf.ap, s=ones_bf_dram: e.dma_start(out=d, in_=s), step=16)
        P.write(P.DQ, self.ones_f, [],
                lambda e, d=self.ones_f.ap, s=ones_f_dram: e.dma_start(out=d, in_=s), step=16)


def emit_ln(P, kit, x_list, ps_a, ps_b, out_fn):
    for c, (xb_, xa) in enumerate(x_list):
        sc = kit.xb[c % 2]
        sc2 = kit.x2b[c % 2]
        P.write(P.VQ, sc, [xb_], lambda e, o=sc.ap, i=xa: e.tensor_copy(out=o, in_=i))
        P.write(P.SQ, sc2, [xb_], lambda e, o=sc2.ap, i=xa:
                e.activation(out=o, in_=i, func=AF.Square))
        first, last = c == 0, c == NCH - 1
        P.TQ.wait(sc.wtick)
        P.TQ.wait_many(ps_a.rticks)
        t = P.TQ.run(lambda e, o=ps_a.ap[0:1, :], w=kit.ones_bf.ap, r=sc.ap, s=first, p=last:
                     e.matmul(o, w, r, start=s, stop=p))
        sc.rticks.append(t)
        if last:
            ps_a.rticks = []
            ps_a.wtick = t
        P.TQ.wait(sc2.wtick)
        P.TQ.wait_many(ps_b.rticks)
        t2 = P.TQ.run(lambda e, o=ps_b.ap[0:1, :], w=kit.ones_bf.ap, r=sc2.ap, s=first, p=last:
                      e.matmul(o, w, r, start=s, stop=p))
        sc2.rticks.append(t2)
        if last:
            ps_b.rticks = []
            ps_b.wtick = t2
    P.write(P.VQ, kit.m, [ps_a], lambda e, o=kit.m.ap, i=ps_a.ap[0:1, :]:
            e.tensor_scalar_mul(out=o, in0=i, scalar1=1.0 / D))
    P.write(P.VQ, kit.v, [ps_b], lambda e, o=kit.v.ap, i=ps_b.ap[0:1, :]:
            e.tensor_scalar_mul(out=o, in0=i, scalar1=1.0 / D))
    P.write(P.VQ, kit.r1, [kit.m], lambda e, o=kit.r1.ap, a=kit.m.ap:
            e.tensor_mul(out=o, in0=a, in1=a))
    P.rw(P.VQ, kit.v, [kit.r1], lambda e, o=kit.v.ap, a=kit.v.ap, b=kit.r1.ap:
         e.scalar_tensor_tensor(out=o, in0=a, scalar=LN_EPS, in1=b,
                                op0=ALU.add, op1=ALU.subtract))
    P.write(P.SQ, kit.r1, [kit.v], lambda e, o=kit.r1.ap, i=kit.v.ap:
            e.activation(out=o, in_=i, func=AF.Sqrt))
    P.write(P.VQ, kit.rstd, [kit.r1], lambda e, o=kit.rstd.ap, i=kit.r1.ap:
            e.reciprocal(out=o, in_=i))
    P.write(P.VQ, kit.mr, [kit.m, kit.rstd], lambda e, o=kit.mr.ap, a=kit.m.ap, b=kit.rstd.ap:
            e.tensor_mul(out=o, in0=a, in1=b))
    P.TQ.wait(kit.rstd.wtick)
    P.TQ.wait_many(ps_a.rticks)
    t = P.TQ.run(lambda e, o=ps_a.ap, w=kit.ones_f.ap, r=kit.rstd.ap:
                 e.matmul(o, w, r, start=True, stop=True))
    kit.rstd.rticks.append(t)
    ps_a.rticks = []
    ps_a.wtick = t
    P.TQ.wait(kit.mr.wtick)
    P.TQ.wait_many(ps_b.rticks)
    t = P.TQ.run(lambda e, o=ps_b.ap, w=kit.ones_f.ap, r=kit.mr.ap:
                 e.matmul(o, w, r, start=True, stop=True))
    kit.mr.rticks.append(t)
    ps_b.rticks = []
    ps_b.wtick = t
    P.write(P.VQ, kit.rstdB, [ps_a], lambda e, o=kit.rstdB.ap, i=ps_a.ap:
            e.tensor_copy(out=o, in_=i))
    P.write(P.VQ, kit.mrB, [ps_b], lambda e, o=kit.mrB.ap, i=ps_b.ap:
            e.tensor_copy(out=o, in_=i))
    for c, (xb_, xa) in enumerate(x_list):
        P.write(P.PQ, kit.t1, [xb_, kit.rstdB],
                lambda e, o=kit.t1.ap, a=xa, b=kit.rstdB.ap:
                e.tensor_mul(out=o, in0=a, in1=b), step=1)
        out_fn(c, kit.t1, kit.mrB)



def f32c(a):
    return np.ascontiguousarray(a, dtype=np.float32)


def to_cols(vec):
    n = vec.shape[0] // 128
    return np.ascontiguousarray(np.asarray(vec, dtype=np.float32).reshape(n, 128).T)


def swz_w(WT, mcw=128):
    WT = np.asarray(WT, dtype=np.float32)
    Kd, M = WT.shape
    nkc, nmc = Kd // 128, M // mcw
    return np.ascontiguousarray(
        WT.reshape(nkc, 128, nmc, mcw).transpose(2, 1, 0, 3).reshape(nmc, 128, nkc * mcw))


def swz_act(xT, tw=TW):
    xT = np.ascontiguousarray(xT)
    Dr, T = xT.shape
    nch, nT = Dr // 128, T // tw
    return np.ascontiguousarray(
        xT.reshape(nch, 128, nT, tw).transpose(2, 1, 0, 3).reshape(nT, 128, nch * tw))


def unswz_act(a, Dr, T, tw=TW):
    nch, nT = Dr // 128, T // tw
    return np.ascontiguousarray(
        np.asarray(a).reshape(nT, 128, nch, tw).transpose(2, 1, 0, 3).reshape(Dr, T))


_host_consts = None


def host_consts():
    global _host_consts
    if _host_consts is not None:
        return _host_consts
    ones_bf = np.ones((128, 1), BF)
    ones_f = np.ones((1, 128), np.float32)
    masks = np.zeros((4, 128, 512), np.float32)
    for v in range(4):
        qpos = 128 * v + np.arange(128)[:, None]
        masks[v] = np.where(qpos >= np.arange(512)[None, :], 0.0, NEG_BIG)
    masks = masks.astype(BF)
    _host_consts = dict(ones_bf=ones_bf, ones_f=ones_f, masks=masks)
    return _host_consts



def build_launch1():
    P = Prog()
    nT = L // TW
    EC = 8
    xT = P.din("xT", [nT, 128, NCH * TW])
    WqT = P.din("WqT", [EC, 128, D])
    WkT = P.din("WkT", [EC, 128, D])
    WvT = P.din("WvT", [EC, 128, D])
    bq = P.din("bq", [128, EC])
    bk = P.din("bk", [128, EC])
    bv = P.din("bv", [128, EC])
    ones_bf_d = P.din("ones_bf", [128, 1], BF16)
    ones_f_d = P.din("ones_f", [1, 128])
    ctx_o = P.dout("ctx", [nT, 128, EC * TW], BF16)

    SCALE = 128.0 ** -0.5

    kit = LNKit(P)
    kit.load_consts(ones_bf_d, ones_f_d)

    wq = WResident(P, "wq", WqT, NCH, EC)
    wk = WResident(P, "wk", WkT, NCH, EC)
    wv = WResident(P, "wv", WvT, NCH, EC)

    bqs = P.sb("bqs", [128, EC])
    bks = P.sb("bks", [128, EC])
    bvs = P.sb("bvs", [128, EC])
    for buf, src in ((bqs, bq), (bks, bk), (bvs, bv)):
        P.write(P.DQ, buf, [], lambda e, d=buf.ap, s=src: e.dma_start(out=d, in_=s), step=16)

    xt = P.sb("xt", [128, NCH * TW])
    xln = P.sb("xln", [128, NCH * TW], BF16)
    s_qf = P.sb("s_qf", [128, TW])
    s_phq = P.sb("s_phq", [128, TW])
    s_mn = P.sb("s_mn", [128, TW])
    s_ex = P.sb("s_ex", [128, TW])
    s_phk = P.sb("s_phk", [128, TW])
    s_vf = P.sb("s_vf", [128, TW])
    s_kvs = P.sb("s_kvs", [128, TW])
    zeros = P.sb("zeros", [128, TW])
    carry = P.sb("carry", [128, EC])
    ctx_bf = P.sb("ctx_bf", [128, EC * TW], BF16)

    ps_a = P.ps("ps_a", [128, TW])
    ps_b = P.ps("ps_b", [128, TW])
    ps_q = P.ps("ps_q", [128, TW])
    ps_k = P.ps("ps_k", [128, TW])
    ps_v = P.ps("ps_v", [128, TW])

    P.write(P.VQ, zeros, [], lambda e, o=zeros.ap: e.memset(o, 0.0))
    P.write(P.VQ, carry, [], lambda e, o=carry.ap: e.memset(o, 0.0))

    def phi(P, src_psum, bias_buf, bias_col, out_buf):
        P.write(P.VQ, s_qf, [src_psum, bias_buf],
                lambda e, o=s_qf.ap, i=src_psum.ap, b=bias_col:
                e.tensor_scalar_add(out=o, in0=i, scalar1=b))
        P.write(P.PQ, s_mn, [s_qf],
                lambda e, o=s_mn.ap, i=s_qf.ap: e.tensor_scalar_min(out=o, in0=i, scalar1=0.0),
                step=1)
        P.write(P.SQ, s_ex, [s_mn],
                lambda e, o=s_ex.ap, i=s_mn.ap: e.activation(out=o, in_=i, func=AF.Exp))
        P.write(P.VQ, out_buf, [s_qf, s_ex],
                lambda e, o=out_buf.ap, a=s_qf.ap, x=s_ex.ap:
                e.scalar_tensor_tensor(out=o, in0=a, scalar=0.0, in1=x,
                                       op0=ALU.max, op1=ALU.add))

    for tt in range(nT):
        P.write(P.DQ, xt, [],
                lambda e, d=xt.ap, s=xT[tt, :, :]:
                e.dma_start(out=d, in_=s), step=16)
        x_list = [(xt, xt.ap[:, c * TW:(c + 1) * TW]) for c in range(NCH)]

        def ln_out(c, t1, mrB):
            P.write(P.VQ, xln, [t1, mrB],
                    lambda e, o=xln.ap[:, c * TW:(c + 1) * TW], a=t1.ap, b=mrB.ap:
                    e.tensor_sub(out=o, in0=a, in1=b))

        emit_ln(P, kit, x_list, ps_a, ps_b, ln_out)
        xln_chunks = [(xln, xln.ap[:, c * TW:(c + 1) * TW]) for c in range(NCH)]

        for ec in range(EC):
            for (wres, psm) in ((wq, ps_q), (wk, ps_k), (wv, ps_v)):
                pieces = []
                for kc in range(NCH):
                    lb, la = wres.chunk(ec, kc)
                    rb, ra = xln_chunks[kc]
                    pieces.append((lb, la, rb, ra))
                emit_mm(P, psm, None, pieces)
            phi(P, ps_q, bqs, bqs.ap[:, ec:ec + 1], s_phq)
            phi(P, ps_k, bks, bks.ap[:, ec:ec + 1], s_phk)
            P.write(P.VQ, s_vf, [ps_v, bvs],
                    lambda e, o=s_vf.ap, i=ps_v.ap, b=bvs.ap[:, ec:ec + 1]:
                    e.tensor_scalar_add(out=o, in0=i, scalar1=b))
            P.rw(P.VQ, s_phk, [s_vf],
                 lambda e, o=s_phk.ap, a=s_phk.ap, b=s_vf.ap:
                 e.tensor_mul(out=o, in0=a, in1=b))
            P.write(P.VQ, s_kvs, [s_phk, zeros, carry],
                    lambda e, o=s_kvs.ap, d0=s_phk.ap, d1=zeros.ap,
                    ini=carry.ap[:, ec:ec + 1]:
                    e.tensor_tensor_scan(out=o, data0=d0, data1=d1, initial=ini,
                                         op0=ALU.add, op1=ALU.bypass), hazard=True)
            P.rw(P.PQ, carry, [s_kvs],
                 lambda e, o=carry.ap[:, ec:ec + 1], i=s_kvs.ap[:, TW - 1:TW]:
                 e.tensor_copy(out=o, in_=i))
            P.write(P.VQ, ctx_bf, [s_phq, s_kvs],
                    lambda e, o=ctx_bf.ap[:, ec * TW:(ec + 1) * TW], a=s_phq.ap,
                    b=s_kvs.ap:
                    e.scalar_tensor_tensor(out=o, in0=a, scalar=SCALE, in1=b,
                                           op0=ALU.mult, op1=ALU.mult), fence=True)
        P.read(P.PQ, ctx_bf,
               lambda e, d=ctx_o[tt, :, :], s=ctx_bf.ap:
               e.dma_start(out=d, in_=s), step=16)

    return P.assemble()


def run_launch1(x, Wq, bq, Wk, bk, Wv, bv):
    hc = host_consts()
    nc = build_launch1()
    in_maps = []
    for c in range(NCORE):
        b, hh = c // 2, c % 2
        sl = slice(hh * 1024, hh * 1024 + 1024)
        in_maps.append(dict(
            xT=swz_act(f32c(x[b].T)),
            WqT=swz_w(Wq.T[:, sl]), WkT=swz_w(Wk.T[:, sl]), WvT=swz_w(Wv.T[:, sl]),
            bq=to_cols(np.asarray(bq)[sl]), bk=to_cols(np.asarray(bk)[sl]),
            bv=to_cols(np.asarray(bv)[sl]),
            ones_bf=hc["ones_bf"], ones_f=hc["ones_f"],
        ))
    res = run_bass_kernel_spmd(nc, in_maps, core_ids=list(range(NCORE)))
    ctx = np.zeros((B, L, D), np.float32)
    for c in range(NCORE):
        b, hh = c // 2, c % 2
        ctxT = unswz_act(res.results[c]["ctx"].astype(np.float32), 1024, L)
        ctx[b, :, hh * 1024:hh * 1024 + 1024] = ctxT.T
    return ctx



def build_launch2(P=None, io=None):
    standalone = P is None
    nTT = TC // TW
    if standalone:
        P = Prog()
        io = dict(
            xT=P.din("xT", [nTT, 128, NCH * TW]),
            ctx=P.din("ctx", [nTT, 128, NCH * TW], BF16),
            WoT=P.din("WoT", [NCH, 128, D]),
            bo=P.din("bo", [128, NCH]),
            wT=P.din("wT", [NCH, 128, D]),
            rrT=P.din("rrT", [NCH, 128, D]),
            WqT=P.din("WqT", [NCH, 128, D]),
            WkT=P.din("WkT", [NCH, 128, D]),
            WvT=P.din("WvT", [NCH, 128, D]),
            bq=P.din("bq", [128, NCH]),
            bk=P.din("bk", [128, NCH]),
            bv=P.din("bv", [128, NCH]),
            ones_bf=P.din("ones_bf", [128, 1], BF16),
            ones_f=P.din("ones_f", [1, 128]),
            x2=P.dout("x2", [nTT, 128, NCH * TW]),
            q=P.dout("q", [NCH, 128, TC], BF16),
            k=P.dout("k", [NCH, 128, TC], BF16),
            v=P.dout("v", [NCH, 128, TC], BF16),
            wr=P.dout("wr", [NCH, 128, D], BF16),
        )
    xT_i = io["xT"]; ctx_i = io["ctx"]; WoT_i = io["WoT"]; bo_i = io["bo"]
    wT_i = io["wT"]; rrT_i = io["rrT"]
    WqT_i = io["WqT"]; WkT_i = io["WkT"]; WvT_i = io["WvT"]
    bq_i = io["bq"]; bk_i = io["bk"]; bv_i = io["bv"]
    ones_bf_d = io["ones_bf"]; ones_f_d = io["ones_f"]
    x2_o = io["x2"]; q_o = io["q"]; k_o = io["k"]; v_o = io["v"]; wr_o = io["wr"]

    x_scr = P.dtmp("x_scr", [nTT, 128, NCH * TW])
    y_scr = P.dtmp("y_scr", [nTT, 128, NCH * TW], BF16)

    SCALE = 128.0 ** -0.5
    TH_X, SC_X = 0.33, 0.5
    TH_W, SC_W = 0.2, 0.6

    kit = LNKit(P)
    kit.load_consts(ones_bf_d, ones_f_d)

    bo_s = P.sb("bo_s", [128, NCH])
    bq_s = P.sb("bq_s", [128, NCH])
    bk_s = P.sb("bk_s", [128, NCH])
    bv_s = P.sb("bv_s", [128, NCH])
    for buf, src in ((bo_s, bo_i), (bq_s, bq_i), (bk_s, bk_i), (bv_s, bv_i)):
        P.write(P.DQ, buf, [], lambda e, d=buf.ap, s=src: e.dma_start(out=d, in_=s), step=16)

    xt = P.sb("xt", [128, NCH * TW])
    ct = P.sb("ct", [128, NCH * TW], BF16)
    xbig = P.sb("xbig", [128, NCH * TC], BF16)
    xlnf = P.sb("xlnf", [128, TW])
    tg = P.sb("tg", [128, TW])
    tg2 = P.sb("tg2", [128, TW])
    sevac = [P.sb(f"sevac{i}", [128, TW], BF16) for i in range(2)]
    wrcol = [P.sb(f"wrcol{i}", [128, NCH * 128], BF16) for i in range(2)]
    wcol_f = P.sb("wcol_f", [128, NCH * 128])
    rcol_f = P.sb("rcol_f", [128, NCH * 128])
    tw1 = P.sb("tw1", [128, NCH * 128])

    ps_a = P.ps("ps_a", [128, TW])
    ps_b = P.ps("ps_b", [128, TW])
    ps_m = [P.ps(f"ps_m{i}", [128, TW]) for i in range(2)]

    ws_wo = WStream(P, "wo", WoT_i, NCH)
    ws_q = WStream(P, "wsq", WqT_i, NCH)
    ws_k = WStream(P, "wsk", WkT_i, NCH)
    ws_v = WStream(P, "wsv", WvT_i, NCH)

    def xbig_sl(c, tt):
        return xbig.ap[:, (c * nTT + tt) * TW:(c * nTT + tt + 1) * TW]

    for tt in range(nTT):
        P.write(P.DQ, xt, [], lambda e, d=xt.ap, s=xT_i[tt, :, :]:
                e.dma_start(out=d, in_=s), step=16)
        P.write(P.DQ, ct, [], lambda e, d=ct.ap, s=ctx_i[tt, :, :]:
                e.dma_start(out=d, in_=s), step=16)
        ctx_chunks = [(ct, ct.ap[:, c * TW:(c + 1) * TW]) for c in range(NCH)]
        for mc in range(NCH):
            if mc + 1 < NCH:
                ws_wo.prefetch(mc + 1)
            elif tt + 1 < nTT:
                ws_wo.prefetch(0)
            psm = ps_m[mc % 2]
            pieces = []
            for kc in range(NCH):
                lb, la = ws_wo.chunk(mc, kc)
                rb, ra = ctx_chunks[kc]
                pieces.append((lb, la, rb, ra))
            emit_mm(P, psm, None, pieces)
            P.rw(P.VQ, xt, [psm, bo_s],
                 lambda e, o=xt.ap[:, mc * TW:(mc + 1) * TW], i=psm.ap,
                 b=bo_s.ap[:, mc:mc + 1],
                 xx=xt.ap[:, mc * TW:(mc + 1) * TW]:
                 e.scalar_tensor_tensor(out=o, in0=i, scalar=b, in1=xx,
                                        op0=ALU.add, op1=ALU.add))
        x_list = [(xt, xt.ap[:, c * TW:(c + 1) * TW]) for c in range(NCH)]

        def tern_out(c, t1, mrB, tt=tt):
            P.write(P.VQ, xlnf, [t1, mrB],
                    lambda e, o=xlnf.ap, a=t1.ap, b=mrB.ap:
                    e.tensor_sub(out=o, in0=a, in1=b))
            P.write(P.PQ, tg, [xlnf],
                    lambda e, o=tg.ap, i=xlnf.ap:
                    e.tensor_scalar(out=o, in0=i, scalar1=TH_X, scalar2=SC_X,
                                    op0=ALU.is_gt, op1=ALU.mult), step=1)
            P.write(P.VQ, tg2, [xlnf],
                    lambda e, o=tg2.ap, i=xlnf.ap:
                    e.tensor_scalar(out=o, in0=i, scalar1=-TH_X, scalar2=-SC_X,
                                    op0=ALU.is_lt, op1=ALU.mult))
            P.write(P.VQ, xbig, [tg, tg2],
                    lambda e, o=xbig_sl(c, tt), a=tg.ap, b=tg2.ap:
                    e.tensor_add(out=o, in0=a, in1=b))

        emit_ln(P, kit, x_list, ps_a, ps_b, tern_out)
        P.read(P.DQ, xt, lambda e, d=x_scr[tt, :, :], s=xt.ap:
               e.dma_start(out=d, in_=s), step=16)

    for mc in range(NCH):
        wr = wrcol[mc % 2]
        P.write(P.DQ, wcol_f, [], lambda e, d=wcol_f.ap, s=wT_i[mc, :, :]:
                e.dma_start(out=d, in_=s), step=16)
        P.write(P.DQ, rcol_f, [], lambda e, d=rcol_f.ap, s=rrT_i[mc, :, :]:
                e.dma_start(out=d, in_=s), step=16)
        P.write(P.PQ, tw1, [wcol_f],
                lambda e, o=tw1.ap, i=wcol_f.ap:
                e.tensor_scalar(out=o, in0=i, scalar1=TH_W, scalar2=SC_W,
                                op0=ALU.is_gt, op1=ALU.mult), step=1)
        P.rw(P.VQ, tw1, [rcol_f],
             lambda e, o=tw1.ap, a=tw1.ap, b=rcol_f.ap:
             e.tensor_add(out=o, in0=a, in1=b))
        P.write(P.VQ, rcol_f, [wcol_f],
                lambda e, o=rcol_f.ap, i=wcol_f.ap:
                e.tensor_scalar(out=o, in0=i, scalar1=-TH_W, scalar2=-SC_W,
                                op0=ALU.is_lt, op1=ALU.mult))
        P.write(P.VQ, wr, [tw1, rcol_f],
                lambda e, o=wr.ap, a=tw1.ap, b=rcol_f.ap:
                e.tensor_add(out=o, in0=a, in1=b))
        P.read(P.PQ, wr, lambda e, d=wr_o[mc, :, :], s=wr.ap:
               e.dma_start(out=d, in_=s), step=16)
        for tt in range(nTT):
            psm = ps_m[tt % 2]
            pieces = []
            for kc in range(NCH):
                pieces.append((wr, wr.ap[:, kc * 128:(kc + 1) * 128],
                               xbig, xbig_sl(kc, tt)))
            emit_mm(P, psm, None, pieces)
            sv = sevac[tt % 2]
            P.write(P.VQ, sv, [psm],
                    lambda e, o=sv.ap, i=psm.ap: e.tensor_copy(out=o, in_=i))
            P.read(P.DQ, sv,
                   lambda e, d=y_scr[tt, :, mc * TW:(mc + 1) * TW], s=sv.ap:
                   e.dma_start(out=d, in_=s), step=16)

    for tt in range(nTT):
        P.write(P.DQ, xt, [], lambda e, d=xt.ap, s=x_scr[tt, :, :]:
                e.dma_start(out=d, in_=s), step=16)
        P.write(P.DQ, ct, [], lambda e, d=ct.ap, s=y_scr[tt, :, :]:
                e.dma_start(out=d, in_=s), step=16)
        for c in range(NCH):
            P.rw(P.VQ, xt, [ct],
                 lambda e, o=xt.ap[:, c * TW:(c + 1) * TW],
                 b=ct.ap[:, c * TW:(c + 1) * TW]:
                 e.tensor_tensor(out=o, in0=o, in1=b, op=ALU.add))
        x_list = [(xt, xt.ap[:, c * TW:(c + 1) * TW]) for c in range(NCH)]

        def ln_out(c, t1, mrB, tt=tt):
            P.write(P.VQ, xbig, [t1, mrB],
                    lambda e, o=xbig_sl(c, tt), a=t1.ap, b=mrB.ap:
                    e.tensor_sub(out=o, in0=a, in1=b))

        emit_ln(P, kit, x_list, ps_a, ps_b, ln_out)
        P.read(P.DQ, xt, lambda e, d=x2_o[tt, :, :], s=xt.ap:
               e.dma_start(out=d, in_=s), step=16)

    for (ws, out_d, bias_s, qscale) in ((ws_q, q_o, bq_s, SCALE),
                                        (ws_k, k_o, bk_s, None),
                                        (ws_v, v_o, bv_s, None)):
        for mc in range(NCH):
            if mc + 1 < NCH:
                ws.prefetch(mc + 1)
            for tt in range(nTT):
                psm = ps_m[tt % 2]
                pieces = []
                for kc in range(NCH):
                    lb, la = ws.chunk(mc, kc)
                    pieces.append((lb, la, xbig, xbig_sl(kc, tt)))
                emit_mm(P, psm, None, pieces)
                dst = ct.ap[:, tt * TW:(tt + 1) * TW]
                if qscale is not None:
                    scb = P.constcol(qscale)
                    P.write(P.VQ, ct, [psm, bias_s, scb],
                            lambda e, o=dst, i=psm.ap, b=bias_s.ap[:, mc:mc + 1],
                            s2=scb.ap[:, 0:1]:
                            e.tensor_scalar(out=o, in0=i, scalar1=b, scalar2=s2,
                                            op0=ALU.add, op1=ALU.mult))
                else:
                    P.write(P.VQ, ct, [psm, bias_s],
                            lambda e, o=dst, i=psm.ap, b=bias_s.ap[:, mc:mc + 1]:
                            e.tensor_scalar_add(out=o, in0=i, scalar1=b))
            P.read(P.DQ, ct,
                   lambda e, d=out_d[mc, :, :], s=ct.ap[:, 0:TC]:
                   e.dma_start(out=d, in_=s), step=16)

    if standalone:
        return P.assemble()


def run_launch2(x, ctx, Wo, bo, ffn_w, ffn_rr, w_qkv, b_qkv):
    hc = host_consts()
    nc = build_launch2()
    Wq_b = w_qkv[0:D, :]
    Wk_b = w_qkv[D:2 * D, :]
    Wv_b = w_qkv[2 * D:3 * D, :]
    in_maps = []
    for c in range(NCORE):
        b, hf = c // 2, c % 2
        sl = slice(hf * TC, hf * TC + TC)
        in_maps.append(dict(
            xT=swz_act(f32c(x[b, sl].T)),
            ctx=swz_act(np.ascontiguousarray(ctx[b, sl].T.astype(BF))),
            WoT=swz_w(Wo.T), bo=to_cols(bo),
            wT=swz_w(ffn_w.T), rrT=swz_w(ffn_rr.T),
            WqT=swz_w(Wq_b.T), WkT=swz_w(Wk_b.T), WvT=swz_w(Wv_b.T),
            bq=to_cols(b_qkv[0:D]), bk=to_cols(b_qkv[D:2 * D]),
            bv=to_cols(b_qkv[2 * D:3 * D]),
            ones_bf=hc["ones_bf"], ones_f=hc["ones_f"],
        ))
    res = run_bass_kernel_spmd(nc, in_maps, core_ids=list(range(NCORE)))
    x2 = np.zeros((B, L, D), np.float32)
    q = np.zeros((B, L, D), np.float32)
    k = np.zeros((B, L, D), np.float32)
    v = np.zeros((B, L, D), np.float32)
    wr = None
    for c in range(NCORE):
        b, hf = c // 2, c % 2
        sl = slice(hf * TC, hf * TC + TC)
        r = res.results[c]
        x2[b, sl] = unswz_act(r["x2"], D, TC).T
        for name, arr in (("q", q), ("k", k), ("v", v)):
            arr[b, sl] = r[name].astype(np.float32).reshape(D, TC).T
        if wr is None:
            wr = r["wr"]
    return x2, q, k, v, wr



def emit_group(P, q, dst, srcs, fns):
    for s in srcs:
        if isinstance(s, Buf):
            q.wait(s.wtick)
    if dst is not None:
        q.wait_many(dst.rticks)
        if dst.wtick is not None:
            q.wait(dst.wtick)
    t = None
    for i, fn in enumerate(fns):
        if i == len(fns) - 1:
            t = q.run(fn)
        else:
            q.runq(fn)
    if dst is not None:
        dst.rticks = []
        dst.wtick = t
    for s in srcs:
        if isinstance(s, Buf):
            s.rticks.append(t)
    return t


def build_flash(rope):
    P = Prog()
    NH = 8
    NQT = L // 128
    qn_i = P.din("qn", [NH, 128, L], BF16)
    kn_i = P.din("kn", [NH, 128, L], BF16)
    v_i = P.din("v", [NH, 128, L], BF16)
    if rope:
        qr_i = P.din("qr", [NH, 64, L], BF16)
        kr_i = P.din("kr", [NH, 64, L], BF16)
    masks_i = P.din("masks", [4, 128, TW], BF16)
    ident_i = P.din("ident", [128, 128], BF16)
    O_o = P.dout("O", [NH, 128, L], BF16)


    qn = [P.sb(f"qn{i}", [128, L], BF16) for i in range(2)]
    kn = [P.sb(f"kn{i}", [128, L], BF16) for i in range(2)]
    vv = [P.sb(f"vv{i}", [128, L], BF16) for i in range(2)]
    if rope:
        qr = [P.sb(f"qr{i}", [64, L], BF16) for i in range(2)]
        kr = [P.sb(f"kr{i}", [64, L], BF16) for i in range(2)]
    ost = [P.sb(f"ost{i}", [128, L], BF16) for i in range(2)]
    masks = P.sb("masks_sb", [128, 4 * TW], BF16)
    idt = P.sb("idt", [128, 128], BF16)
    P_sb = P.sb("P_sb", [128, TW], BF16)
    pt_sb = P.sb("pt_sb", [128, TW], BF16)
    O_acc = P.sb("O_acc", [128, 128])
    s_nmt = P.sb("s_nmt", [128, 1])
    s_nmrun = P.sb("s_nmrun", [128, 1])
    s_nmnew = P.sb("s_nmnew", [128, 1])
    s_d = P.sb("s_d", [128, 1])
    s_alpha = P.sb("s_alpha", [128, 1])
    s_rsum = P.sb("s_rsum", [128, 1])
    s_lrun = P.sb("s_lrun", [128, 1])
    s_rcp = P.sb("s_rcp", [128, 1])

    ps_s = [P.ps(f"ps_s{i}", [128, TW]) for i in range(2)]
    ps_t = P.ps("ps_t", [128, TW], BF16)
    ps_pv = P.ps("ps_pv", [128, 128])

    for mv_ in range(4):
        P.write(P.DQ, masks, [],
                lambda e, d=masks.ap[:, mv_ * TW:(mv_ + 1) * TW], s=masks_i[mv_, :, :]:
                e.dma_start(out=d, in_=s), step=16)
    P.write(P.DQ, idt, [], lambda e, d=idt.ap, s=ident_i[:, :]:
            e.dma_start(out=d, in_=s), step=16)

    def load_head(h):
        sl = h % 2
        bufs = [(qn[sl], qn_i), (kn[sl], kn_i), (vv[sl], v_i)]
        if rope:
            bufs += [(qr[sl], qr_i), (kr[sl], kr_i)]
        for buf, src in bufs:
            P.write(P.DQ, buf, [], lambda e, d=buf.ap, s=src[h, :, :]:
                    e.dma_start(out=d, in_=s), step=16)

    load_head(0)
    for h in range(NH):
        sl = h % 2
        if h + 1 < NH:
            load_head(h + 1)
        ostage = ost[sl]
        for qt in range(NQT):
            KM = qt // 4 + 1
            for km in range(KM):
                psm = ps_s[km % 2]
                pieces = [(qn[sl], qn[sl].ap[:, qt * 128:(qt + 1) * 128],
                           kn[sl], kn[sl].ap[:, km * TW:(km + 1) * TW])]
                if rope:
                    pieces.append((qr[sl], qr[sl].ap[:, qt * 128:(qt + 1) * 128],
                                   kr[sl], kr[sl].ap[:, km * TW:(km + 1) * TW]))
                if km == qt // 4:
                    mv = qt % 4
                    pieces.append((idt, idt.ap,
                                   masks, masks.ap[:, mv * TW:(mv + 1) * TW]))
                emit_mm(P, psm, None, pieces)
                P.write(P.VQ, s_nmt, [psm],
                        lambda e, o=s_nmt.ap, i=psm.ap:
                        e.reduce_max(out=o, in_=i, axis=AX.X, negate=True),
                        hazard=True)
                if km == 0:
                    P.write(P.VQ, s_nmrun, [s_nmt],
                            lambda e, o=s_nmrun.ap, i=s_nmt.ap: e.tensor_copy(out=o, in_=i))
                else:
                    P.write(P.VQ, s_nmnew, [s_nmt, s_nmrun],
                            lambda e, o=s_nmnew.ap, a=s_nmrun.ap, b=s_nmt.ap:
                            e.tensor_tensor(out=o, in0=a, in1=b, op=ALU.min))
                    P.write(P.VQ, s_d, [s_nmnew, s_nmrun],
                            lambda e, o=s_d.ap, a=s_nmnew.ap, b=s_nmrun.ap:
                            e.tensor_sub(out=o, in0=a, in1=b))
                    P.write(P.VQ, s_nmrun, [s_nmnew],
                            lambda e, o=s_nmrun.ap, i=s_nmnew.ap: e.tensor_copy(out=o, in_=i))
                    P.write(P.SQ, s_alpha, [s_d],
                            lambda e, o=s_alpha.ap, i=s_d.ap:
                            e.activation(out=o, in_=i, func=AF.Exp))
                P.SQ.wait(psm.wtick)
                P.SQ.wait(s_nmrun.wtick)
                P.SQ.wait_many(P_sb.rticks)
                P.SQ.wait_many(s_rsum.rticks)
                if P_sb.wtick is not None:
                    P.SQ.wait(P_sb.wtick)
                t = P.SQ.run(lambda e, o=P_sb.ap, i=psm.ap, b=s_nmrun.ap[:, 0:1],
                             a=s_rsum.ap[:, 0:1]:
                             e.activation(out=o, in_=i, func=AF.Exp, bias=b,
                                          accum_out=a))
                P_sb.rticks = []
                P_sb.wtick = t
                s_rsum.rticks = []
                s_rsum.wtick = t
                psm.rticks.append(t)
                s_nmrun.rticks.append(t)
                if km == 0:
                    P.write(P.VQ, s_lrun, [s_rsum],
                            lambda e, o=s_lrun.ap, i=s_rsum.ap: e.tensor_copy(out=o, in_=i))
                else:
                    P.rw(P.VQ, s_lrun, [s_rsum, s_alpha],
                         lambda e, o=s_lrun.ap, a=s_lrun.ap, sc=s_alpha.ap[:, 0:1],
                         b=s_rsum.ap:
                         e.scalar_tensor_tensor(out=o, in0=a, scalar=sc, in1=b,
                                                op0=ALU.mult, op1=ALU.add))
                fns = [
                    (lambda e, o=ps_t.ap[:, j * 128:(j + 1) * 128],
                     i=P_sb.ap[:, j * 128:(j + 1) * 128], dd=idt.ap:
                     e.transpose(o, i, dd))
                    for j in range(4)
                ]
                emit_group(P, P.TQ, ps_t, [P_sb, idt], fns)
                P.write(P.VQ, pt_sb, [ps_t],
                        lambda e, o=pt_sb.ap, i=ps_t.ap: e.tensor_copy(out=o, in_=i))
                pieces = [(pt_sb, pt_sb.ap[:, j * 128:(j + 1) * 128],
                           vv[sl], vv[sl].ap[:, (4 * km + j) * 128:(4 * km + j + 1) * 128])
                          for j in range(4)]
                emit_mm(P, ps_pv, None, pieces)
                if km == 0:
                    P.write(P.VQ, O_acc, [ps_pv],
                            lambda e, o=O_acc.ap, i=ps_pv.ap: e.tensor_copy(out=o, in_=i))
                else:
                    P.rw(P.VQ, O_acc, [ps_pv, s_alpha],
                         lambda e, o=O_acc.ap, a=O_acc.ap, sc=s_alpha.ap[:, 0:1],
                         b=ps_pv.ap:
                         e.scalar_tensor_tensor(out=o, in0=a, scalar=sc, in1=b,
                                                op0=ALU.mult, op1=ALU.add))
            P.write(P.VQ, s_rcp, [s_lrun],
                    lambda e, o=s_rcp.ap, i=s_lrun.ap: e.reciprocal(out=o, in_=i),
                    hazard=True)
            P.write(P.VQ, ostage, [O_acc, s_rcp],
                    lambda e, o=ostage.ap[:, qt * 128:(qt + 1) * 128], i=O_acc.ap,
                    sc=s_rcp.ap[:, 0:1]:
                    e.tensor_scalar_mul(out=o, in0=i, scalar1=sc), fence=True)
        P.read(P.PQ, ostage, lambda e, d=O_o[h, :, :], s=ostage.ap:
               e.dma_start(out=d, in_=s), step=16)

    return P.assemble()


def _flash_inmaps(q_heads, k_heads, v_heads, qr_heads=None, kr_heads=None):
    hc = host_consts()
    vsw = np.ascontiguousarray(
        v_heads.reshape(8, 32, 128, 128).transpose(0, 2, 1, 3).reshape(8, 128, L))
    m = dict(qn=np.ascontiguousarray(q_heads), kn=np.ascontiguousarray(k_heads),
             v=vsw, masks=hc["masks"], ident=np.eye(128).astype(BF))
    if qr_heads is not None:
        m["qr"] = np.ascontiguousarray(qr_heads)
        m["kr"] = np.ascontiguousarray(kr_heads)
    return m


def _flash_out(res_c):
    O = res_c["O"].astype(np.float32)
    return O.reshape(8, 128, 32, 128).transpose(2, 1, 0, 3).reshape(L, 1024)


def run_launch3(q, k, v):
    nc = build_flash(rope=False)
    in_maps = []
    for c in range(NCORE):
        b, hh = c // 2, c % 2
        qT = q[b].T.astype(BF)
        kT = k[b].T.astype(BF)
        hsl = slice(hh * 8, hh * 8 + 8)
        qh = qT.reshape(16, 128, L)[hsl]
        kh = kT.reshape(16, 128, L)[hsl]
        vh = np.ascontiguousarray(
            v[b].astype(BF).reshape(L, 16, 128).transpose(1, 0, 2)[hsl])
        in_maps.append(_flash_inmaps(qh, kh, vh))
    res = run_bass_kernel_spmd(nc, in_maps, core_ids=list(range(NCORE)))
    out = np.zeros((B, L, D), np.float32)
    for c in range(NCORE):
        b, hh = c // 2, c % 2
        out[b, :, hh * 1024:hh * 1024 + 1024] = _flash_out(res.results[c])
    return out



def build_launch4(P=None, io=None):
    standalone = P is None
    nTT = TC // TW
    S192 = 192.0 ** -0.5
    if standalone:
        P = Prog()
        io = dict(
            xT=P.din("xT", [nTT, 128, NCH * TW]),
            O=P.din("O", [nTT, 128, NCH * TW], BF16),
            WoT=P.din("WoT", [NCH, 128, D]),
            bo=P.din("bo", [128, NCH]),
            wr=P.din("wr", [NCH, 128, D], BF16),
            Wqkv=P.din("Wqkv", [28, 128, D]),
            Wku=P.din("Wku", [6, 128, 512]),
            Wvu=P.din("Wvu", [4, 128, 512]),
            cosq=P.din("cosq", [128, TC], BF16),
            sinq=P.din("sinq", [128, TC], BF16),
            cosk=P.din("cosk", [128, TC], BF16),
            sink=P.din("sink", [128, TC], BF16),
            RT=P.din("RT", [128, 128], BF16),
            ones_bf=P.din("ones_bf", [128, 1], BF16),
            ones_f=P.din("ones_f", [1, 128]),
            x4=P.dout("x4", [nTT, 128, NCH * TW]),
            qn=P.dout("qn", [16, 128, TC], BF16),
            qr=P.dout("qr", [8, 128, TC], BF16),
            kn=P.dout("kn", [4, 128, TC], BF16),
            kr=P.dout("kr", [2, 128, TC], BF16),
            vv=P.dout("vv", [4, 128, TC], BF16),
        )
    xT_i = io["xT"]; O_i = io["O"]; WoT_i = io["WoT"]; bo_i = io["bo"]
    wr_i = io["wr"]; Wqkv_i = io["Wqkv"]; Wku_i = io["Wku"]; Wvu_i = io["Wvu"]
    cosq_i = io["cosq"]; sinq_i = io["sinq"]; cosk_i = io["cosk"]; sink_i = io["sink"]
    RT_i = io["RT"]; ones_bf_d = io["ones_bf"]; ones_f_d = io["ones_f"]
    x4_o = io["x4"]; qn_o = io["qn"]; qr_o = io["qr"]; kn_o = io["kn"]
    kr_o = io["kr"]; vv_o = io["vv"]

    x_scr = P.dtmp("x_scr", [nTT, 128, NCH * TW])
    y_scr = P.dtmp("y_scr", [nTT, 128, NCH * TW], BF16)

    TH_X, SC_X = 0.33, 0.5

    kit = LNKit(P)
    kit.load_consts(ones_bf_d, ones_f_d)

    bo_s = P.sb("bo_s", [128, NCH])
    P.write(P.DQ, bo_s, [], lambda e, d=bo_s.ap, s=bo_i: e.dma_start(out=d, in_=s), step=16)
    cosq = P.sb("cosq_s", [128, TC], BF16)
    sinq = P.sb("sinq_s", [128, TC], BF16)
    cosk = P.sb("cosk_s", [128, TC], BF16)
    sink = P.sb("sink_s", [128, TC], BF16)
    for buf, src in ((cosq, cosq_i), (sinq, sinq_i), (cosk, cosk_i), (sink, sink_i)):
        P.write(P.DQ, buf, [], lambda e, d=buf.ap, s=src: e.dma_start(out=d, in_=s), step=16)
    RT = P.sb("RT_s", [128, 128], BF16)
    P.write(P.DQ, RT, [], lambda e, d=RT.ap, s=RT_i: e.dma_start(out=d, in_=s), step=16)

    xt = P.sb("xt", [128, NCH * TW])
    ct = P.sb("ct", [128, NCH * TW], BF16)
    xbig = P.sb("xbig", [128, NCH * TC], BF16)
    xlnf = P.sb("xlnf", [128, TW])
    tg = P.sb("tg", [128, TW])
    tg2 = P.sb("tg2", [128, TW])
    sevac = [P.sb(f"sevac{i}", [128, TW], BF16) for i in range(2)]
    wrcol = [P.sb(f"wrcol{i}", [128, NCH * 128], BF16) for i in range(2)]
    ckv = P.sb("ckv", [128, 4 * TC], BF16)
    rope_in = P.sb("rope_in", [128, TW], BF16)
    r_t1 = P.sb("r_t1", [128, TW])
    r_t2 = P.sb("r_t2", [128, TW])

    ps_a = P.ps("ps_a", [128, TW])
    ps_b = P.ps("ps_b", [128, TW])
    ps_m = [P.ps(f"ps_m{i}", [128, TW]) for i in range(2)]
    ps_r = P.ps("ps_r", [128, TW])

    ws_wo = WStream(P, "wo", WoT_i, NCH)
    ws_qkv = WStream(P, "wqkv", Wqkv_i, NCH)
    ws_ku = WStream(P, "wku", Wku_i, 4, nbuf=2)
    ws_vu = WStream(P, "wvu", Wvu_i, 4, nbuf=2)
    sc192 = P.constcol(S192)

    def xbig_sl(c, tt):
        return xbig.ap[:, (c * nTT + tt) * TW:(c * nTT + tt + 1) * TW]

    for tt in range(nTT):
        P.write(P.DQ, xt, [], lambda e, d=xt.ap, s=xT_i[tt, :, :]:
                e.dma_start(out=d, in_=s), step=16)
        oq = io.get("O_queue", P.DQ)
        P.write(oq, ct, [], lambda e, d=ct.ap, tt=tt: e.dma_start(out=d, in_=O_i[tt, :, :]),
                step=16)
        o_chunks = [(ct, ct.ap[:, c * TW:(c + 1) * TW]) for c in range(NCH)]
        for mc in range(NCH):
            if mc + 1 < NCH:
                ws_wo.prefetch(mc + 1)
            psm = ps_m[mc % 2]
            pieces = [(ws_wo.chunk(mc, kc)[0], ws_wo.chunk(mc, kc)[1],
                       o_chunks[kc][0], o_chunks[kc][1]) for kc in range(NCH)]
            emit_mm(P, psm, None, pieces)
            P.rw(P.VQ, xt, [psm, bo_s],
                 lambda e, o=xt.ap[:, mc * TW:(mc + 1) * TW], i=psm.ap,
                 b=bo_s.ap[:, mc:mc + 1],
                 xx=xt.ap[:, mc * TW:(mc + 1) * TW]:
                 e.scalar_tensor_tensor(out=o, in0=i, scalar=b, in1=xx,
                                        op0=ALU.add, op1=ALU.add))
        x_list = [(xt, xt.ap[:, c * TW:(c + 1) * TW]) for c in range(NCH)]

        def tern_out(c, t1, mrB, tt=tt):
            P.write(P.VQ, xlnf, [t1, mrB],
                    lambda e, o=xlnf.ap, a=t1.ap, b=mrB.ap:
                    e.tensor_sub(out=o, in0=a, in1=b))
            P.write(P.PQ, tg, [xlnf],
                    lambda e, o=tg.ap, i=xlnf.ap:
                    e.tensor_scalar(out=o, in0=i, scalar1=TH_X, scalar2=SC_X,
                                    op0=ALU.is_gt, op1=ALU.mult), step=1)
            P.write(P.VQ, tg2, [xlnf],
                    lambda e, o=tg2.ap, i=xlnf.ap:
                    e.tensor_scalar(out=o, in0=i, scalar1=-TH_X, scalar2=-SC_X,
                                    op0=ALU.is_lt, op1=ALU.mult))
            P.write(P.VQ, xbig, [tg, tg2],
                    lambda e, o=xbig_sl(c, tt), a=tg.ap, b=tg2.ap:
                    e.tensor_add(out=o, in0=a, in1=b))

        emit_ln(P, kit, x_list, ps_a, ps_b, tern_out)
        P.read(P.DQ, xt, lambda e, d=x_scr[tt, :, :], s=xt.ap:
               e.dma_start(out=d, in_=s), step=16)

    for mc in range(NCH):
        wr = wrcol[mc % 2]
        P.write(P.DQ, wr, [], lambda e, d=wr.ap, s=wr_i[mc, :, :]:
                e.dma_start(out=d, in_=s), step=16)
        for tt in range(nTT):
            psm = ps_m[tt % 2]
            pieces = [(wr, wr.ap[:, kc * 128:(kc + 1) * 128],
                       xbig, xbig_sl(kc, tt)) for kc in range(NCH)]
            emit_mm(P, psm, None, pieces)
            sv = sevac[tt % 2]
            P.write(P.VQ, sv, [psm],
                    lambda e, o=sv.ap, i=psm.ap: e.tensor_copy(out=o, in_=i))
            P.read(P.DQ, sv,
                   lambda e, d=y_scr[tt, :, mc * TW:(mc + 1) * TW], s=sv.ap:
                   e.dma_start(out=d, in_=s), step=16)

    for tt in range(nTT):
        P.write(P.DQ, xt, [], lambda e, d=xt.ap, s=x_scr[tt, :, :]:
                e.dma_start(out=d, in_=s), step=16)
        P.write(P.DQ, ct, [], lambda e, d=ct.ap, s=y_scr[tt, :, :]:
                e.dma_start(out=d, in_=s), step=16)
        for c in range(NCH):
            P.rw(P.VQ, xt, [ct],
                 lambda e, o=xt.ap[:, c * TW:(c + 1) * TW],
                 b=ct.ap[:, c * TW:(c + 1) * TW]:
                 e.tensor_tensor(out=o, in0=o, in1=b, op=ALU.add))
        x_list = [(xt, xt.ap[:, c * TW:(c + 1) * TW]) for c in range(NCH)]

        def ln_out(c, t1, mrB, tt=tt):
            P.write(P.VQ, xbig, [t1, mrB],
                    lambda e, o=xbig_sl(c, tt), a=t1.ap, b=mrB.ap:
                    e.tensor_sub(out=o, in0=a, in1=b))

        emit_ln(P, kit, x_list, ps_a, ps_b, ln_out)
        P.read(P.DQ, xt, lambda e, d=x4_o[tt, :, :], s=xt.ap:
               e.dma_start(out=d, in_=s), step=16)

    def rope_emit(psm, cos_sb, sin_sb, tt, dst_stage_ap):
        P.write(P.VQ, rope_in, [psm],
                lambda e, o=rope_in.ap, i=psm.ap: e.tensor_copy(out=o, in_=i))
        P.write(P.VQ, r_t1, [rope_in, cos_sb],
                lambda e, o=r_t1.ap, a=rope_in.ap,
                b=cos_sb.ap[:, tt * TW:(tt + 1) * TW]:
                e.tensor_mul(out=o, in0=a, in1=b))
        emit_mm(P, ps_r, None, [(RT, RT.ap, rope_in, rope_in.ap)])
        P.write(P.VQ, r_t2, [ps_r, sin_sb],
                lambda e, o=r_t2.ap, a=ps_r.ap,
                b=sin_sb.ap[:, tt * TW:(tt + 1) * TW]:
                e.tensor_mul(out=o, in0=a, in1=b))
        P.write(P.VQ, ct, [r_t1, r_t2],
                lambda e, o=dst_stage_ap, a=r_t1.ap, b=r_t2.ap:
                e.tensor_add(out=o, in0=a, in1=b))

    for mc in range(28):
        if mc + 1 < 28:
            ws_qkv.prefetch(mc + 1)
        for tt in range(nTT):
            psm = ps_m[tt % 2]
            pieces = [(ws_qkv.chunk(mc, kc)[0], ws_qkv.chunk(mc, kc)[1],
                       xbig, xbig_sl(kc, tt)) for kc in range(NCH)]
            emit_mm(P, psm, None, pieces)
            dst = ct.ap[:, tt * TW:(tt + 1) * TW]
            if mc < 16:
                P.write(P.VQ, ct, [psm, sc192],
                        lambda e, o=dst, i=psm.ap, s2=sc192.ap[:, 0:1]:
                        e.tensor_scalar_mul(out=o, in0=i, scalar1=s2))
            elif mc < 24:
                rope_emit(psm, cosq, sinq, tt, dst)
            else:
                cc = mc - 24
                P.write(P.VQ, ckv, [psm],
                        lambda e, o=ckv.ap[:, (cc * nTT + tt) * TW:(cc * nTT + tt + 1) * TW],
                        i=psm.ap: e.tensor_copy(out=o, in_=i))
        if mc < 16:
            P.read(P.DQ, ct, lambda e, d=qn_o[mc, :, :], s=ct.ap[:, 0:TC]:
                   e.dma_start(out=d, in_=s), step=16)
        elif mc < 24:
            P.read(P.DQ, ct, lambda e, d=qr_o[mc - 16, :, :], s=ct.ap[:, 0:TC]:
                   e.dma_start(out=d, in_=s), step=16)

    ckv_chunks = lambda tt: [(ckv, ckv.ap[:, (c * nTT + tt) * TW:(c * nTT + tt + 1) * TW])
                             for c in range(4)]
    for mc in range(6):
        if mc + 1 < 6:
            ws_ku.prefetch(mc + 1)
        for tt in range(nTT):
            psm = ps_m[tt % 2]
            pieces = [(ws_ku.chunk(mc, kc)[0], ws_ku.chunk(mc, kc)[1],
                       ckv_chunks(tt)[kc][0], ckv_chunks(tt)[kc][1]) for kc in range(4)]
            emit_mm(P, psm, None, pieces)
            dst = ct.ap[:, tt * TW:(tt + 1) * TW]
            if mc < 4:
                P.write(P.VQ, ct, [psm],
                        lambda e, o=dst, i=psm.ap: e.tensor_copy(out=o, in_=i))
            else:
                rope_emit(psm, cosk, sink, tt, dst)
        if mc < 4:
            P.read(P.DQ, ct, lambda e, d=kn_o[mc, :, :], s=ct.ap[:, 0:TC]:
                   e.dma_start(out=d, in_=s), step=16)
        else:
            P.read(P.DQ, ct, lambda e, d=kr_o[mc - 4, :, :], s=ct.ap[:, 0:TC]:
                   e.dma_start(out=d, in_=s), step=16)
    for mc in range(4):
        if mc + 1 < 4:
            ws_vu.prefetch(mc + 1)
        for tt in range(nTT):
            psm = ps_m[tt % 2]
            pieces = [(ws_vu.chunk(mc, kc)[0], ws_vu.chunk(mc, kc)[1],
                       ckv_chunks(tt)[kc][0], ckv_chunks(tt)[kc][1]) for kc in range(4)]
            emit_mm(P, psm, None, pieces)
            P.write(P.VQ, ct, [psm],
                    lambda e, o=ct.ap[:, tt * TW:(tt + 1) * TW], i=psm.ap:
                    e.tensor_copy(out=o, in_=i))
        P.read(P.DQ, ct, lambda e, d=vv_o[mc, :, :], s=ct.ap[:, 0:TC]:
               e.dma_start(out=d, in_=s), step=16)

    if standalone:
        return P.assemble()


def reorder_wqkv_cols(Wqkv):
    idx = []
    for h in range(16):
        idx.extend(range(h * 192, h * 192 + 128))
    for h in range(16):
        idx.extend(range(h * 192 + 128, h * 192 + 192))
    idx.extend(range(3072, 3584))
    return np.ascontiguousarray(Wqkv[:, idx])


def reorder_wup_cols(Wu, dqk=192, dn=128):
    nkv = Wu.shape[1] // dqk
    idx = []
    for g in range(nkv):
        idx.extend(range(g * dqk, g * dqk + dn))
    for g in range(nkv):
        idx.extend(range(g * dqk + dn, g * dqk + dqk))
    return np.ascontiguousarray(Wu[:, idx])


def rope_tables():
    inv = 1.0 / (10000.0 ** (np.arange(0, 64, 2, dtype=np.float64) / 64.0))
    t = np.arange(L, dtype=np.float64)
    freqs = np.outer(t, inv)
    emb = np.concatenate([freqs, freqs], 1)
    cosT = np.cos(emb).T.astype(np.float32)
    sinT = np.sin(emb).T.astype(np.float32)
    return cosT, sinT


def rot_matrix():
    Rb = np.zeros((64, 64), np.float32)
    for i in range(32):
        Rb[i, i + 32] = -1.0
        Rb[i + 32, i] = 1.0
    Rfull = np.zeros((128, 128), np.float32)
    Rfull[:64, :64] = Rb
    Rfull[64:, 64:] = Rb
    return Rfull.T.copy()


def rope_host_tables(scale):
    cosT, sinT = rope_tables()
    cos2 = np.vstack([cosT, cosT]) * scale
    sin2 = np.vstack([sinT, sinT]) * scale
    return f32c(cos2), f32c(sin2)


def run_launch4(x2, O, w_out, b_out, wr, Wqkv, Wk_up, Wv_up):
    hc = host_consts()
    nc = build_launch4()
    Wqkv_r = reorder_wqkv_cols(np.asarray(Wqkv))
    Wku_r = reorder_wup_cols(np.asarray(Wk_up))
    S192 = 192.0 ** -0.5
    cq_full, sq_full = rope_host_tables(S192)
    ck_full, sk_full = rope_host_tables(1.0)
    in_maps = []
    for c in range(NCORE):
        b, hf = c // 2, c % 2
        sl = slice(hf * TC, hf * TC + TC)
        in_maps.append(dict(
            xT=swz_act(f32c(x2[b, sl].T)),
            O=swz_act(np.ascontiguousarray(O[b, sl].T.astype(BF))),
            WoT=swz_w(np.asarray(w_out).T), bo=to_cols(b_out),
            wr=wr,
            Wqkv=swz_w(Wqkv_r), Wku=swz_w(Wku_r), Wvu=swz_w(np.asarray(Wv_up)),
            cosq=cq_full[:, sl].astype(BF), sinq=sq_full[:, sl].astype(BF),
            cosk=ck_full[:, sl].astype(BF), sink=sk_full[:, sl].astype(BF),
            RT=rot_matrix().astype(BF),
            ones_bf=hc["ones_bf"], ones_f=hc["ones_f"],
        ))
    res = run_bass_kernel_spmd(nc, in_maps, core_ids=list(range(NCORE)))
    x4 = np.zeros((B, L, D), np.float32)
    qn = np.zeros((B, L, 2048), np.float32)
    qr = np.zeros((B, L, 1024), np.float32)
    kn = np.zeros((B, L, 512), np.float32)
    kr = np.zeros((B, L, 256), np.float32)
    vv = np.zeros((B, L, 512), np.float32)
    for c in range(NCORE):
        b, hf = c // 2, c % 2
        sl = slice(hf * TC, hf * TC + TC)
        r = res.results[c]
        x4[b, sl] = unswz_act(r["x4"], D, TC).T
        for name, arr, nc_ in (("qn", qn, 16), ("qr", qr, 8), ("kn", kn, 4),
                               ("kr", kr, 2), ("vv", vv, 4)):
            arr[b, sl] = r[name].astype(np.float32).reshape(nc_ * 128, TC).T
    return x4, qn, qr, kn, kr, vv


def run_launch5(qn, qr, kn, kr, vv):
    nc = build_flash(rope=True)
    in_maps = []
    for c in range(NCORE):
        b, hh = c // 2, c % 2
        heads = range(hh * 8, hh * 8 + 8)
        qh = np.stack([qn[b, :, h * 128:(h + 1) * 128].T.astype(BF) for h in heads])
        qrh = np.stack([qr[b, :, h * 64:(h + 1) * 64].T.astype(BF) for h in heads])
        kh = np.stack([kn[b, :, (h // 4) * 128:(h // 4 + 1) * 128].T.astype(BF)
                       for h in heads])
        krh = np.stack([kr[b, :, (h // 4) * 64:(h // 4 + 1) * 64].T.astype(BF)
                        for h in heads])
        vh = np.stack([vv[b, :, (h // 4) * 128:(h // 4 + 1) * 128].astype(BF)
                       for h in heads])
        in_maps.append(_flash_inmaps(qh, kh, vh, qrh, krh))
    res = run_bass_kernel_spmd(nc, in_maps, core_ids=list(range(NCORE)))
    out = np.zeros((B, L, D), np.float32)
    for c in range(NCORE):
        b, hh = c // 2, c % 2
        out[b, :, hh * 1024:hh * 1024 + 1024] = _flash_out(res.results[c])
    return out



def build_launch6(with_outproj, emit_xln, P=None, io=None):
    standalone = P is None
    nTT = TC // TW
    NEX = 16
    if standalone:
        P = Prog()
        io = dict(
            xT=P.din("xT", [nTT, 128, NCH * TW]),
            gwT=P.din("gwT", [1, 128, NEX * NCH]),
            eb=P.din("eb", [1, NEX]),
            sel=P.din("sel", [16, NEX * 128], BF16),
            gw=P.din("gw", [NEX, 8, 128, D]),
            uw=P.din("uw", [NEX, 8, 128, D]),
            dw=P.din("dw", [NEX, 16, 128, 1024]),
            shg=P.din("shg", [16, 128, D]),
            shu=P.din("shu", [16, 128, D]),
            shd=P.din("shd", [16, 128, D]),
            idt16=P.din("idt16", [16, 16], BF16),
            idt128f=P.din("idt128f", [128, 128], BF16),
            ones_bf=P.din("ones_bf", [128, 1], BF16),
            ones_f=P.din("ones_f", [1, 128]),
            x=P.dout("x", [nTT, 128, NCH * TW]),
        )
        if with_outproj:
            io["O"] = P.din("O", [nTT, 128, NCH * TW], BF16)
            io["WoT"] = P.din("WoT", [NCH, 128, D])
        if emit_xln:
            io["xln"] = P.dout("xln", [nTT, 128, NCH * TW], BF16)
    xT_i = io["xT"]
    if with_outproj:
        O_i = io["O"]; WoT_i = io["WoT"]
    gwT_i = io["gwT"]; eb_i = io["eb"]; sel_i = io["sel"]
    gw_i = io["gw"]; uw_i = io["uw"]; dw_i = io["dw"]
    shg_i = io["shg"]; shu_i = io["shu"]; shd_i = io["shd"]
    idt16_i = io["idt16"]; idt128f_i = io["idt128f"]
    ones_bf_d = io["ones_bf"]; ones_f_d = io["ones_f"]
    x_o = io["x"]
    if emit_xln:
        xln_o = io["xln"]
    xln_scr = P.dtmp("xln_scr", [nTT, 128, NCH * TW], BF16)
    x_scr = P.dtmp("x_scr6", [nTT, 128, NCH * TW]) if with_outproj else xT_i

    kit = LNKit(P)
    kit.load_consts(ones_bf_d, ones_f_d)

    gwres = WResident(P, "gwt", gwT_i, n_kc=NCH, n_mc=1, mcw=16)
    sel_s = P.sb("sel_s", [16, NEX * 128], BF16)
    P.write(P.DQ, sel_s, [], lambda e, d=sel_s.ap, s=sel_i: e.dma_start(out=d, in_=s), step=16)
    eb_row = P.sb("eb_row", [1, NEX])
    P.write(P.DQ, eb_row, [], lambda e, d=eb_row.ap, s=eb_i: e.dma_start(out=d, in_=s), step=16)
    idt16 = P.sb("idt16_s", [16, 16], BF16)
    P.write(P.DQ, idt16, [], lambda e, d=idt16.ap, s=idt16_i: e.dma_start(out=d, in_=s), step=16)
    idt128f = P.sb("idt128f_s", [128, 128], BF16)
    P.write(P.DQ, idt128f, [], lambda e, d=idt128f.ap, s=idt128f_i: e.dma_start(out=d, in_=s), step=16)

    xt = P.sb("xt", [128, NCH * TW])
    xbh = P.sb("xbh", [128, NCH * 2 * TW], BF16)
    ct = xbh
    acc = P.sb("acc", [128, NCH * 2 * TW])
    aw = P.sb("aw", [128, 8 * 2 * TW], BF16)
    gcol = [P.sb(f"gcol{i}", [128, D], BF16) for i in range(2)]
    ucol = [P.sb(f"ucol{i}", [128, D], BF16) for i in range(2)]
    dcol = [P.sb(f"dcol{i}", [128, 1024], BF16) for i in range(2)]
    wb_sb = P.sb("wb_sb", [128, 2 * TW], BF16)
    r_sg = P.sb("r_sg", [128, TW])
    r_t = P.sb("r_t", [128, TW])
    ln_stage = P.sb("ln_stage", [128, TW], BF16)
    sg_T = P.sb("sg_T", [16, TC], BF16)
    wT_sb = P.sb("wT_sb", [16, TC], BF16)
    sc_tm = P.sb("sc_tm", [128, NEX])
    ebB = P.sb("ebB", [128, NEX])
    r_s4c = P.sb("r_s4c", [128, NEX])
    r_m1 = P.sb("r_m1", [128, 4])
    r_m2 = P.sb("r_m2", [128, 4])
    r_eq = P.sb("r_eq", [128, NEX])
    r_msk = P.sb("r_msk", [128, NEX])
    r_ms0 = P.sb("r_ms0", [128, NEX])
    r_gsc = P.sb("r_gsc", [128, 4])
    r_gm = P.sb("r_gm", [128, 4])
    r_pen = P.sb("r_pen", [128, 4])
    r_mx = P.sb("r_mx", [128, 1])
    r_w = P.sb("r_w", [128, NEX])
    r_wb = P.sb("r_wb", [128, NEX], BF16)
    r_sum = P.sb("r_sum", [128, 1])
    r_rcp = P.sb("r_rcp", [128, 1])
    cc25 = P.constcol(2.5)

    ps_a = P.ps("ps_a", [128, TW])
    ps_b = P.ps("ps_b", [128, TW])
    ps_g = P.ps("ps_g", [128, TW])
    ps_u = P.ps("ps_u", [128, TW])
    ps_d = P.ps("ps_d", [128, TW])
    ps_wb = P.ps("ps_wb", [128, TW])
    ps_r16 = P.ps("ps_r16", [16, TW])
    ps_tr = P.ps("ps_tr", [128, 128], BF16)

    if with_outproj:
        ws_wo = WStream(P, "wo", WoT_i, NCH, nbuf=1)

    for tt in range(nTT):
        P.write(P.DQ, xt, [], lambda e, d=xt.ap, s=xT_i[tt, :, :]:
                e.dma_start(out=d, in_=s), step=16)
        if with_outproj:
            oq = io.get("O_queue", P.DQ)
            P.write(oq, ct, [], lambda e, d=ct.ap[:, 0:NCH * TW], tt=tt:
                    e.dma_start(out=d, in_=O_i[tt, :, :]), step=16)
            o_chunks = [(ct, ct.ap[:, c * TW:(c + 1) * TW]) for c in range(NCH)]
            for mc in range(NCH):
                if mc + 1 < NCH:
                    ws_wo.prefetch(mc + 1)
                psm = ps_g if mc % 2 == 0 else ps_u
                pieces = [(ws_wo.chunk(mc, kc)[0], ws_wo.chunk(mc, kc)[1],
                           o_chunks[kc][0], o_chunks[kc][1]) for kc in range(NCH)]
                emit_mm(P, psm, None, pieces)
                P.rw(P.VQ, xt, [psm],
                     lambda e, o=xt.ap[:, mc * TW:(mc + 1) * TW], i=psm.ap,
                     xx=xt.ap[:, mc * TW:(mc + 1) * TW]:
                     e.tensor_tensor(out=o, in0=i, in1=xx, op=ALU.add))
        x_list = [(xt, xt.ap[:, c * TW:(c + 1) * TW]) for c in range(NCH)]

        def ln_out(c, t1, mrB, tt=tt):
            P.write(P.VQ, ln_stage, [t1, mrB],
                    lambda e, o=ln_stage.ap, a=t1.ap, b=mrB.ap:
                    e.tensor_sub(out=o, in0=a, in1=b))
            P.read(P.DQ, ln_stage,
                   lambda e, d=xln_scr[tt, :, c * TW:(c + 1) * TW], s=ln_stage.ap:
                   e.dma_start(out=d, in_=s), step=16)

        emit_ln(P, kit, x_list, ps_a, ps_b, ln_out)
        if with_outproj:
            P.read(P.DQ, xt, lambda e, d=x_scr[tt, :, :], s=xt.ap:
                   e.dma_start(out=d, in_=s), step=16)

    def xbh_sl(c, tt2):
        return xbh.ap[:, (c * 2 + tt2) * TW:(c * 2 + tt2 + 1) * TW]

    P.TQ.wait(eb_row.wtick)
    P.TQ.wait(kit.ones_f.wtick)
    t = P.TQ.run(lambda e, o=ps_wb.ap[:, 0:NEX], w=kit.ones_f.ap, r=eb_row.ap:
                 e.matmul(o, w, r, start=True, stop=True))
    eb_row.rticks.append(t)
    ps_wb.rticks = []
    ps_wb.wtick = t
    P.write(P.VQ, ebB, [ps_wb], lambda e, o=ebB.ap, i=ps_wb.ap[:, 0:NEX]:
            e.tensor_copy(out=o, in_=i))

    for tt in range(nTT):
        P.write(P.DQ, xbh, [],
                lambda e, d=xbh.ap[:, 0:NCH * TW].rearrange("p (c t) -> p c t", c=NCH),
                s=xln_scr[tt, :, :].rearrange("p (c t) -> p c t", c=NCH):
                e.dma_start(out=d, in_=s), step=16)
        pieces = [(gwres.chunk(0, kc)[0], gwres.chunk(0, kc)[1],
                   xbh, xbh.ap[:, kc * TW:(kc + 1) * TW]) for kc in range(NCH)]
        emit_mm(P, ps_r16, None, pieces)
        P.write(P.SQ, sg_T, [ps_r16],
                lambda e, o=sg_T.ap[:, tt * TW:(tt + 1) * TW], i=ps_r16.ap:
                e.activation(out=o, in_=i, func=AF.Sigmoid))
    for j in range(TC // 128):
        emit_group(P, P.TQ, ps_tr, [sg_T, idt16],
                   [lambda e, o=ps_tr.ap[:, 0:NEX], i=sg_T.ap[:, j * 128:(j + 1) * 128],
                    dd=idt16.ap: e.transpose(o, i, dd)])
        P.write(P.VQ, sc_tm, [ps_tr], lambda e, o=sc_tm.ap, i=ps_tr.ap[:, 0:NEX]:
                e.tensor_copy(out=o, in_=i))
        P.write(P.VQ, r_s4c, [sc_tm, ebB], lambda e, o=r_s4c.ap, a=sc_tm.ap, b=ebB.ap:
                e.tensor_add(out=o, in0=a, in1=b))
        for g in range(4):
            P.write(P.VQ, r_m1, [r_s4c],
                    lambda e, o=r_m1.ap[:, g:g + 1], i=r_s4c.ap[:, 4 * g:4 * g + 4]:
                    e.reduce_max(out=o, in_=i, axis=AX.X))
        for g in range(4):
            P.write(P.VQ, r_eq, [r_s4c, r_m1],
                    lambda e, o=r_eq.ap[:, 4 * g:4 * g + 4],
                    i=r_s4c.ap[:, 4 * g:4 * g + 4], m=r_m1.ap[:, g:g + 1]:
                    e.tensor_scalar(out=o, in0=i, scalar1=m, scalar2=None,
                                    op0=ALU.is_equal))
        P.write(P.VQ, r_msk, [r_eq, r_s4c],
                lambda e, o=r_msk.ap, a=r_eq.ap, b=r_s4c.ap:
                e.scalar_tensor_tensor(out=o, in0=a, scalar=NEG_BIG, in1=b,
                                       op0=ALU.mult, op1=ALU.add))
        for g in range(4):
            P.write(P.VQ, r_m2, [r_msk],
                    lambda e, o=r_m2.ap[:, g:g + 1], i=r_msk.ap[:, 4 * g:4 * g + 4]:
                    e.reduce_max(out=o, in_=i, axis=AX.X))
        P.write(P.VQ, r_gsc, [r_m1, r_m2], lambda e, o=r_gsc.ap, a=r_m1.ap, b=r_m2.ap:
                e.tensor_add(out=o, in0=a, in1=b))
        P.write(P.VQ, r_mx, [r_gsc], lambda e, o=r_mx.ap, i=r_gsc.ap:
                e.reduce_max(out=o, in_=i, axis=AX.X))
        P.write(P.VQ, r_gm, [r_gsc, r_mx],
                lambda e, o=r_gm.ap, i=r_gsc.ap, m=r_mx.ap[:, 0:1]:
                e.tensor_scalar(out=o, in0=i, scalar1=m, scalar2=None,
                                op0=ALU.is_equal))
        P.rw(P.VQ, r_gm, [r_gsc],
             lambda e, o=r_gm.ap, a=r_gm.ap, b=r_gsc.ap:
             e.scalar_tensor_tensor(out=o, in0=a, scalar=NEG_BIG, in1=b,
                                    op0=ALU.mult, op1=ALU.add))
        P.write(P.VQ, r_mx, [r_gm], lambda e, o=r_mx.ap, i=r_gm.ap:
                e.reduce_max(out=o, in_=i, axis=AX.X))
        P.write(P.VQ, r_gm, [r_gsc, r_mx],
                lambda e, o=r_gm.ap, i=r_gsc.ap, m=r_mx.ap[:, 0:1]:
                e.tensor_scalar(out=o, in0=i, scalar1=m, scalar2=None,
                                op0=ALU.is_ge))
        P.write(P.VQ, r_pen, [r_gm],
                lambda e, o=r_pen.ap, i=r_gm.ap:
                e.tensor_scalar(out=o, in0=i, scalar1=-1.0, scalar2=-NEG_BIG,
                                op0=ALU.add, op1=ALU.mult))
        for g in range(4):
            P.write(P.VQ, r_ms0, [r_s4c, r_pen],
                    lambda e, o=r_ms0.ap[:, 4 * g:4 * g + 4],
                    i=r_s4c.ap[:, 4 * g:4 * g + 4], m=r_pen.ap[:, g:g + 1]:
                    e.tensor_scalar(out=o, in0=i, scalar1=m, scalar2=None,
                                    op0=ALU.add))
        P.write(P.VQ, r_msk, [r_ms0], lambda e, o=r_msk.ap, i=r_ms0.ap:
                e.tensor_copy(out=o, in_=i))
        for _round in range(3):
            P.write(P.VQ, r_mx, [r_msk], lambda e, o=r_mx.ap, i=r_msk.ap:
                    e.reduce_max(out=o, in_=i, axis=AX.X))
            P.write(P.VQ, r_eq, [r_msk, r_mx],
                    lambda e, o=r_eq.ap, i=r_msk.ap, m=r_mx.ap[:, 0:1]:
                    e.tensor_scalar(out=o, in0=i, scalar1=m, scalar2=None,
                                    op0=ALU.is_equal))
            P.rw(P.VQ, r_msk, [r_eq],
                 lambda e, o=r_msk.ap, a=r_eq.ap, b=r_msk.ap:
                 e.scalar_tensor_tensor(out=o, in0=a, scalar=NEG_BIG, in1=b,
                                        op0=ALU.mult, op1=ALU.add))
        P.write(P.VQ, r_mx, [r_msk], lambda e, o=r_mx.ap, i=r_msk.ap:
                e.reduce_max(out=o, in_=i, axis=AX.X))
        P.write(P.VQ, r_eq, [r_ms0, r_mx],
                lambda e, o=r_eq.ap, i=r_ms0.ap, m=r_mx.ap[:, 0:1]:
                e.tensor_scalar(out=o, in0=i, scalar1=m, scalar2=None,
                                op0=ALU.is_ge))
        P.write(P.VQ, r_w, [sc_tm, r_eq], lambda e, o=r_w.ap, a=sc_tm.ap, b=r_eq.ap:
                e.tensor_mul(out=o, in0=a, in1=b))
        P.write(P.VQ, r_sum, [r_w], lambda e, o=r_sum.ap, i=r_w.ap:
                e.reduce_sum(out=o, in_=i, axis=AX.X))
        P.rw(P.VQ, r_sum, [], lambda e, o=r_sum.ap:
             e.tensor_scalar_add(out=o, in0=o, scalar1=1e-20))
        P.write(P.VQ, r_rcp, [r_sum], lambda e, o=r_rcp.ap, i=r_sum.ap:
                e.reciprocal(out=o, in_=i))
        P.rw(P.VQ, r_w, [r_rcp, cc25],
             lambda e, o=r_w.ap, i=r_w.ap, m=r_rcp.ap[:, 0:1], c2=cc25.ap[:, 0:1]:
             e.tensor_scalar(out=o, in0=i, scalar1=m, scalar2=c2,
                             op0=ALU.mult, op1=ALU.mult))
        P.write(P.VQ, r_wb, [r_w],
                lambda e, o=r_wb.ap, i=r_w.ap: e.tensor_copy(out=o, in_=i))
        emit_group(P, P.TQ, ps_tr, [r_wb, idt128f],
                   [lambda e, o=ps_tr.ap[0:16, 0:128], i=r_wb.ap, dd=idt128f.ap:
                    e.transpose(o, i, dd)])
        P.write(P.VQ, wT_sb, [ps_tr],
                lambda e, o=wT_sb.ap[:, j * 128:(j + 1) * 128], i=ps_tr.ap[0:16, 0:128]:
                e.tensor_copy(out=o, in_=i))

    for H in range(2):
        for tt2 in range(2):
            P.write(P.DQ, xbh, [],
                    lambda e,
                    d=xbh.ap.rearrange("p (c two t) -> p c two t", two=2, t=TW)[:, :, tt2, :],
                    s=xln_scr[2 * H + tt2, :, :].rearrange("p (c t) -> p c t", c=NCH):
                    e.dma_start(out=d, in_=s), step=16)
        P.write(P.VQ, acc, [], lambda e, o=acc.ap: e.memset(o, 0.0))
        jobs = [("e", n) for n in range(NEX)] + [("s", 0), ("s", 1)]
        for (kind, n) in jobs:
            if kind == "e":
                for tt2 in range(2):
                    pieces = [(sel_s, sel_s.ap[:, n * 128:(n + 1) * 128],
                               wT_sb, wT_sb.ap[:, (2 * H + tt2) * TW:(2 * H + tt2 + 1) * TW])]
                    emit_mm(P, ps_wb, None, pieces)
                    P.write(P.VQ, wb_sb, [ps_wb],
                            lambda e, o=wb_sb.ap[:, tt2 * TW:(tt2 + 1) * TW],
                            i=ps_wb.ap: e.tensor_copy(out=o, in_=i))
            for fc in range(8):
                gc = gcol[fc % 2]
                uc = ucol[fc % 2]
                gsrc = gw_i[n, fc, :, :] if kind == "e" else shg_i[n * 8 + fc, :, :]
                usrc = uw_i[n, fc, :, :] if kind == "e" else shu_i[n * 8 + fc, :, :]
                P.write(P.PQ, gc, [], lambda e, d=gc.ap, s=gsrc:
                        e.dma_start(out=d, in_=s), step=16)
                P.write(P.PQ, uc, [], lambda e, d=uc.ap, s=usrc:
                        e.dma_start(out=d, in_=s), step=16)
                for tt2 in range(2):
                    psg = ps_g if tt2 == 0 else ps_a
                    psu = ps_u if tt2 == 0 else ps_b
                    pieces = [(gc, gc.ap[:, kc * 128:(kc + 1) * 128],
                               xbh, xbh_sl(kc, tt2)) for kc in range(NCH)]
                    emit_mm(P, psg, None, pieces)
                    pieces = [(uc, uc.ap[:, kc * 128:(kc + 1) * 128],
                               xbh, xbh_sl(kc, tt2)) for kc in range(NCH)]
                    emit_mm(P, psu, None, pieces)
                    P.write(P.SQ, r_sg, [psg],
                            lambda e, o=r_sg.ap, i=psg.ap:
                            e.activation(out=o, in_=i, func=AF.Silu))
                    P.write(P.VQ, r_t, [r_sg, psu],
                            lambda e, o=r_t.ap, a=r_sg.ap, b=psu.ap:
                            e.tensor_mul(out=o, in0=a, in1=b))
                    dst = aw.ap[:, (fc * 2 + tt2) * TW:(fc * 2 + tt2 + 1) * TW]
                    if kind == "e":
                        P.write(P.VQ, aw, [r_t, wb_sb],
                                lambda e, o=dst, a=r_t.ap,
                                b=wb_sb.ap[:, tt2 * TW:(tt2 + 1) * TW]:
                                e.tensor_mul(out=o, in0=a, in1=b))
                    else:
                        P.write(P.VQ, aw, [r_t],
                                lambda e, o=dst, a=r_t.ap: e.tensor_copy(out=o, in_=a))
            for hc in range(NCH):
                dc = dcol[hc % 2]
                dsrc = (dw_i[n, hc, :, :] if kind == "e"
                        else shd_i[hc, :, n * 1024:(n + 1) * 1024])
                P.write(P.PQ, dc, [], lambda e, d=dc.ap, s=dsrc:
                        e.dma_start(out=d, in_=s), step=16)
                nkc = 8
                for tt2 in range(2):
                    psd = ps_d if (hc * 2 + tt2) % 2 == 0 else ps_g
                    pieces = [(dc, dc.ap[:, kc * 128:(kc + 1) * 128],
                               aw, aw.ap[:, (kc * 2 + tt2) * TW:(kc * 2 + tt2 + 1) * TW])
                              for kc in range(nkc)]
                    emit_mm(P, psd, None, pieces)
                    P.rw(P.VQ, acc, [psd],
                         lambda e, o=acc.ap[:, (hc * 2 + tt2) * TW:(hc * 2 + tt2 + 1) * TW],
                         i=psd.ap,
                         xx=acc.ap[:, (hc * 2 + tt2) * TW:(hc * 2 + tt2 + 1) * TW]:
                         e.tensor_tensor(out=o, in0=i, in1=xx, op=ALU.add))
        for tt2 in range(2):
            tt = 2 * H + tt2
            P.write(P.DQ, xt, [], lambda e, d=xt.ap, s=x_scr[tt, :, :]:
                    e.dma_start(out=d, in_=s), step=16)
            for c in range(NCH):
                P.rw(P.VQ, xt, [acc],
                     lambda e, o=xt.ap[:, c * TW:(c + 1) * TW],
                     a=acc.ap[:, (c * 2 + tt2) * TW:(c * 2 + tt2 + 1) * TW],
                     xx=xt.ap[:, c * TW:(c + 1) * TW]:
                     e.tensor_tensor(out=o, in0=a, in1=xx, op=ALU.add))
            if emit_xln:
                x_list = [(xt, xt.ap[:, c * TW:(c + 1) * TW]) for c in range(NCH)]

                def ln_out2(c, t1, mrB, tt=tt):
                    P.write(P.VQ, ln_stage, [t1, mrB],
                            lambda e, o=ln_stage.ap, a=t1.ap, b=mrB.ap:
                            e.tensor_sub(out=o, in0=a, in1=b))
                    P.read(P.DQ, ln_stage,
                           lambda e, d=xln_o[tt, :, c * TW:(c + 1) * TW],
                           s=ln_stage.ap: e.dma_start(out=d, in_=s), step=16)

                emit_ln(P, kit, x_list, ps_a, ps_b, ln_out2)
            P.read(P.DQ, xt, lambda e, d=x_o[tt, :, :], s=xt.ap:
                   e.dma_start(out=d, in_=s), step=16)

    if standalone:
        return P.assemble()


def moe_host_weights(inp, prefix="L1_ffn"):
    gate_w = np.asarray(inp[f"{prefix}_gate_w"])
    up_w = np.asarray(inp[f"{prefix}_up_w"])
    down_w = np.asarray(inp[f"{prefix}_down_w"])
    gw = np.stack([swz_w(gate_w[n].T) for n in range(16)])
    uw = np.stack([swz_w(up_w[n].T) for n in range(16)])
    dw = np.stack([swz_w(down_w[n].T) for n in range(16)])
    shg = swz_w(np.asarray(inp[f"{prefix}_sh_gate_w"]).T)
    shu = swz_w(np.asarray(inp[f"{prefix}_sh_up_w"]).T)
    shd = swz_w(np.asarray(inp[f"{prefix}_sh_down_w"]).T)
    gwT = swz_w(np.asarray(inp[f"{prefix}_gate_weight"]).T, mcw=16)
    sel = np.zeros((16, 16 * 128), np.float32)
    for n in range(16):
        sel[n, n * 128:(n + 1) * 128] = 1.0
    return dict(gw=gw, uw=uw, dw=dw, shg=shg, shu=shu, shd=shd, gwT=gwT,
                sel=sel.astype(BF),
                eb=f32c(np.asarray(inp[f"{prefix}_e_score_correction_bias"]).reshape(1, 16)),
                idt16=np.eye(16).astype(BF),
                idt128f=np.eye(128).astype(BF))


def run_launch6(x, O, Wo, moe_w, with_outproj=True, emit_xln=True):
    hc = host_consts()
    nc = build_launch6(with_outproj, emit_xln)
    common = dict(moe_w)
    common.update(ones_bf=hc["ones_bf"], ones_f=hc["ones_f"])
    if with_outproj:
        common["WoT"] = swz_w(np.asarray(Wo))
    in_maps = []
    for c in range(NCORE):
        b, hf = c // 2, c % 2
        sl = slice(hf * TC, hf * TC + TC)
        m = dict(common)
        m["xT"] = swz_act(f32c(x[b, sl].T))
        if with_outproj:
            m["O"] = swz_act(np.ascontiguousarray(O[b, sl].T.astype(BF)))
        in_maps.append(m)
    res = run_bass_kernel_spmd(nc, in_maps, core_ids=list(range(NCORE)))
    xo = np.zeros((B, L, D), np.float32)
    xlno = np.zeros((B, L, D), np.float32) if emit_xln else None
    for c in range(NCORE):
        b, hf = c // 2, c % 2
        sl = slice(hf * TC, hf * TC + TC)
        r = res.results[c]
        xo[b, sl] = unswz_act(r["x"], D, TC).T
        if emit_xln:
            xlno[b, sl] = unswz_act(r["xln"].astype(np.float32), D, TC).T
    return xo, xlno



def build_launch7(P=None, io=None):
    standalone = P is None
    nTT = TC // TW
    if standalone:
        P = Prog()
        io = dict(
            xT=P.din("xT", [nTT, 128, NCH * TW]),
            xc=P.din("xc", [4, nTT, 128, NCH * TW], BF16),
            ar=P.din("ar", [128, 128]),
            ai=P.din("ai", [128, 128]),
            cr=P.din("cr", [128, 128]),
            ci=P.din("ci", [128, 128]),
            ldt8=P.din("ldt8", [128, 128]),
            dp=P.din("dp", [128, NCH]),
            x=P.dout("x", [nTT, 128, NCH * TW]),
        )
    xT_i = io["xT"]; xc_i = io["xc"]
    ar_i = io["ar"]; ai_i = io["ai"]; cr_i = io["cr"]; ci_i = io["ci"]
    ldt8_i = io["ldt8"]; dp_i = io["dp"]; x_o = io["x"]
    kmask_i = io.get("kmask")

    names = ["ar", "ai", "cr", "ci", "ldt8", "dt8", "t_a", "t_b", "t_c", "t_d",
             "e1r", "e1i", "Cr2", "Ci2", "er", "ei", "er2", "ei2", "prod"]
    w = {n: P.sb("w_" + n, [128, 128]) for n in names}
    kct = [P.sb(f"kct{t}", [128, NCH]) for t in range(4)]
    dp = P.sb("dp_s", [128, NCH])
    xt = P.sb("xt", [128, NCH * TW])
    cb = P.sb("cb", [128, NCH * TW], BF16)
    pihalf = P.constcol(math.pi / 2.0)

    for nm, src in (("ar", ar_i), ("ai", ai_i), ("cr", cr_i), ("ci", ci_i),
                    ("ldt8", ldt8_i)):
        P.write(P.DQ, w[nm], [], lambda e, d=w[nm].ap, s=src:
                e.dma_start(out=d, in_=s), step=16)
    P.write(P.DQ, dp, [], lambda e, d=dp.ap, s=dp_i: e.dma_start(out=d, in_=s), step=16)

    def S_(dst, srcb, func, bias=None, scale=None):
        kw = {}
        if bias is not None:
            kw["bias"] = bias
        if scale is not None:
            kw["scale"] = scale
        P.write(P.SQ, w[dst], [w[srcb]] + ([pihalf] if bias is not None else []),
                lambda e, o=w[dst].ap, i=w[srcb].ap, k=dict(kw):
                e.activation(out=o, in_=i, func=func, **k))

    def V2(dst, a, b_, op):
        P.write(P.VQ, w[dst], [w[a], w[b_]],
                lambda e, o=w[dst].ap, x=w[a].ap, y=w[b_].ap:
                e.tensor_tensor(out=o, in0=x, in1=y, op=op))

    def Vs(dst, a, s1, op):
        P.write(P.VQ, w[dst], [w[a]],
                lambda e, o=w[dst].ap, x=w[a].ap:
                e.tensor_single_scalar(out=o, in_=x, scalar=s1, op=op))

    S_("dt8", "ldt8", AF.Exp)
    S_("t_a", "ar", AF.Exp)
    Vs("ar", "t_a", -1.0, ALU.mult)
    V2("t_a", "ar", "dt8", ALU.mult)
    V2("t_b", "ai", "dt8", ALU.mult)
    S_("t_c", "t_a", AF.Exp)
    S_("t_d", "t_b", AF.Sin)
    V2("e1i", "t_c", "t_d", ALU.mult)
    Vs("t_d", "t_b", -1.0, ALU.mult)
    P.write(P.SQ, w["t_b"], [w["t_d"], pihalf],
            lambda e, o=w["t_b"].ap, i=w["t_d"].ap, b=pihalf.ap[:, 0:1]:
            e.activation(out=o, in_=i, func=AF.Sin, bias=b))
    V2("e1r", "t_c", "t_b", ALU.mult)
    V2("t_a", "ar", "ar", ALU.mult)
    V2("t_b", "ai", "ai", ALU.mult)
    V2("t_a", "t_a", "t_b", ALU.add)
    P.write(P.VQ, w["t_b"], [w["t_a"]],
            lambda e, o=w["t_b"].ap, i=w["t_a"].ap: e.reciprocal(out=o, in_=i),
            hazard=True)
    V2("t_c", "ar", "t_b", ALU.mult)
    V2("t_d", "ai", "t_b", ALU.mult)
    Vs("t_d", "t_d", -1.0, ALU.mult)
    Vs("t_a", "e1r", -1.0, ALU.add)
    V2("er", "t_a", "t_c", ALU.mult)
    V2("ei", "e1i", "t_d", ALU.mult)
    V2("er2", "t_a", "t_d", ALU.mult)
    V2("ei2", "e1i", "t_c", ALU.mult)
    V2("t_b", "er", "ei", ALU.subtract)
    V2("t_c", "er2", "ei2", ALU.add)
    V2("er", "cr", "t_b", ALU.mult)
    V2("ei", "ci", "t_c", ALU.mult)
    V2("Cr2", "er", "ei", ALU.subtract)
    Vs("Cr2", "Cr2", 2.0, ALU.mult)
    V2("er", "cr", "t_c", ALU.mult)
    V2("ei", "ci", "t_b", ALU.mult)
    V2("Ci2", "er", "ei", ALU.add)
    Vs("Ci2", "Ci2", 2.0, ALU.mult)

    def segred(dst_buf, src_name):
        P.write(P.VQ, dst_buf, [w[src_name]],
                lambda e, o=dst_buf.ap,
                i=w[src_name].ap.rearrange("p (c n) -> p c n", n=8):
                e.reduce_sum(out=o, in_=i, axis=AX.X), hazard=True)

    segred(kct[0], "Cr2")
    P.write(P.VQ, w["er"], [w["e1r"]],
            lambda e, o=w["er"].ap, i=w["e1r"].ap: e.tensor_copy(out=o, in_=i))
    P.write(P.VQ, w["ei"], [w["e1i"]],
            lambda e, o=w["ei"].ap, i=w["e1i"].ap: e.tensor_copy(out=o, in_=i))
    for t in range(1, 4):
        V2("t_a", "Cr2", "er", ALU.mult)
        V2("t_b", "Ci2", "ei", ALU.mult)
        V2("prod", "t_a", "t_b", ALU.subtract)
        segred(kct[t], "prod")
        if t < 3:
            V2("t_a", "er", "e1r", ALU.mult)
            V2("t_b", "ei", "e1i", ALU.mult)
            V2("t_c", "er", "e1i", ALU.mult)
            V2("t_d", "ei", "e1r", ALU.mult)
            V2("er2", "t_a", "t_b", ALU.subtract)
            V2("ei2", "t_c", "t_d", ALU.add)
            P.write(P.VQ, w["er"], [w["er2"]],
                    lambda e, o=w["er"].ap, i=w["er2"].ap: e.tensor_copy(out=o, in_=i))
            P.write(P.VQ, w["ei"], [w["ei2"]],
                    lambda e, o=w["ei"].ap, i=w["ei2"].ap: e.tensor_copy(out=o, in_=i))

    if kmask_i is not None:
        km_s = P.sb("km_s", [128, 4])
        P.write(P.DQ, km_s, [], lambda e, d=km_s.ap, s=kmask_i:
                e.dma_start(out=d, in_=s), step=16)
        for j in range(4):
            P.rw(P.VQ, kct[j], [km_s],
                 lambda e, o=kct[j].ap, i=kct[j].ap, m=km_s.ap[:, j:j + 1]:
                 e.tensor_scalar_mul(out=o, in0=i, scalar1=m))

    for tt in range(nTT):
        P.write(P.DQ, xt, [], lambda e, d=xt.ap, s=xT_i[tt, :, :]:
                e.dma_start(out=d, in_=s), step=16)
        xq_ = io.get("xc_queue", P.DQ)
        for j in range(4):
            P.write(xq_, cb, [], lambda e, d=cb.ap, j=j, tt=tt:
                    e.dma_start(out=d, in_=xc_i[j, tt, :, :]), step=16)
            for c in range(NCH):
                P.rw(P.VQ, xt, [cb, kct[j]],
                     lambda e, o=xt.ap[:, c * TW:(c + 1) * TW],
                     a=cb.ap[:, c * TW:(c + 1) * TW], k=kct[j].ap[:, c:c + 1],
                     xx=xt.ap[:, c * TW:(c + 1) * TW]:
                     e.scalar_tensor_tensor(out=o, in0=a, scalar=k, in1=xx,
                                            op0=ALU.mult, op1=ALU.add))
            if j == 0:
                for c in range(NCH):
                    P.rw(P.VQ, xt, [cb, dp],
                         lambda e, o=xt.ap[:, c * TW:(c + 1) * TW],
                         a=cb.ap[:, c * TW:(c + 1) * TW], k=dp.ap[:, c:c + 1],
                         xx=xt.ap[:, c * TW:(c + 1) * TW]:
                         e.scalar_tensor_tensor(out=o, in0=a, scalar=k, in1=xx,
                                                op0=ALU.mult, op1=ALU.add))
        P.read(P.DQ, xt, lambda e, d=x_o[tt, :, :], s=xt.ap:
               e.dma_start(out=d, in_=s), step=16)

    if standalone:
        return P.assemble()


def pmaj8(X):
    return np.ascontiguousarray(
        np.asarray(X, np.float32).reshape(16, 128, 8).transpose(1, 0, 2).reshape(128, 128))


def run_launch7(x6, xln6, D_param, log_dt, A_real_log, A_imag, C_re, C_im):
    nc = build_launch7()
    zeros_half = np.zeros((TC // TW, 128, NCH * TW), BF)
    ldt8 = np.ascontiguousarray(np.repeat(to_cols(log_dt), 8, axis=1))
    common = dict(ar=pmaj8(A_real_log), ai=pmaj8(A_imag), cr=pmaj8(C_re),
                  ci=pmaj8(C_im), ldt8=ldt8, dp=to_cols(D_param))
    in_maps = []
    for c in range(NCORE):
        b, hf = c // 2, c % 2
        sl = slice(hf * TC, hf * TC + TC)
        xc = np.stack([
            swz_act(np.ascontiguousarray(xln6[b - j, sl].T.astype(BF)))
            if b - j >= 0 else zeros_half
            for j in range(4)])
        m = dict(common)
        m["xT"] = swz_act(f32c(x6[b, sl].T))
        m["xc"] = xc
        in_maps.append(m)
    res = run_bass_kernel_spmd(nc, in_maps, core_ids=list(range(NCORE)))
    x7 = np.zeros((B, L, D), np.float32)
    for c in range(NCORE):
        b, hf = c // 2, c % 2
        sl = slice(hf * TC, hf * TC + TC)
        x7[b, sl] = unswz_act(res.results[c]["x"], D, TC).T
    return x7



def kernel(**inputs):
    inp = {k: np.asarray(v) for k, v in inputs.items()}
    return kernel_fused(inp)


def kernel_multilaunch(inp):
    x = f32c(inp["L0_attn_a_x"])
    ctx = run_launch1(x, inp["L0_attn_a_Wq"], inp["L0_attn_a_bq"],
                      inp["L0_attn_a_Wk"], inp["L0_attn_a_bk"],
                      inp["L0_attn_a_Wv"], inp["L0_attn_a_bv"])
    x2, q, k, v, wr = run_launch2(
        x, ctx, inp["L0_attn_a_Wo"], inp["L0_attn_a_bo"],
        inp["L0_ffn_weight"], inp["L0_ffn_rram_variation"],
        inp["L0_attn_b_w_qkv"], inp["L0_attn_b_b_qkv"])
    O = run_launch3(q, k, v)
    x4, qn, qr, kn, kr, vv = run_launch4(
        x2, O, inp["L0_attn_b_w_out"], inp["L0_attn_b_b_out"], wr,
        inp["L1_attn_a_Wqkv"], inp["L1_attn_a_Wk_up"], inp["L1_attn_a_Wv_up"])
    O1 = run_launch5(qn, qr, kn, kr, vv)
    moe_w = moe_host_weights(inp, "L1_ffn")
    x6, xln6 = run_launch6(x4, O1, inp["L1_attn_a_Wo"], moe_w,
                           with_outproj=True, emit_xln=True)
    x7 = run_launch7(x6, xln6, inp["L1_attn_b_D_param"], inp["L1_attn_b_log_dt"],
                     inp["L1_attn_b_A_real_log"], inp["L1_attn_b_A_imag"],
                     inp["L1_attn_b_C_re"], inp["L1_attn_b_C_im"])
    x8, _ = run_launch6(x7, None, None, moe_w, with_outproj=False, emit_xln=False)
    return x8.astype(np.float32)



_cc_counter = [0]


def emit_allgather(P, in_ap, out_ap, groups, wait_ticks):
    P.PQ.wait_many(wait_ticks)
    _cc_counter[0] += 1
    name = f"ccsem_{_cc_counter[0]}"
    semh = P.ctx.enter_context(P.nc.semaphore(name))
    P.PQ.runq(lambda e, i=in_ap, o=out_ap, g=groups, s=semh:
              e.collective_compute("AllGather", ALU.bypass, replica_groups=g,
                                   ins=[i.opt()], outs=[o.opt()]).then_inc(s, 1))
    return CCTick(semh, name)


class XCView:

    def __init__(self, gathered_ap, sec):
        self.g = gathered_ap
        self.sec = sec

    def __getitem__(self, idx):
        j, tt = idx[0], idx[1]
        r0 = (self.sec[j] * 4 + tt) * 128
        return self.g[r0:r0 + 128, :]


def emit_flash_fused(P, io, rope):
    NQT = L // 128
    qkv = io["qkv"]
    SEC = io["sec"]

    qn = [P.sb(f"fqn{i}", [128, L], BF16) for i in range(2)]
    kn = [P.sb(f"fkn{i}", [128, L], BF16) for i in range(2)]
    vraw = [P.sb(f"fvr{i}", [128, L], BF16) for i in range(2)]
    vv = [P.sb(f"fvv{i}", [128, L], BF16) for i in range(2)]
    if rope:
        qr = [P.sb(f"fqr{i}", [64, L], BF16) for i in range(2)]
        kr = [P.sb(f"fkr{i}", [64, L], BF16) for i in range(2)]
    ost = [P.sb(f"fost{i}", [128, L], BF16) for i in range(2)]
    masks = P.sb("fmasks", [128, 4 * TW], BF16)
    idt = P.sb("fidt", [128, 128], BF16)
    P_sb = P.sb("fP_sb", [128, TW], BF16)
    pt_sb = P.sb("fpt_sb", [128, TW], BF16)
    O_acc = P.sb("fO_acc", [128, 128])
    O_bf = P.sb("fO_bf", [128, 128], BF16)
    s_nmt = P.sb("fs_nmt", [128, 1])
    s_nmrun = P.sb("fs_nmrun", [128, 1])
    s_nmnew = P.sb("fs_nmnew", [128, 1])
    s_d = P.sb("fs_d", [128, 1])
    s_alpha = P.sb("fs_alpha", [128, 1])
    s_rsum = P.sb("fs_rsum", [128, 1])
    s_lrun = P.sb("fs_lrun", [128, 1])
    s_rcp = P.sb("fs_rcp", [128, 1])

    ps_s = [P.ps(f"fps_s{i}", [128, TW]) for i in range(2)]
    ps_t = P.ps("fps_t", [128, TW], BF16)
    ps_pv = P.ps("fps_pv", [128, 128])

    for mv_ in range(4):
        P.write(P.DQ, masks, [],
                lambda e, d=masks.ap[:, mv_ * TW:(mv_ + 1) * TW], s=io["masks"][mv_, :, :]:
                e.dma_start(out=d, in_=s), step=16)
    P.write(P.DQ, idt, [], lambda e, d=idt.ap, s=io["ident"][:, :]:
            e.dma_start(out=d, in_=s), step=16)
    P.PQ.wait(io["in_tick"])

    snap = io["snap"]
    HB = io["hbsz"]
    qkv_my = io["qkv_my"]
    myticks = []
    for r in range(2):
        t = P.PQ.run(lambda e, r=r:
                     e.dma_start(out=qkv_my[r * HB:(r + 1) * HB, :],
                                 in_=qkv[bass.ds(snap[io["hb_key"]] + r * SEC, HB), :]),
                     step=16)
        myticks.append(t)

    def load_rows(buf, base, local0, nrows):
        for r in range(2):
            P.DQ.wait(myticks[r])
            P.write(P.DQ, buf, [],
                    lambda e, d=buf.ap[0:nrows, r * TC:(r + 1) * TC],
                    s=qkv_my[r * HB + base + local0: r * HB + base + local0 + nrows, :]:
                    e.dma_start(out=d, in_=s), step=16)

    def load_head(hidx):
        sl = hidx % 2
        i = hidx
        if rope:
            load_rows(qn[sl], 0, i * 128, 128)
            load_rows(qr[sl], 1024, (i // 2) * 128 + 64 * (i % 2), 64)
            load_rows(kn[sl], 1536, (i // 4) * 128, 128)
            load_rows(kr[sl], 1792, 64 * (i // 4), 64)
            load_rows(vraw[sl], 1920, (i // 4) * 128, 128)
        else:
            load_rows(qn[sl], 0, i * 128, 128)
            load_rows(kn[sl], 1024, i * 128, 128)
            load_rows(vraw[sl], 2048, i * 128, 128)

    o_ticks = []
    load_head(0)
    for hidx in range(8):
        sl = hidx % 2
        if hidx + 1 < 8:
            load_head(hidx + 1)
        for tc in range(32):
            emit_group(P, P.TQ, ps_t, [vraw[sl], idt],
                       [lambda e, o=ps_t.ap[:, (tc % 4) * 128:(tc % 4 + 1) * 128],
                        i=vraw[sl].ap[:, tc * 128:(tc + 1) * 128], dd=idt.ap:
                        e.transpose(o, i, dd)])
            P.write(P.VQ, vv[sl], [ps_t],
                    lambda e, o=vv[sl].ap[:, tc * 128:(tc + 1) * 128],
                    i=ps_t.ap[:, (tc % 4) * 128:(tc % 4 + 1) * 128]:
                    e.tensor_copy(out=o, in_=i))
        ostage = ost[sl]
        for qt in range(NQT):
            KM = qt // 4 + 1
            for km in range(KM):
                psm = ps_s[km % 2]
                pieces = [(qn[sl], qn[sl].ap[:, qt * 128:(qt + 1) * 128],
                           kn[sl], kn[sl].ap[:, km * TW:(km + 1) * TW])]
                if rope:
                    pieces.append((qr[sl], qr[sl].ap[:, qt * 128:(qt + 1) * 128],
                                   kr[sl], kr[sl].ap[:, km * TW:(km + 1) * TW]))
                if km == qt // 4:
                    mv = qt % 4
                    pieces.append((idt, idt.ap,
                                   masks, masks.ap[:, mv * TW:(mv + 1) * TW]))
                emit_mm(P, psm, None, pieces)
                P.write(P.VQ, s_nmt, [psm],
                        lambda e, o=s_nmt.ap, i=psm.ap:
                        e.reduce_max(out=o, in_=i, axis=AX.X, negate=True),
                        hazard=True)
                if km == 0:
                    P.write(P.VQ, s_nmrun, [s_nmt],
                            lambda e, o=s_nmrun.ap, i=s_nmt.ap: e.tensor_copy(out=o, in_=i))
                else:
                    P.write(P.VQ, s_nmnew, [s_nmt, s_nmrun],
                            lambda e, o=s_nmnew.ap, a=s_nmrun.ap, b=s_nmt.ap:
                            e.tensor_tensor(out=o, in0=a, in1=b, op=ALU.min))
                    P.write(P.VQ, s_d, [s_nmnew, s_nmrun],
                            lambda e, o=s_d.ap, a=s_nmnew.ap, b=s_nmrun.ap:
                            e.tensor_sub(out=o, in0=a, in1=b))
                    P.write(P.VQ, s_nmrun, [s_nmnew],
                            lambda e, o=s_nmrun.ap, i=s_nmnew.ap: e.tensor_copy(out=o, in_=i))
                    P.write(P.SQ, s_alpha, [s_d],
                            lambda e, o=s_alpha.ap, i=s_d.ap:
                            e.activation(out=o, in_=i, func=AF.Exp))
                P.SQ.wait(psm.wtick)
                P.SQ.wait(s_nmrun.wtick)
                P.SQ.wait_many(P_sb.rticks)
                P.SQ.wait_many(s_rsum.rticks)
                if P_sb.wtick is not None:
                    P.SQ.wait(P_sb.wtick)
                t = P.SQ.run(lambda e, o=P_sb.ap, i=psm.ap, b=s_nmrun.ap[:, 0:1],
                             a=s_rsum.ap[:, 0:1]:
                             e.activation(out=o, in_=i, func=AF.Exp, bias=b,
                                          accum_out=a))
                P_sb.rticks = []
                P_sb.wtick = t
                s_rsum.rticks = []
                s_rsum.wtick = t
                psm.rticks.append(t)
                s_nmrun.rticks.append(t)
                if km == 0:
                    P.write(P.VQ, s_lrun, [s_rsum],
                            lambda e, o=s_lrun.ap, i=s_rsum.ap: e.tensor_copy(out=o, in_=i))
                else:
                    P.rw(P.VQ, s_lrun, [s_rsum, s_alpha],
                         lambda e, o=s_lrun.ap, a=s_lrun.ap, sc=s_alpha.ap[:, 0:1],
                         b=s_rsum.ap:
                         e.scalar_tensor_tensor(out=o, in0=a, scalar=sc, in1=b,
                                                op0=ALU.mult, op1=ALU.add))
                fns = [
                    (lambda e, o=ps_t.ap[:, j * 128:(j + 1) * 128],
                     i=P_sb.ap[:, j * 128:(j + 1) * 128], dd=idt.ap:
                     e.transpose(o, i, dd))
                    for j in range(4)
                ]
                emit_group(P, P.TQ, ps_t, [P_sb, idt], fns)
                P.write(P.VQ, pt_sb, [ps_t],
                        lambda e, o=pt_sb.ap, i=ps_t.ap: e.tensor_copy(out=o, in_=i))
                pieces = [(pt_sb, pt_sb.ap[:, j * 128:(j + 1) * 128],
                           vv[sl], vv[sl].ap[:, (4 * km + j) * 128:(4 * km + j + 1) * 128])
                          for j in range(4)]
                emit_mm(P, ps_pv, None, pieces)
                if km == 0:
                    P.write(P.VQ, O_acc, [ps_pv],
                            lambda e, o=O_acc.ap, i=ps_pv.ap: e.tensor_copy(out=o, in_=i))
                else:
                    P.rw(P.VQ, O_acc, [ps_pv, s_alpha],
                         lambda e, o=O_acc.ap, a=O_acc.ap, sc=s_alpha.ap[:, 0:1],
                         b=ps_pv.ap:
                         e.scalar_tensor_tensor(out=o, in0=a, scalar=sc, in1=b,
                                                op0=ALU.mult, op1=ALU.add))
            P.write(P.VQ, s_rcp, [s_lrun],
                    lambda e, o=s_rcp.ap, i=s_lrun.ap: e.reciprocal(out=o, in_=i),
                    hazard=True)
            P.write(P.VQ, O_bf, [O_acc, s_rcp],
                    lambda e, o=O_bf.ap, i=O_acc.ap, sc=s_rcp.ap[:, 0:1]:
                    e.tensor_scalar_mul(out=o, in0=i, scalar1=sc), fence=True)
            emit_group(P, P.TQ, ps_t, [O_bf, idt],
                       [lambda e, o=ps_t.ap[:, 0:128], i=O_bf.ap, dd=idt.ap:
                        e.transpose(o, i, dd)])
            P.write(P.VQ, ostage, [ps_t],
                    lambda e, o=ostage.ap[:, qt * 128:(qt + 1) * 128],
                    i=ps_t.ap[:, 0:128]: e.tensor_copy(out=o, in_=i))
        t = P.read(P.PQ, ostage, lambda e, d=io["Ocat"][hidx * 128:(hidx + 1) * 128, :],
                   s=ostage.ap: e.dma_start(out=d, in_=s), step=16)
        o_ticks.append(t)
    return o_ticks


def emit_ph1_fused(P, io):
    nTT = TC // TW
    SCALE = 128.0 ** -0.5

    maskc = P.sb("p1maskc", [128, 1])
    P.write(P.DQ, maskc, [], lambda e, d=maskc.ap, s=io["maskc"]:
            e.dma_start(out=d, in_=s), step=16)
    bqs = P.sb("p1bq", [128, NCH])
    bks = P.sb("p1bk", [128, NCH])
    bvs = P.sb("p1bv", [128, NCH])
    for buf, key in ((bqs, "bq"), (bks, "bk"), (bvs, "bv")):
        P.write(P.DQ, buf, [], lambda e, d=buf.ap, s=io[key]: e.dma_start(out=d, in_=s), step=16)
    xln = P.sb("p1xln", [128, NCH * TC], BF16)

    def xln_sl(c, tt):
        return xln.ap[:, (c * nTT + tt) * TW:(c * nTT + tt + 1) * TW]

    with P.arena():
        kit = LNKit(P, tag="p1")
        kit.load_consts(io["ones_bf"], io["ones_f"])
        xt = P.sb("p1xt", [128, NCH * TW])
        ps_a = P.ps("p1ps_a", [128, TW])
        ps_b = P.ps("p1ps_b", [128, TW])
        for tt in range(nTT):
            P.write(P.DQ, xt, [], lambda e, d=xt.ap, s=io["xT"][tt, :, :]:
                    e.dma_start(out=d, in_=s), step=16)
            x_list = [(xt, xt.ap[:, c * TW:(c + 1) * TW]) for c in range(NCH)]

            def ln_out(c, t1, mrB, tt=tt):
                P.write(P.VQ, xln, [t1, mrB],
                        lambda e, o=xln_sl(c, tt), a=t1.ap, b=mrB.ap:
                        e.tensor_sub(out=o, in0=a, in1=b))

            emit_ln(P, kit, x_list, ps_a, ps_b, ln_out)

    GN = 8
    ctx_bf = P.sb("p1ctx", [128, GN * TC], BF16)
    phisc = P.sb("p1phisc", [128, GN * TC], BF16)
    s_qf = P.sb("p1qf", [128, TW])
    s_phq = P.sb("p1phq", [128, TW])
    s_mn = P.sb("p1mn", [128, TW])
    s_ex = P.sb("p1ex", [128, TW])
    s_phk = P.sb("p1phk", [128, TW])
    s_vf = P.sb("p1vf", [128, TW])
    s_kvs = P.sb("p1kvs", [128, TW])
    zeros = P.sb("p1zeros", [128, TW])
    carry = P.sb("p1carry", [128, GN])
    carry_use = P.sb("p1cu", [128, GN])
    ps_q = P.ps("p1ps_q", [128, TW])
    ps_k = P.ps("p1ps_k", [128, TW])
    ps_v = P.ps("p1ps_v", [128, TW])

    P.write(P.VQ, zeros, [], lambda e, o=zeros.ap: e.memset(o, 0.0))

    ws_q = WStream(P, "p1wq", io["WqT"], NCH)
    ws_k = WStream(P, "p1wk", io["WkT"], NCH)
    ws_v = WStream(P, "p1wv", io["WvT"], NCH)

    def gsl(buf, gc, tt):
        return buf.ap[:, (gc * nTT + tt) * TW:(gc * nTT + tt + 1) * TW]

    def phi(src_psum, bias_buf, bias_col, out_buf):
        P.write(P.VQ, s_qf, [src_psum, bias_buf],
                lambda e, o=s_qf.ap, i=src_psum.ap, b=bias_col:
                e.tensor_scalar_add(out=o, in0=i, scalar1=b))
        P.write(P.PQ, s_mn, [s_qf],
                lambda e, o=s_mn.ap, i=s_qf.ap: e.tensor_scalar_min(out=o, in0=i, scalar1=0.0),
                step=1)
        P.write(P.SQ, s_ex, [s_mn],
                lambda e, o=s_ex.ap, i=s_mn.ap: e.activation(out=o, in_=i, func=AF.Exp))
        P.write(P.VQ, out_buf, [s_qf, s_ex],
                lambda e, o=out_buf.ap, a=s_qf.ap, x=s_ex.ap:
                e.scalar_tensor_tensor(out=o, in0=a, scalar=0.0, in1=x,
                                       op0=ALU.max, op1=ALU.add))

    for g in range(2):
        P.write(P.VQ, carry, [], lambda e, o=carry.ap: e.memset(o, 0.0))
        for gc in range(GN):
            mc = g * GN + gc
            for ws in (ws_q, ws_k, ws_v):
                if mc + 1 < NCH:
                    ws.prefetch(mc + 1)
            for tt in range(nTT):
                for (ws, psm) in ((ws_q, ps_q), (ws_k, ps_k), (ws_v, ps_v)):
                    pieces = [(ws.chunk(mc, kc)[0], ws.chunk(mc, kc)[1],
                               xln, xln_sl(kc, tt)) for kc in range(NCH)]
                    emit_mm(P, psm, None, pieces)
                phi(ps_q, bqs, bqs.ap[:, mc:mc + 1], s_phq)
                phi(ps_k, bks, bks.ap[:, mc:mc + 1], s_phk)
                P.write(P.VQ, s_vf, [ps_v, bvs],
                        lambda e, o=s_vf.ap, i=ps_v.ap, b=bvs.ap[:, mc:mc + 1]:
                        e.tensor_scalar_add(out=o, in0=i, scalar1=b))
                P.rw(P.VQ, s_phk, [s_vf],
                     lambda e, o=s_phk.ap, a=s_phk.ap, b=s_vf.ap:
                     e.tensor_mul(out=o, in0=a, in1=b))
                P.write(P.VQ, s_kvs, [s_phk, zeros, carry],
                        lambda e, o=s_kvs.ap, d0=s_phk.ap, d1=zeros.ap,
                        ini=carry.ap[:, gc:gc + 1]:
                        e.tensor_tensor_scan(out=o, data0=d0, data1=d1, initial=ini,
                                             op0=ALU.add, op1=ALU.bypass), hazard=True)
                P.rw(P.PQ, carry, [s_kvs],
                     lambda e, o=carry.ap[:, gc:gc + 1], i=s_kvs.ap[:, TW - 1:TW]:
                     e.tensor_copy(out=o, in_=i), step=1)
                P.write(P.VQ, phisc, [s_phq],
                        lambda e, o=gsl(phisc, gc, tt), a=s_phq.ap:
                        e.tensor_scalar_mul(out=o, in0=a, scalar1=SCALE))
                P.write(P.VQ, ctx_bf, [s_phq, s_kvs],
                        lambda e, o=gsl(ctx_bf, gc, tt), a=s_phq.ap, b=s_kvs.ap:
                        e.scalar_tensor_tensor(out=o, in0=a, scalar=SCALE, in1=b,
                                               op0=ALU.mult, op1=ALU.mult), fence=True)
        carry_src = P.dtmp(f"p1carry_src{g}", [128, GN])
        carry_all = P.dtmp(f"p1carry_all{g}", [256, GN])
        t_st = P.read(P.DQ, carry, lambda e, d=carry_src[:, :], s=carry.ap:
                      e.dma_start(out=d, in_=s), step=16)
        t_cc = emit_allgather(P, carry_src, carry_all,
                              [[0, 1], [2, 3], [4, 5], [6, 7]], [t_st])
        P.DQ.wait(t_cc)
        P.write(P.DQ, carry_use, [], lambda e, d=carry_use.ap, s=carry_all[0:128, :]:
                e.dma_start(out=d, in_=s), step=16)
        P.rw(P.VQ, carry_use, [maskc],
             lambda e, o=carry_use.ap, i=carry_use.ap, m=maskc.ap[:, 0:1]:
             e.tensor_scalar_mul(out=o, in0=i, scalar1=m))
        for tt in range(nTT):
            for gc in range(GN):
                mc = g * GN + gc
                P.rw(P.VQ, ctx_bf, [phisc, carry_use],
                     lambda e, o=gsl(ctx_bf, gc, tt), a=gsl(phisc, gc, tt),
                     k=carry_use.ap[:, gc:gc + 1], xx=gsl(ctx_bf, gc, tt):
                     e.scalar_tensor_tensor(out=o, in0=a, scalar=k, in1=xx,
                                            op0=ALU.mult, op1=ALU.add))
                P.read(P.DQ, ctx_bf,
                       lambda e, d=io["ctx"][tt, :, mc * TW:(mc + 1) * TW],
                       s=gsl(ctx_bf, gc, tt): e.dma_start(out=d, in_=s), step=16)



I32 = mybir.dt.int32


class RowMapView:

    def __init__(self, ap, rowmap, nrows=128):
        self.ap_ = ap
        self.rm = rowmap
        self.n = nrows

    def __getitem__(self, idx):
        mc = idx[0]
        r0 = self.rm(mc)
        return self.ap_[r0:r0 + self.n, :]


class OViewStatic:

    def __init__(self, ap):
        self.ap_ = ap

    def __getitem__(self, idx):
        tt = idx[0]
        return self.ap_.rearrange("(c p) t -> p c t", p=128)[:, :, tt * TW:(tt + 1) * TW]


class XCViewStatic:

    def __init__(self, ap):
        self.ap_ = ap

    def __getitem__(self, idx):
        j, tt = idx[0], idx[1]
        return self.ap_[(j * 4 + tt) * 128:(j * 4 + tt + 1) * 128, :]


def build_fused(debug=False):
    P = Prog()
    nTT = TC // TW
    PAIRS = [[0, 1], [2, 3], [4, 5], [6, 7]]
    ALL8 = [list(range(8))]

    ones_bf = P.din("ones_bf", [128, 1], BF16)
    ones_f = P.din("ones_f", [1, 128])
    ident = P.din("ident", [128, 128], BF16)
    masks = P.din("masks", [4, 128, TW], BF16)
    offs_i = P.din("offs", [1, 16], I32)
    maskc_i = P.din("maskc", [128, 1])
    kmask_i = P.din("kmask", [128, 4])

    x_in = P.din("xT1", [nTT, 128, NCH * TW])
    l1 = {k: P.din("l1_" + k, [NCH, 128, D]) for k in ("WqT", "WkT", "WvT")}
    l1b = {k: P.din("l1_" + k, [128, NCH]) for k in ("bq", "bk", "bv")}
    l2 = {k: P.din("l2_" + k, [NCH, 128, D]) for k in
          ("WoT", "wT", "rrT", "WqT", "WkT", "WvT")}
    l2b = {k: P.din("l2_" + k, [128, NCH]) for k in ("bo", "bq", "bk", "bv")}
    l4_WoT = P.din("l4_WoT", [NCH, 128, D])
    l4_bo = P.din("l4_bo", [128, NCH])
    l4_Wqkv = P.din("l4_Wqkv", [28, 128, D])
    l4_Wku = P.din("l4_Wku", [6, 128, 512])
    l4_Wvu = P.din("l4_Wvu", [4, 128, 512])
    l4_cosq = P.din("l4_cosq", [128, TC], BF16)
    l4_sinq = P.din("l4_sinq", [128, TC], BF16)
    l4_cosk = P.din("l4_cosk", [128, TC], BF16)
    l4_sink = P.din("l4_sink", [128, TC], BF16)
    l4_RT = P.din("l4_RT", [128, 128], BF16)
    l6_WoT = P.din("l6_WoT", [NCH, 128, D])
    moe = dict(
        gwT=P.din("m_gwT", [1, 128, 16 * NCH]),
        eb=P.din("m_eb", [1, 16]),
        sel=P.din("m_sel", [16, 16 * 128], BF16),
        gw=P.din("m_gw", [16, 8, 128, D]),
        uw=P.din("m_uw", [16, 8, 128, D]),
        dw=P.din("m_dw", [16, 16, 128, 1024]),
        shg=P.din("m_shg", [16, 128, D]),
        shu=P.din("m_shu", [16, 128, D]),
        shd=P.din("m_shd", [16, 128, D]),
        idt16=P.din("m_idt16", [16, 16], BF16),
        idt128f=P.din("m_idt128f", [128, 128], BF16),
    )
    s4d = {k: P.din("s_" + k, [128, 128]) for k in ("ar", "ai", "cr", "ci", "ldt8")}
    s4d_dp = P.din("s_dp", [128, NCH])

    x8_o = P.dout("x8", [nTT, 128, NCH * TW])

    ctx_scr = P.dtmp("ctx_scr", [nTT, 128, NCH * TW], BF16)
    x2_scr = P.dtmp("x2_scr", [nTT, 128, NCH * TW])
    qkv_cat = P.dtmp("qkv_cat", [6144, TC], BF16)
    qkv_all = P.dtmp("qkv_all", [12288, TC], BF16)
    Ocat = P.dtmp("Ocat", [1024, L], BF16)
    Ocat_all = P.dtmp("Ocat_all", [2048, L], BF16)
    x4_scr = P.dtmp("x4_scr", [nTT, 128, NCH * TW])
    qkv2_cat = P.dtmp("qkv2_cat", [4352, TC], BF16)
    qkv2_all = P.dtmp("qkv2_all", [8704, TC], BF16)
    Ocat2 = P.dtmp("Ocat2", [1024, L], BF16)
    Ocat2_all = P.dtmp("Ocat2_all", [2048, L], BF16)
    x6_scr = P.dtmp("x6_scr", [nTT, 128, NCH * TW])
    xln6_scr = P.dtmp("xln6_scr", [nTT, 128, NCH * TW], BF16)
    xc_all = P.dtmp("xc_all", [8 * 512, NCH * TW], BF16)
    x7_scr = P.dtmp("x7_scr", [nTT, 128, NCH * TW])
    wr_scr = P.dtmp("wr_scr", [NCH, 128, D], BF16)

    offs_sb = P.sb("offs_sb", [1, 16], I32)
    t_ofs = P.write(P.DQ, offs_sb, [], lambda e, d=offs_sb.ap, s=offs_i:
                    e.dma_start(out=d, in_=s), step=16)
    snapH = {}
    REGTAB = [(0, "h3072", 3072), (1, "h2176", 2176), (2, "ocol", 2048),
              (3, "xc0", 3584), (4, "xc1", 3584), (5, "xc2", 3584), (6, "xc3", 3584)]
    P.PQ.wait(t_ofs)

    def _setup_regs(e, H=snapH, tab=REGTAB, sb=offs_sb):
        for idx, key, mx in tab:
            r = e.alloc_register()
            e.reg_load(r, sb.ap[0:1, idx:idx + 1])
            H[key] = e.snap(r, min_val=0, max_val=mx)
    P.PQ.runq(_setup_regs)

    qkv_my1 = P.dtmp("qkv_my1", [6144, TC], BF16)
    qkv_my2 = P.dtmp("qkv_my2", [4352, TC], BF16)
    O_my1 = P.dtmp("O_my1", [2048, TC], BF16)
    O_my2 = P.dtmp("O_my2", [2048, TC], BF16)
    xc_my = P.dtmp("xc_my", [2048, NCH * TW], BF16)

    with P.arena():
        P.pfx = "f1_"
        emit_ph1_fused(P, dict(
            xT=x_in, WqT=l1["WqT"], WkT=l1["WkT"], WvT=l1["WvT"],
            bq=l1b["bq"], bk=l1b["bk"], bv=l1b["bv"],
            ones_bf=ones_bf, ones_f=ones_f, maskc=maskc_i, ctx=ctx_scr))
    with P.arena():
        P.pfx = "f2_"
        build_launch2(P, dict(
            xT=x_in, ctx=ctx_scr, WoT=l2["WoT"], bo=l2b["bo"],
            wT=l2["wT"], rrT=l2["rrT"], WqT=l2["WqT"], WkT=l2["WkT"],
            WvT=l2["WvT"], bq=l2b["bq"], bk=l2b["bk"], bv=l2b["bv"],
            ones_bf=ones_bf, ones_f=ones_f,
            x2=x2_scr,
            q=RowMapView(qkv_cat, lambda mc: (mc // 8) * 3072 + (mc % 8) * 128),
            k=RowMapView(qkv_cat, lambda mc: (mc // 8) * 3072 + 1024 + (mc % 8) * 128),
            v=RowMapView(qkv_cat, lambda mc: (mc // 8) * 3072 + 2048 + (mc % 8) * 128),
            wr=wr_scr))
    P.pfx = ""
    t_g1 = emit_allgather(P, qkv_cat, qkv_all, PAIRS, [])
    with P.arena():
        P.pfx = "f3_"
        emit_flash_fused(P, dict(
            qkv=qkv_all, sec=6144, hbsz=3072, hb_key="h3072", qkv_my=qkv_my1,
            snap=snapH, masks=masks, ident=ident, Ocat=Ocat, in_tick=t_g1), rope=False)
    P.pfx = ""
    t_g2 = emit_allgather(P, Ocat, Ocat_all, PAIRS, [])
    P.PQ.wait(t_g2)
    t_oc = P.PQ.run(lambda e: e.dma_start(
        out=O_my1[:, :], in_=Ocat_all[:, bass.ds(snapH["ocol"], TC)]), step=16)
    P.DQ.wait(t_oc)
    with P.arena():
        P.pfx = "f4_"
        build_launch4(P, dict(
            xT=x2_scr, O=OViewStatic(O_my1),
            WoT=l4_WoT, bo=l4_bo, wr=wr_scr, Wqkv=l4_Wqkv, Wku=l4_Wku,
            Wvu=l4_Wvu, cosq=l4_cosq, sinq=l4_sinq, cosk=l4_cosk, sink=l4_sink,
            RT=l4_RT, ones_bf=ones_bf, ones_f=ones_f,
            x4=x4_scr,
            qn=RowMapView(qkv2_cat, lambda mc: (mc // 8) * 2176 + (mc % 8) * 128),
            qr=RowMapView(qkv2_cat, lambda mc: (mc // 4) * 2176 + 1024 + (mc % 4) * 128),
            kn=RowMapView(qkv2_cat, lambda mc: (mc // 2) * 2176 + 1536 + (mc % 2) * 128),
            kr=RowMapView(qkv2_cat, lambda mc: mc * 2176 + 1792),
            vv=RowMapView(qkv2_cat, lambda mc: (mc // 2) * 2176 + 1920 + (mc % 2) * 128)))
    P.pfx = ""
    t_g3 = emit_allgather(P, qkv2_cat, qkv2_all, PAIRS, [])
    with P.arena():
        P.pfx = "f5_"
        emit_flash_fused(P, dict(
            qkv=qkv2_all, sec=4352, hbsz=2176, hb_key="h2176", qkv_my=qkv_my2,
            snap=snapH, masks=masks, ident=ident, Ocat=Ocat2, in_tick=t_g3), rope=True)
    P.pfx = ""
    t_g4 = emit_allgather(P, Ocat2, Ocat2_all, PAIRS, [])
    P.PQ.wait(t_g4)
    t_oc2 = P.PQ.run(lambda e: e.dma_start(
        out=O_my2[:, :], in_=Ocat2_all[:, bass.ds(snapH["ocol"], TC)]), step=16)
    P.DQ.wait(t_oc2)
    with P.arena():
        P.pfx = "f6_"
        io6 = dict(moe)
        io6.update(xT=x4_scr, O=OViewStatic(O_my2),
                   WoT=l6_WoT, ones_bf=ones_bf, ones_f=ones_f,
                   x=x6_scr, xln=xln6_scr)
        build_launch6(True, True, P, io6)
    P.pfx = ""
    t_g5 = emit_allgather(P, xln6_scr.rearrange("t p f -> (t p) f"), xc_all, ALL8, [])
    P.PQ.wait(t_g5)
    t_xc = None
    for j in range(4):
        t_xc = P.PQ.run(lambda e, j=j: e.dma_start(
            out=xc_my[j * 512:(j + 1) * 512, :],
            in_=xc_all[bass.ds(snapH["xc%d" % j], 512), :]), step=16)
        P.DQ.wait(t_xc)
    with P.arena():
        P.pfx = "f7_"
        io7 = dict(s4d)
        io7.update(xT=x6_scr, xc=XCViewStatic(xc_my),
                   dp=s4d_dp, x=x7_scr, kmask=kmask_i)
        build_launch7(P, io7)
    with P.arena():
        P.pfx = "f8_"
        io8 = dict(moe)
        io8.update(xT=x7_scr, ones_bf=ones_bf, ones_f=ones_f, x=x8_o)
        build_launch6(False, False, P, io8)
    P.pfx = ""

    if debug:
        dbgs = [("ctx_scr", ctx_scr, [4, 128, NCH * TW], BF16),
                ("x2_scr", x2_scr, [4, 128, NCH * TW], F32),
                ("qkv_cat", qkv_cat, [6144, TC], BF16),
                ("O_my1", O_my1, [2048, TC], BF16),
                ("x4_scr", x4_scr, [4, 128, NCH * TW], F32),
                ("qkv2_cat", qkv2_cat, [4352, TC], BF16),
                ("O_my2", O_my2, [2048, TC], BF16),
                ("x6_scr", x6_scr, [4, 128, NCH * TW], F32),
                ("xln6_scr", xln6_scr, [4, 128, NCH * TW], BF16),
                ("xc_my", xc_my, [2048, NCH * TW], BF16),
                ("x7_scr", x7_scr, [4, 128, NCH * TW], F32)]
        for name, ap, shape, dt_ in dbgs:
            o = P.dout("dbg_" + name, shape, dt_)
            nd = len(shape)
            P.PQ.run(lambda e, d=o, s=ap, nd=nd: e.dma_start(
                out=d[:, :] if nd == 2 else d[:, :, :],
                in_=s if nd == 2 else s[:, :, :]), step=16)

    return P.assemble()


def kernel_fused(inp):
    hc = host_consts()
    nc = build_fused()
    moe_w = moe_host_weights(inp, "L1_ffn")
    Wqkv_r = reorder_wqkv_cols(np.asarray(inp["L1_attn_a_Wqkv"]))
    Wku_r = reorder_wup_cols(np.asarray(inp["L1_attn_a_Wk_up"]))
    S192 = 192.0 ** -0.5
    cq_full, sq_full = rope_host_tables(S192)
    ck_full, sk_full = rope_host_tables(1.0)
    w_qkv = np.asarray(inp["L0_attn_b_w_qkv"])
    b_qkv = np.asarray(inp["L0_attn_b_b_qkv"])
    common = dict(
        ones_bf=hc["ones_bf"], ones_f=hc["ones_f"],
        ident=np.eye(128).astype(BF), masks=hc["masks"],
        l1_WqT=swz_w(np.asarray(inp["L0_attn_a_Wq"]).T),
        l1_WkT=swz_w(np.asarray(inp["L0_attn_a_Wk"]).T),
        l1_WvT=swz_w(np.asarray(inp["L0_attn_a_Wv"]).T),
        l1_bq=to_cols(inp["L0_attn_a_bq"]), l1_bk=to_cols(inp["L0_attn_a_bk"]),
        l1_bv=to_cols(inp["L0_attn_a_bv"]),
        l2_WoT=swz_w(np.asarray(inp["L0_attn_a_Wo"]).T),
        l2_bo=to_cols(inp["L0_attn_a_bo"]),
        l2_wT=swz_w(np.asarray(inp["L0_ffn_weight"]).T),
        l2_rrT=swz_w(np.asarray(inp["L0_ffn_rram_variation"]).T),
        l2_WqT=swz_w(w_qkv[0:D].T), l2_WkT=swz_w(w_qkv[D:2 * D].T),
        l2_WvT=swz_w(w_qkv[2 * D:3 * D].T),
        l2_bq=to_cols(b_qkv[0:D]), l2_bk=to_cols(b_qkv[D:2 * D]),
        l2_bv=to_cols(b_qkv[2 * D:3 * D]),
        l4_WoT=swz_w(np.asarray(inp["L0_attn_b_w_out"]).T),
        l4_bo=to_cols(inp["L0_attn_b_b_out"]),
        l4_Wqkv=swz_w(Wqkv_r), l4_Wku=swz_w(Wku_r),
        l4_Wvu=swz_w(np.asarray(inp["L1_attn_a_Wv_up"])),
        l4_RT=rot_matrix().astype(BF),
        l6_WoT=swz_w(np.asarray(inp["L1_attn_a_Wo"])),
        m_gwT=moe_w["gwT"], m_eb=moe_w["eb"], m_sel=moe_w["sel"],
        m_gw=moe_w["gw"], m_uw=moe_w["uw"], m_dw=moe_w["dw"],
        m_shg=moe_w["shg"], m_shu=moe_w["shu"], m_shd=moe_w["shd"],
        m_idt16=moe_w["idt16"], m_idt128f=moe_w["idt128f"],
        s_ar=pmaj8(inp["L1_attn_b_A_real_log"]), s_ai=pmaj8(inp["L1_attn_b_A_imag"]),
        s_cr=pmaj8(inp["L1_attn_b_C_re"]), s_ci=pmaj8(inp["L1_attn_b_C_im"]),
        s_ldt8=np.ascontiguousarray(np.repeat(to_cols(inp["L1_attn_b_log_dt"]), 8, axis=1)),
        s_dp=to_cols(inp["L1_attn_b_D_param"]),
    )
    x = f32c(np.asarray(inp["L0_attn_a_x"]))
    in_maps = []
    for c in range(NCORE):
        b, hf = c // 2, c % 2
        sl = slice(hf * TC, hf * TC + TC)
        offs = np.zeros((1, 16), np.int32)
        offs[0, 0] = 3072 * hf
        offs[0, 1] = 2176 * hf
        offs[0, 2] = 2048 * hf
        for j in range(4):
            offs[0, 3 + j] = (2 * max(b - j, 0) + hf) * 512
        kmask = np.zeros((128, 4), np.float32)
        for j in range(4):
            kmask[:, j] = 1.0 if b - j >= 0 else 0.0
        m = dict(common)
        m.update(
            xT1=swz_act(f32c(x[b, sl].T)),
            offs=offs,
            maskc=np.full((128, 1), float(hf), np.float32),
            kmask=kmask,
            l4_cosq=cq_full[:, sl].astype(BF), l4_sinq=sq_full[:, sl].astype(BF),
            l4_cosk=ck_full[:, sl].astype(BF), l4_sink=sk_full[:, sl].astype(BF),
        )
        in_maps.append(m)
    res = run_bass_kernel_spmd(nc, in_maps, core_ids=list(range(NCORE)))
    x8 = np.zeros((B, L, D), np.float32)
    for c in range(NCORE):
        b, hf = c // 2, c % 2
        sl = slice(hf * TC, hf * TC + TC)
        x8[b, sl] = unswz_act(res.results[c]["x8"], D, TC).T
    return x8
